# version 37
# baseline (speedup 1.0000x reference)
import numpy as np
import ml_dtypes
from contextlib import ExitStack

import concourse.bass as bass
import concourse.mybir as mybir
from concourse.bass_utils import run_bass_kernel_spmd

F32 = mybir.dt.float32
BF16 = mybir.dt.bfloat16
FP8 = mybir.dt.float8e4
AF = mybir.ActivationFunctionType
ALU = mybir.AluOpType
DR = mybir.MatmulPerfMode.DoubleRow
U32 = mybir.dt.uint32

BF = ml_dtypes.bfloat16

B, S, D = 4, 4096, 2048
H, NOPE, ROPE, VD = 16, 128, 64, 128
QHD = NOPE + ROPE
SCALE = QHD ** -0.5
THETA = 10000.0
CKV = 512
QA = 1536
EPS_LN = 1e-5
EPS_RMS = 1e-6

TG = 512
NG = S // TG
OG = 4
NKB_SEG = 4

ENGINES = ("sync", "tensor", "vector", "scalar", "gpsimd")


class Em:
    def __init__(self, nc, block):
        self.nc = nc
        self.block = block
        self._stack = ExitStack()
        self.sem = {}
        for name in ("ten", "act", "vec", "gsm", "ph"):
            self.sem[name] = self._stack.enter_context(nc.semaphore(name))
        self.cnt = {k: 0 for k in self.sem}
        self.q = {e: [] for e in ENGINES}
        self.lastwait = {}
        self.wst_marks = {}
        self.dstreams = {}

    def stream(self, name, queue):
        if name not in self.sem:
            self.sem[name] = self._stack.enter_context(self.nc.semaphore(name))
            self.cnt[name] = 0
        self.dstreams[name] = queue
        return name

    def emit(self, eng, build, inc=None):
        mark = None
        if inc is not None:
            s, a = inc
            self.cnt[s] += a
            mark = self.cnt[s]
            h = self.sem[s]

            def f(e, build=build, h=h, a=a):
                build(e).then_inc(h, a)
        else:
            def f(e, build=build):
                build(e)
        self.q[eng].append(f)
        return mark

    def wait(self, eng, s, val):
        if not val or val <= 0:
            return
        key = (eng, s)
        if self.lastwait.get(key, 0) >= val:
            return
        self.lastwait[key] = val
        h = self.sem[s]
        self.q[eng].append(lambda e, h=h, v=val: e.wait_ge(h, v))

    def flush(self):
        for eng in ENGINES:
            fns = self.q[eng]
            if not fns:
                continue
            self.q[eng] = []

            def body(e, fns=fns):
                for f in fns:
                    f(e)
            getattr(self.block, eng)(body)

    def barrier(self):
        for name, queue in self.dstreams.items():
            eng = "sync" if queue == "sync" else "gpsimd"
            self.wait(eng, name, self.cnt[name])
        for eng in ENGINES:
            self.cnt["ph"] += 1
            h = self.sem["ph"]
            self.q[eng].append(lambda e, h=h: e.drain().then_inc(h, 1))
        tgt = self.cnt["ph"]
        self.flush()
        for eng in ENGINES:
            self.wait(eng, "ph", tgt)


def _dma(em, stream, out_ap, in_ap):
    queue = em.dstreams[stream]
    return em.emit(queue, lambda e, o=out_ap, i=in_ap: e.dma_start(out=o, in_=i),
                   inc=(stream, 16))


def _r3(ap, p=128):
    return ap.rearrange("(c p) q -> p c q", p=p)


def _s3(ap, q):
    return ap.rearrange("p (c q) -> p c q", q=q)


def load_weight(em, wst, dst, src2d, C, K, fp8_scale=None, queue="sync"):
    key = id(wst)
    HS = wst.shape[1] // 2
    marks = em.wst_marks.setdefault(key, [])
    for c in range(C):
        for k0 in range(0, K, HS):
            kw = min(HS, K - k0)
            half = len(marks) % 2
            st = em.stream(f"w{half}" if queue == "sync" else f"wg{half}", queue)
            if len(marks) >= 2:
                em.wait("sync" if queue == "sync" else "gpsimd", "vec", marks[-2])
            m = _dma(em, st, wst[:, half * HS:half * HS + kw],
                     src2d[c * 128:(c + 1) * 128, k0:k0 + kw])
            em.wait("vector", st, m)
            d = dst[:, c * K + k0:c * K + k0 + kw]
            s = wst[:, half * HS:half * HS + kw]
            if fp8_scale is None:
                cm = em.emit("vector", lambda e, d=d, s=s: e.tensor_copy(d, s),
                             inc=("vec", 1))
            else:
                cm = em.emit("vector", lambda e, d=d, s=s, sc=fp8_scale:
                             e.tensor_scalar(d, s, sc, None, ALU.mult), inc=("vec", 1))
            marks.append(cm)
    return marks[-1]


def build_program(debug_outputs=()):
    nc = bass.Bass("TRN2", target_bir_lowering=False, debug=False)

    dk = "ExternalOutput" if "caches" in debug_outputs else "Internal"
    dq = "ExternalOutput" if "q" in debug_outputs else "Internal"
    da = "ExternalOutput" if "at" in debug_outputs else "Internal"

    xT_all = nc.dram_tensor("xT_all", [D, S], F32, kind="ExternalInput").ap()
    xT_own = nc.dram_tensor("xT_own", [D, 2048], F32, kind="ExternalInput").ap()
    wqa = nc.dram_tensor("wqa", [D, QA], F32, kind="ExternalInput").ap()
    wqbn = nc.dram_tensor("wqbn", [QA, H * NOPE], F32, kind="ExternalInput").ap()
    wqbr = nc.dram_tensor("wqbr", [QA, H * ROPE], F32, kind="ExternalInput").ap()
    wkvc = nc.dram_tensor("wkvc", [D, CKV], F32, kind="ExternalInput").ap()
    wkvr = nc.dram_tensor("wkvr", [D, ROPE], F32, kind="ExternalInput").ap()
    wkbk = nc.dram_tensor("wkbk", [CKV, H * NOPE], F32, kind="ExternalInput").ap()
    wkbv = nc.dram_tensor("wkbv", [CKV, H * VD], F32, kind="ExternalInput").ap()
    wo = nc.dram_tensor("wo", [H * VD, D], F32, kind="ExternalInput").ap()
    w2 = nc.dram_tensor("w2", [D, D], F32, kind="ExternalInput").ap()
    gqa = nc.dram_tensor("gqa", [128, QA // 128], F32, kind="ExternalInput").ap()
    gkv = nc.dram_tensor("gkv", [128, CKV // 128], F32, kind="ExternalInput").ap()
    cosg = nc.dram_tensor("cosg", [ROPE, S], F32, kind="ExternalInput").ap()
    sing = nc.dram_tensor("sing", [ROPE, S], F32, kind="ExternalInput").ap()
    coso = nc.dram_tensor("coso", [ROPE, 2048], F32, kind="ExternalInput").ap()
    sino = nc.dram_tensor("sino", [ROPE, 2048], F32, kind="ExternalInput").ap()
    masks_d = nc.dram_tensor("masks", [128, 8 * 512], BF16, kind="ExternalInput").ap()
    ones_c_d = nc.dram_tensor("ones_c", [128, 1], F32, kind="ExternalInput").ap()
    ones_b_d = nc.dram_tensor("ones_b", [128, 1], BF16, kind="ExternalInput").ap()
    ones_r_d = nc.dram_tensor("ones_r", [1, 128], F32, kind="ExternalInput").ap()
    ones_rb_d = nc.dram_tensor("ones_rb", [1, 128], BF16, kind="ExternalInput").ap()
    rotT_d = nc.dram_tensor("rotT", [128, 128], F32, kind="ExternalInput").ap()
    sc_d = nc.dram_tensor("scales", [128, 2], F32, kind="ExternalInput").ap()
    outT = nc.dram_tensor("outT", [D, 2048], F32, kind="ExternalOutput").ap()

    kcache = nc.dram_tensor("kcache", [H * NOPE, S], BF16, kind=dk).ap()
    vcache = nc.dram_tensor("vcache", [S, H * VD], BF16, kind=dk).ap()
    qacache = nc.dram_tensor("qacache", [QA, 2048], BF16, kind="Internal").ap()
    qncache = nc.dram_tensor("qncache", [H * NOPE, 2048], BF16, kind=dq).ap()
    qrcache = nc.dram_tensor("qrcache", [ROPE, H * 2048], BF16, kind=dq).ap()
    atcache = nc.dram_tensor("atcache", [H * VD, 2048], BF16, kind=da).ap()
    kpe_dbg = (nc.dram_tensor("kpe_dbg", [ROPE, S], BF16, kind="ExternalOutput").ap()
               if "caches" in debug_outputs else None)

    top = ExitStack()
    sb = lambda name, shape, dt: top.enter_context(nc.sbuf_tensor(name, shape, dt))
    ones_c = sb("ones_c_t", [128, 1], F32)
    ones_b = sb("ones_b_t", [128, 1], BF16)
    ones_r = sb("ones_r_t", [1, 128], F32)
    ones_rb = sb("ones_rb_t", [1, 128], BF16)
    rowsb = sb("rowsb", [1, 4 * 512], BF16)
    rotT = sb("rotT_t", [128, 128], F32)
    gqa_t = sb("gqa_t", [128, QA // 128], F32)
    gkv_t = sb("gkv_t", [128, CKV // 128], F32)
    sc_t = sb("sc_t", [128, 2], F32)
    rr_t = sb("rr_t", [128, 2], F32)
    rcol = sb("rcol", [128, 1], F32)
    kpe_t = sb("kpe_t", [128, S], BF16)
    rows = sb("rows", [1, 512 * 8], F32)

    def row(i):
        return rows[0:1, i * 512:(i + 1) * 512]

    def rowb(i):
        return rowsb[0:1, i * 512:(i + 1) * 512]

    rowb2 = rowb

    ps = [top.enter_context(nc.psum_tensor(f"ps{i}", [128, 512], F32)) for i in range(8)]

    block = top.enter_context(nc.Block())
    em = Em(nc, block)
    st_x = em.stream("stx", "sync")
    st_cs = em.stream("stcs", "sync")
    st_mask = em.stream("stmask", "sync")
    st_ksp = em.stream("stksp", "gpsimd")
    st_xbf = em.stream("stxbf", "gpsimd")
    st_vsp = em.stream("stvsp", "gpsimd")
    st_kpd = em.stream("stkpd", "gpsimd")
    st_qasp = em.stream("stqasp", "gpsimd")
    st_qnsp = em.stream("stqnsp", "gpsimd")
    st_qrsp = em.stream("stqrsp", "gpsimd")
    st_seg = [em.stream("stseg0", "sync"), em.stream("stseg1", "sync")]
    st_at = [em.stream("stat0", "gpsimd"), em.stream("stat1", "gpsimd")]
    st_xr = [em.stream("stxr0", "sync"), em.stream("stxr1", "sync")]
    st_out = [em.stream("stout0", "gpsimd"), em.stream("stout1", "gpsimd")]

    st_k = em.stream("konst", "sync")
    for dst, src in ((ones_c, ones_c_d), (ones_b, ones_b_d), (ones_r, ones_r_d),
                     (ones_rb, ones_rb_d),
                     (rotT, rotT_d), (gqa_t, gqa), (gkv_t, gkv), (sc_t, sc_d)):
        _dma(em, st_k, dst[:], src[:])
    em.wait("vector", st_k, em.cnt[st_k])
    rm0 = em.emit("vector", lambda e: e.reciprocal(rr_t[:], sc_t[:]), inc=("vec", 1))
    em.wait("vector", "vec", rm0)
    em.emit("vector", lambda e: e.tensor_mul(rcol[:], rr_t[:, 0:1], rr_t[:, 1:2]),
            inc=("vec", 1))

    def ln_rows(ndim, eps):
        inv = 1.0 / ndim
        em.emit("vector", lambda e: e.tensor_scalar(row(0), ps[6][0:1, :], inv, None, ALU.mult), inc=("vec", 1))
        em.emit("vector", lambda e: e.tensor_scalar(row(1), ps[7][0:1, :], inv, None, ALU.mult), inc=("vec", 1))
        em.wait("vector", "vec", em.cnt["vec"])
        em.emit("vector", lambda e: e.tensor_mul(row(2), row(0), row(0)), inc=("vec", 1))
        em.wait("vector", "vec", em.cnt["vec"])
        em.emit("vector", lambda e: e.tensor_sub(row(2), row(1), row(2)), inc=("vec", 1))
        em.wait("vector", "vec", em.cnt["vec"])
        vm = em.emit("vector", lambda e: e.tensor_scalar(row(2), row(2), eps, None, ALU.add), inc=("vec", 1))
        em.wait("scalar", "vec", vm)
        sm = em.emit("scalar", lambda e: e.sqrt(row(3), row(2)), inc=("act", 1))
        em.wait("vector", "act", sm)
        em.emit("vector", lambda e: e.reciprocal(row(4), row(3)), inc=("vec", 1))
        em.wait("vector", "vec", em.cnt["vec"])
        em.emit("vector", lambda e: e.tensor_mul(row(5), row(0), row(4)), inc=("vec", 1))
        em.wait("vector", "vec", em.cnt["vec"])
        return em.emit("vector", lambda e: e.tensor_scalar(row(5), row(5), -1.0, None, ALU.mult), inc=("vec", 1))

    def rms_row(ndim, eps):
        em.emit("vector", lambda e: e.tensor_scalar(row(2), ps[6][0:1, :], 1.0 / ndim, eps, ALU.mult, ALU.add), inc=("vec", 1))
        em.wait("scalar", "vec", em.cnt["vec"])
        sm = em.emit("scalar", lambda e: e.sqrt(row(3), row(2)), inc=("act", 1))
        em.wait("vector", "act", sm)
        return em.emit("vector", lambda e: e.reciprocal(row(4), row(3)), inc=("vec", 1))

    with ExitStack() as a1:
        sba = lambda name, shape, dt: a1.enter_context(nc.sbuf_tensor(name, shape, dt))
        wst = sba("wst", [128, 2 * 1024], F32)
        wkvc_t = sba("wkvc_t", [128, 16 * CKV], BF16)
        wkvr_t = sba("wkvr_t", [128, 16 * ROPE], BF16)
        wkbk_t = sba("wkbk_t", [128, 4 * 2048], BF16)
        wkbv_t = sba("wkbv_t", [128, 4 * 2048], BF16)
        xbf = sba("xbf", [128, 16 * 512], BF16)
        xsq = sba("xsq", [128, 16 * 512], BF16)
        ckv_f = sba("ckv_f", [128, 4 * 512], F32)
        cksq = sba("cksq", [128, 2 * 512], BF16)
        ckv_n = sba("ckv_n", [128, 2 * 4 * 512], BF16)
        tmpv = sba("tmpv", [128, 2 * 512], F32)
        w1kv = sba("w1kv", [128, 8], F32)
        kpa = sba("kpa", [ROPE, 512], F32)
        kpr = sba("kpr", [ROPE, 512], F32)
        kpc = sba("kpc", [ROPE, 512], F32)
        kps = sba("kps", [ROPE, 512], F32)
        cosr = sba("cosr", [ROPE, 2 * 512], F32)
        sinr = sba("sinr", [ROPE, 2 * 512], F32)
        kstage = sba("kstage", [128, 16 * 512], BF16)
        vstage = sba("vstage", [128, 4 * 2048], BF16)

        zkp = em.emit("vector", lambda e: e.memset(kpe_t[ROPE:128, :], 0.0), inc=("vec", 1))
        load_weight(em, wst, wkvc_t, wkvc, 16, CKV)
        load_weight(em, wst, wkvr_t, wkvr, 16, ROPE)
        load_weight(em, wst, wkbk_t, wkbk, 4, 2048)
        wcast = load_weight(em, wst, wkbv_t, wkbv, 4, 2048)
        em.wait("tensor", "vec", wcast)
        for oc in range(4):
            for ic in range(16):
                wm = em.emit("tensor", lambda e, oc=oc, ic=ic: e.matmul(
                    ps[oc][:, 0:1], wkvc_t[:, ic * CKV + oc * 128:ic * CKV + oc * 128 + 128],
                    ones_b[:], start=(ic == 0), stop=(ic == 15)),
                    inc=(("ten", 1) if ic == 15 else None))
            em.wait("scalar", "ten", wm)
            em.emit("scalar", lambda e, oc=oc: e.copy(w1kv[:, oc:oc + 1], ps[oc][:, 0:1]), inc=("act", 1))
        for ic in range(16):
            wm = em.emit("tensor", lambda e, ic=ic: e.matmul(
                ps[4][0:ROPE, 0:1], wkvr_t[:, ic * ROPE:(ic + 1) * ROPE],
                ones_b[:], start=(ic == 0), stop=(ic == 15)),
                inc=(("ten", 1) if ic == 15 else None))
        em.wait("scalar", "ten", wm)
        w1done = em.emit("scalar", lambda e: e.copy(w1kv[0:ROPE, 4:5], ps[4][0:ROPE, 0:1]), inc=("act", 1))
        em.wait("tensor", "act", w1done)

        prev = {}

        def emit_xload(j):
            tg = slice(TG * j, TG * (j + 1))
            em.wait("gpsimd", "ten", prev.get("kva", 0))
            em.wait("gpsimd", "act", prev.get("sq_last", 0))
            xbm = None
            for pc in range(4):
                xbm = _dma(em, st_xbf,
                           _s3(xbf, 512)[:, 4 * pc:4 * pc + 4, :],
                           _r3(xT_all[:, tg])[:, 4 * pc:4 * pc + 4, :])
            prev["xbm"] = xbm
            if j > 0:
                em.wait("sync", "vec", prev.get("kpe", 0))
            cm_ = _dma(em, st_cs, cosr[:, (j % 2) * 512:(j % 2) * 512 + 512], cosg[:, tg])
            prev["cs"] = _dma(em, st_cs, sinr[:, (j % 2) * 512:(j % 2) * 512 + 512], sing[:, tg])

        def emit_A(j):
            xbm = prev["xbm"]
            em.wait("tensor", "vec", prev.get("evac_v", 0))
            em.wait("tensor", "act", prev.get("evac_a", 0))
            sq_last = None
            for c in range(16):
                em.wait("scalar", st_xbf, xbm)
                em.wait("scalar", "ten", prev.get("xsq_mm", 0) if c < 2 else 0)
                sq_last = em.emit("scalar", lambda e, c=c: e.square(
                    xsq[:, c * 512:c * 512 + 512], xbf[:, c * 512:c * 512 + 512]), inc=("act", 1))
                prev["sq_last"] = sq_last
            for c in range(16):
                em.wait("tensor", st_xbf, xbm)
                if c == 0:
                    em.wait("tensor", "act", prev.get("evac_a", 0))
                    em.wait("tensor", "act", prev.get("vev", 0))
                    em.wait("tensor", "vec", prev.get("kpe", 0))
                em.emit("tensor", lambda e, c=c: e.matmul(
                    ps[6][0:1, :], ones_b[:], xbf[:, c * 512:c * 512 + 512],
                    start=(c == 0), stop=(c == 15)))
                for o in range(4):
                    em.emit("tensor", lambda e, c=c, o=o: e.ldweights(
                        wkvc_t[:, c * CKV + o * 128:c * CKV + o * 128 + 128]))
                    em.emit("tensor", lambda e, c=c, o=o: e.matmul(
                        ps[o][:], wkvc_t[:, c * CKV + o * 128:c * CKV + o * 128 + 128],
                        xbf[:, c * 512:c * 512 + 512], start=(c == 0), stop=(c == 15)))
                em.emit("tensor", lambda e, c=c: e.ldweights(
                    wkvr_t[:, c * ROPE:c * ROPE + ROPE]))
                prev["kva"] = em.emit("tensor", lambda e, c=c: e.matmul(
                    ps[4][0:ROPE, :], wkvr_t[:, c * ROPE:c * ROPE + ROPE],
                    xbf[:, c * 512:c * 512 + 512], start=(c == 0), stop=(c == 15)), inc=("ten", 1))
            em.wait("tensor", "act", sq_last)
            for c in range(16):
                prev["xsq_mm"] = em.emit("tensor", lambda e, c=c: e.matmul(
                    ps[7][0:1, :], ones_b[:], xsq[:, c * 512:c * 512 + 512],
                    start=(c == 0), stop=(c == 15)), inc=(("ten", 1) if c == 15 else None))
            prev["statmm"] = prev["xsq_mm"]
            em.wait("vector", "ten", prev["statmm"])
            ln_rows(D, EPS_LN)
            em.wait("tensor", "vec", em.cnt["vec"])
            em.emit("tensor", lambda e: e.matmul(ps[6][:], ones_r[:], row(4), start=True, stop=True))
            prev["bm"] = em.emit("tensor", lambda e: e.matmul(ps[7][:], ones_r[:], row(5), start=True, stop=True), inc=("ten", 1))

        def emit_E(j):
            bm = prev["bm"]
            for o in range(4):
                em.wait("scalar", "ten", bm)
                em.wait("scalar", "vec", prev.get(f"ev{o % 2}", 0))
                sc = em.emit("scalar", lambda e, o=o: e.copy(
                    tmpv[:, (o % 2) * 512:(o % 2) * 512 + 512], ps[o][:]), inc=("act", 1))
                em.wait("vector", "act", sc)
                em.wait("vector", "ten", bm)
                em.wait("vector", "vec", em.cnt["vec"])
                em.emit("vector", lambda e, o=o: e.tensor_mul(
                    tmpv[:, (o % 2) * 512:(o % 2) * 512 + 512],
                    tmpv[:, (o % 2) * 512:(o % 2) * 512 + 512], ps[6][:]), inc=("vec", 1))
                em.wait("vector", "vec", em.cnt["vec"])
                prev[f"ev{o % 2}"] = em.emit("vector", lambda e, o=o: e.scalar_tensor_tensor(
                    ckv_f[:, o * 512:o * 512 + 512], ps[7][:], w1kv[:, o:o + 1],
                    tmpv[:, (o % 2) * 512:(o % 2) * 512 + 512], ALU.mult, ALU.add), inc=("vec", 1))
            em.wait("scalar", "ten", bm)
            sc = em.emit("scalar", lambda e: e.copy(kpr[:], ps[4][0:ROPE, :]), inc=("act", 1))
            prev["kpm"] = sc
            em.wait("vector", "act", sc)
            em.emit("vector", lambda e: e.tensor_mul(kpr[:], kpr[:], ps[6][0:ROPE, :]), inc=("vec", 1))
            em.wait("vector", "vec", em.cnt["vec"])
            kpaff = em.emit("vector", lambda e: e.scalar_tensor_tensor(
                kpa[:], ps[7][0:ROPE, :], w1kv[0:ROPE, 4:5], kpr[:], ALU.mult, ALU.add), inc=("vec", 1))
            prev["kpaff"] = kpaff
            prev["evac_a"] = em.cnt["act"]
            prev["evac_v"] = em.cnt["vec"]

        def emit_R(j):
            for o in range(4):
                em.wait("scalar", "vec", prev[f"ev{o % 2}"])
                em.wait("scalar", "ten", prev.get("cksq_mm", 0))
                sqm = em.emit("scalar", lambda e, o=o: e.square(
                    cksq[:, (o % 2) * 512:(o % 2) * 512 + 512], ckv_f[:, o * 512:o * 512 + 512]), inc=("act", 1))
                em.wait("tensor", "act", sqm)
                if o == 0:
                    em.wait("tensor", "vec", prev.get("ckvn", 0))
                    em.wait("tensor", "vec", prev.get("kpe", 0))
                prev["cksq_mm"] = em.emit("tensor", lambda e, o=o: e.matmul(
                    ps[5][0:1, :], ones_b[:], cksq[:, (o % 2) * 512:(o % 2) * 512 + 512],
                    start=(o == 0), stop=(o == 3)), inc=("ten", 1))
            em.wait("vector", "ten", prev["cksq_mm"])
            em.emit("vector", lambda e: e.tensor_scalar(row(6), ps[5][0:1, :], 1.0 / CKV, EPS_RMS, ALU.mult, ALU.add), inc=("vec", 1))
            em.wait("scalar", "vec", em.cnt["vec"])
            smx = em.emit("scalar", lambda e: e.sqrt(row(7), row(6)), inc=("act", 1))
            em.wait("vector", "act", smx)
            rsm = em.emit("vector", lambda e: e.reciprocal(row(6), row(7)), inc=("vec", 1))
            em.wait("tensor", "vec", rsm)
            bmk = em.emit("tensor", lambda e: e.matmul(ps[5][:], ones_r[:], row(6), start=True, stop=True), inc=("ten", 1))
            ckvn_ready = None
            for o in range(4):
                em.wait("vector", "ten", bmk)
                em.wait("vector", "vec", em.cnt["vec"])
                em.emit("vector", lambda e, o=o: e.tensor_mul(
                    tmpv[:, 0:512], ckv_f[:, o * 512:o * 512 + 512], ps[5][:]), inc=("vec", 1))
                em.wait("vector", "vec", em.cnt["vec"])
                ckvn_ready = em.emit("vector", lambda e, o=o, j=j: e.tensor_scalar(
                    ckv_n[:, (j % 2) * 2048 + o * 512:(j % 2) * 2048 + o * 512 + 512],
                    tmpv[:, 0:512], gkv_t[:, o:o + 1], None, ALU.mult), inc=("vec", 1))
            prev["ckvn"] = ckvn_ready
            em.wait("tensor", "vec", prev["kpaff"])
            rotm = em.emit("tensor", lambda e: e.matmul(
                ps[4][0:ROPE, :], rotT[0:ROPE, 0:ROPE], kpa[:], start=True, stop=True), inc=("ten", 1))
            em.wait("vector", st_cs, prev["cs"])
            em.emit("vector", lambda e, j=j: e.tensor_mul(
                kpc[:], kpa[:], cosr[:, (j % 2) * 512:(j % 2) * 512 + 512]), inc=("vec", 1))
            em.wait("vector", "ten", rotm)
            em.emit("vector", lambda e, j=j: e.tensor_mul(
                kps[:], ps[4][0:ROPE, :], sinr[:, (j % 2) * 512:(j % 2) * 512 + 512]), inc=("vec", 1))
            em.wait("vector", "vec", em.cnt["vec"])
            kpe_done = em.emit("vector", lambda e, j=j: e.tensor_add(
                kpe_t[0:ROPE, TG * j:TG * j + TG], kpc[:], kps[:]), inc=("vec", 1))
            prev["kpe"] = kpe_done

        def emit_KB(j):
            tg = slice(TG * j, TG * (j + 1))
            ckvn_ready = prev["ckvn"]
            cb = (j % 2) * 2048
            em.wait("tensor", "act", prev.get("evac_a", 0))
            for oh in range(16):
                pb = oh % 2
                em.wait("tensor", "vec", ckvn_ready)
                em.wait("tensor", "act", prev.get(f"kev{pb}", 0))
                for ic in range(4):
                    em.emit("tensor", lambda e, oh=oh, ic=ic: e.ldweights(
                        wkbk_t[:, ic * 2048 + oh * 128:ic * 2048 + oh * 128 + 128]))
                    lm = em.emit("tensor", lambda e, oh=oh, ic=ic, pb=pb, cb=cb: e.matmul(
                        ps[pb][:], wkbk_t[:, ic * 2048 + oh * 128:ic * 2048 + oh * 128 + 128],
                        ckv_n[:, cb + ic * 512:cb + ic * 512 + 512], start=(ic == 0), stop=(ic == 3)),
                        inc=(("ten", 1) if ic == 3 else None))
                em.wait("scalar", "ten", lm)
                em.wait("scalar", st_ksp, prev.get("kspill", 0) if oh < 2 else 0)
                prev[f"kev{pb}"] = em.emit("scalar", lambda e, oh=oh, pb=pb: e.copy(
                    kstage[:, oh * 512:oh * 512 + 512], ps[pb][:]), inc=("act", 1))
            em.wait("gpsimd", "act", em.cnt["act"])
            for pc in range(4):
                prev["kspill"] = _dma(em, st_ksp,
                                      _r3(kcache[:, tg])[:, 4 * pc:4 * pc + 4, :],
                                      _s3(kstage, 512)[:, 4 * pc:4 * pc + 4, :])
            for ts_ in range(4):
                em.wait("tensor", "vec", ckvn_ready)
                em.wait("tensor", "act", prev.get("vev", 0))
                em.wait("tensor", "act", prev.get("kev0", 0))
                em.wait("tensor", "act", prev.get("kev1", 0))
                if ts_ == 0:
                    em.wait("tensor", st_vsp, prev.get("vspill", 0))
                for ic in range(4):
                    em.emit("tensor", lambda e, ts_=ts_, ic=ic, cb=cb: e.ldweights(
                        ckv_n[:, cb + ic * 512 + ts_ * 128:cb + ic * 512 + ts_ * 128 + 128]))
                    for og in range(4):
                        lm = em.emit("tensor", lambda e, ts_=ts_, ic=ic, og=og, cb=cb: e.matmul(
                            ps[og][:], ckv_n[:, cb + ic * 512 + ts_ * 128:cb + ic * 512 + ts_ * 128 + 128],
                            wkbv_t[:, ic * 2048 + og * 512:ic * 2048 + og * 512 + 512],
                            start=(ic == 0), stop=(ic == 3)),
                            inc=(("ten", 1) if (ic == 3 and og == 3) else None))
                em.wait("scalar", "ten", lm)
                for og in range(4):
                    prev["vev"] = em.emit("scalar", lambda e, ts_=ts_, og=og: e.copy(
                        vstage[:, ts_ * 2048 + og * 512:ts_ * 2048 + og * 512 + 512],
                        ps[og][:]), inc=("act", 1))
            em.wait("gpsimd", "act", em.cnt["act"])
            prev["vspill"] = _dma(em, st_vsp, _r3(vcache[tg, :]), _s3(vstage, 2048))

        for j in range(NG):
            emit_xload(j)
            if j > 0:
                emit_KB(j - 1)
            emit_A(j)
            emit_E(j)
            emit_R(j)
        emit_KB(NG - 1)
        if kpe_dbg is not None:
            em.wait("gpsimd", "vec", em.cnt["vec"])
            _dma(em, st_kpd, kpe_dbg[:], kpe_t[0:ROPE, :])
        em.barrier()

    q1_stack = ExitStack()
    q1_t = q1_stack.enter_context(nc.sbuf_tensor("q1_t", [128, 16 * 2048], FP8))
    with ExitStack() as a2:
        sba = lambda name, shape, dt: a2.enter_context(nc.sbuf_tensor(name, shape, dt))
        wst = sba("wst2", [128, 2 * 512], F32)
        wqa_t = sba("wqa_t", [128, 16 * QA], BF16)
        x_f = sba("x_f2", [128, 16 * 512], F32)
        xbf = sba("xbf2", [128, 16 * 512], BF16)
        xsq = sba("xsq2", [128, 2 * 512], F32)
        xnt = sba("xnt2", [128, 512], F32)
        qa_f = sba("qa_f", [128, 12 * 512], BF16)
        qa_n = sba("qa_n", [128, 12 * 512], BF16)
        qsq = sba("qsq", [128, 12 * 512], BF16)
        tmq = sba("tmq", [128, 2 * 512], F32)
        w1q = sba("w1q", [128, 12], F32)

        wcast = load_weight(em, wst, wqa_t, wqa, 16, QA)
        em.wait("tensor", "vec", wcast)
        w1marks = {}
        for oc in range(12):
            em.wait("tensor", "act", w1marks.get(oc % 4, 0))
            for ic in range(16):
                wm = em.emit("tensor", lambda e, oc=oc, ic=ic: e.matmul(
                    ps[oc % 4][:, 0:1], wqa_t[:, ic * QA + oc * 128:ic * QA + oc * 128 + 128],
                    ones_b[:], start=(ic == 0), stop=(ic == 15)),
                    inc=(("ten", 1) if ic == 15 else None))
            em.wait("scalar", "ten", wm)
            w1marks[oc % 4] = em.emit("scalar", lambda e, oc=oc: e.copy(w1q[:, oc:oc + 1], ps[oc % 4][:, 0:1]), inc=("act", 1))
        em.wait("tensor", "act", max(w1marks.values()))

        prev = {}
        for g in range(OG):
            tg = slice(TG * g, TG * (g + 1))
            if g > 0:
                em.wait("sync", "vec", prev["xn_last"])
            xm = None
            for pc in range(4):
                xm = _dma(em, st_x,
                          _s3(x_f, 512)[:, 4 * pc:4 * pc + 4, :],
                          _r3(xT_own[:, tg])[:, 4 * pc:4 * pc + 4, :])
            em.wait("gpsimd", "ten", prev.get("qa_mm", 0))
            xbm = None
            for pc in range(4):
                xbm = _dma(em, st_xbf,
                           _s3(xbf, 512)[:, 4 * pc:4 * pc + 4, :],
                           _r3(xT_own[:, tg])[:, 4 * pc:4 * pc + 4, :])
            sq_last = None
            for c in range(16):
                em.wait("scalar", st_x, xm)
                em.wait("scalar", "ten", prev.get("statmm", 0))
                sq_last = em.emit("scalar", lambda e, c=c: e.square(
                    xsq[:, (c % 2) * 512:(c % 2) * 512 + 512], x_f[:, c * 512:c * 512 + 512]), inc=("act", 1))
                em.wait("tensor", st_x, xm)
                em.wait("tensor", "act", sq_last)
                if c == 0:
                    em.wait("tensor", "vec", prev.get("xn_last", 0))
                    em.wait("tensor", "vec", prev.get("qan", 0))
                em.emit("tensor", lambda e, c=c: e.matmul(
                    ps[6][0:1, :], ones_c[:], x_f[:, c * 512:c * 512 + 512], start=(c == 0), stop=(c == 15)))
                prev["statmm"] = em.emit("tensor", lambda e, c=c: e.matmul(
                    ps[7][0:1, :], ones_c[:], xsq[:, (c % 2) * 512:(c % 2) * 512 + 512],
                    start=(c == 0), stop=(c == 15)), inc=("ten", 1))
            em.wait("vector", "ten", prev["statmm"])
            ln_rows(D, EPS_LN)
            em.wait("tensor", "vec", em.cnt["vec"])
            em.emit("tensor", lambda e: e.matmul(ps[6][:], ones_r[:], row(4), start=True, stop=True))
            bm = em.emit("tensor", lambda e: e.matmul(ps[7][:], ones_r[:], row(5), start=True, stop=True), inc=("ten", 1))
            def emit_xn_chunk(c, g=g):
                em.wait("vector", "ten", bm)
                em.wait("vector", "vec", em.cnt["vec"])
                em.emit("vector", lambda e, c=c: e.tensor_mul(
                    xnt[:, 0:512], x_f[:, c * 512:c * 512 + 512], ps[6][:]), inc=("vec", 1))
                em.wait("vector", "vec", em.cnt["vec"])
                em.emit("vector", lambda e, c=c: e.tensor_add(
                    xnt[:, 0:512], xnt[:, 0:512], ps[7][:]), inc=("vec", 1))
                em.wait("vector", "vec", em.cnt["vec"])
                prev["xn_last"] = em.emit("vector", lambda e, c=c, g=g: e.tensor_scalar(
                    q1_t[:, c * 2048 + g * 512:c * 2048 + g * 512 + 512],
                    xnt[:, 0:512], sc_t[:, 0:1], None, ALU.mult), inc=("vec", 1))

            sqq = None
            for oc in range(12):
                pb = oc % 4
                ts2 = oc % 2
                em.wait("tensor", st_xbf, xbm)
                em.wait("tensor", "act", prev.get(f"qaev{pb}", 0))
                for c in range(16):
                    em.emit("tensor", lambda e, oc=oc, c=c: e.ldweights(
                        wqa_t[:, c * QA + oc * 128:c * QA + oc * 128 + 128]))
                    qm = em.emit("tensor", lambda e, oc=oc, c=c, pb=pb: e.matmul(
                        ps[pb][:], wqa_t[:, c * QA + oc * 128:c * QA + oc * 128 + 128],
                        xbf[:, c * 512:c * 512 + 512], start=(c == 0), stop=(c == 15)),
                        inc=(("ten", 1) if c == 15 else None))
                prev["qa_mm"] = qm
                em.wait("scalar", "ten", qm)
                em.wait("scalar", "ten", bm)
                em.wait("scalar", "vec", prev.get(f"qmul{ts2}", 0))
                cpm = em.emit("scalar", lambda e, pb=pb, ts2=ts2: e.copy(
                    tmq[:, ts2 * 512:ts2 * 512 + 512], ps[pb][:]), inc=("act", 1))
                prev[f"qaev{pb}"] = cpm
                em.wait("vector", "act", cpm)
                em.wait("vector", "ten", bm)
                em.wait("vector", "vec", em.cnt["vec"])
                prev[f"qmul{ts2}"] = em.emit("vector", lambda e, ts2=ts2: e.tensor_mul(
                    tmq[:, ts2 * 512:ts2 * 512 + 512], tmq[:, ts2 * 512:ts2 * 512 + 512], ps[6][:]), inc=("vec", 1))
                em.wait("vector", "vec", em.cnt["vec"])
                qafm = em.emit("vector", lambda e, oc=oc, ts2=ts2: e.scalar_tensor_tensor(
                    qa_f[:, oc * 512:oc * 512 + 512], ps[7][:], w1q[:, oc:oc + 1],
                    tmq[:, ts2 * 512:ts2 * 512 + 512], ALU.mult, ALU.add), inc=("vec", 1))
                em.wait("scalar", "vec", qafm)
                em.wait("scalar", "ten", prev.get("qsq_mm", 0))
                sqq = em.emit("scalar", lambda e, oc=oc: e.square(
                    qsq[:, oc * 512:oc * 512 + 512], qa_f[:, oc * 512:oc * 512 + 512]), inc=("act", 1))
                emit_xn_chunk(oc)
            for c in range(12, 16):
                emit_xn_chunk(c)
            em.wait("tensor", "act", sqq)
            for oc in range(12):
                prev["qsq_mm"] = em.emit("tensor", lambda e, oc=oc: e.matmul(
                    ps[4][0:1, :], ones_b[:], qsq[:, oc * 512:oc * 512 + 512],
                    start=(oc == 0), stop=(oc == 11)), inc=(("ten", 1) if oc == 11 else None))
            em.wait("vector", "ten", prev["qsq_mm"])
            em.emit("vector", lambda e: e.tensor_scalar(row(6), ps[4][0:1, :], 1.0 / QA, EPS_RMS, ALU.mult, ALU.add), inc=("vec", 1))
            em.wait("scalar", "vec", em.cnt["vec"])
            smx = em.emit("scalar", lambda e: e.sqrt(row(7), row(6)), inc=("act", 1))
            em.wait("vector", "act", smx)
            rsm = em.emit("vector", lambda e: e.reciprocal(row(6), row(7)), inc=("vec", 1))
            em.wait("tensor", "vec", rsm)
            bmq = em.emit("tensor", lambda e: e.matmul(ps[5][:], ones_r[:], row(6), start=True, stop=True), inc=("ten", 1))
            qan = None
            for oc in range(12):
                em.wait("vector", "ten", bmq)
                em.wait("vector", st_qasp, prev.get("qaspill", 0) if oc < 2 else 0)
                em.wait("vector", "vec", em.cnt["vec"])
                em.emit("vector", lambda e, oc=oc: e.tensor_mul(
                    tmq[:, 0:512], qa_f[:, oc * 512:oc * 512 + 512], ps[5][:]), inc=("vec", 1))
                em.wait("vector", "vec", em.cnt["vec"])
                qan = em.emit("vector", lambda e, oc=oc: e.tensor_scalar(
                    qa_n[:, oc * 512:oc * 512 + 512], tmq[:, 0:512], gqa_t[:, oc:oc + 1], None, ALU.mult), inc=("vec", 1))
            prev["qan"] = qan
            em.wait("gpsimd", "vec", qan)
            for pc in range(3):
                prev["qaspill"] = _dma(em, st_qasp,
                                       _r3(qacache[:, tg])[:, 4 * pc:4 * pc + 4, :],
                                       _s3(qa_n, 512)[:, 4 * pc:4 * pc + 4, :])
        em.barrier()

    with ExitStack() as a3:
        sba = lambda name, shape, dt: a3.enter_context(nc.sbuf_tensor(name, shape, dt))
        wst = sba("wst3", [128, 2 * 2048], F32)
        wqb_t = sba("wqb_t", [128, 12 * 3072], BF16)
        qa_ng = sba("qa_ng", [128, 12 * 512], BF16)
        qnstage = sba("qnstage", [128, 16 * 512], BF16)
        qr_f = sba("qr_f", [128, 2 * 512], F32)
        qr_t1 = sba("qr_t1", [128, 512], F32)
        qr_t2 = sba("qr_t2", [128, 512], F32)
        qrstage = sba("qrstage", [128, 8 * 512], BF16)
        cosr2 = sba("cosr2", [128, 512], F32)
        sinr2 = sba("sinr2", [128, 512], F32)

        for ic in range(12):
            load_weight(em, wst,
                        wqb_t[:, ic * 3072:ic * 3072 + 2048],
                        wqbn[ic * 128:(ic + 1) * 128, :], 1, 2048)
        for ic in range(12):
            load_weight(em, wst,
                        wqb_t[:, ic * 3072 + 2048:ic * 3072 + 3072],
                        wqbr[ic * 128:(ic + 1) * 128, :], 1, 1024)
        prev = {}
        for g in range(OG):
            tg = slice(TG * g, TG * (g + 1))
            em.wait("tensor", "vec", em.cnt["vec"])
            em.wait("tensor", "act", em.cnt["act"])
            if g > 0:
                em.wait("sync", "ten", prev["qb_last"])
                em.wait("sync", "vec", prev["qrf_last"])
            qam = None
            for pc in range(3):
                qam = _dma(em, st_x,
                           _s3(qa_ng, 512)[:, 4 * pc:4 * pc + 4, :],
                           _r3(qacache[:, tg])[:, 4 * pc:4 * pc + 4, :])
            _dma(em, st_cs, cosr2[0:ROPE, :], coso[:, tg])
            _dma(em, st_cs, cosr2[ROPE:128, :], coso[:, tg])
            _dma(em, st_cs, sinr2[0:ROPE, :], sino[:, tg])
            sm_ = _dma(em, st_cs, sinr2[ROPE:128, :], sino[:, tg])
            em.wait("tensor", st_x, qam)
            for oh in range(16):
                pb = oh % 2
                em.wait("tensor", "act", prev.get(f"qnev{pb}", 0))
                for ic in range(12):
                    em.emit("tensor", lambda e, oh=oh, ic=ic: e.ldweights(
                        wqb_t[:, ic * 3072 + oh * 128:ic * 3072 + oh * 128 + 128]))
                    qm = em.emit("tensor", lambda e, oh=oh, ic=ic, pb=pb: e.matmul(
                        ps[pb][:], wqb_t[:, ic * 3072 + oh * 128:ic * 3072 + oh * 128 + 128],
                        qa_ng[:, ic * 512:ic * 512 + 512], start=(ic == 0), stop=(ic == 11)),
                        inc=(("ten", 1) if ic == 11 else None))
                em.wait("scalar", "ten", qm)
                em.wait("scalar", st_qnsp, prev.get("qnspill", 0) if oh < 2 else 0)
                prev[f"qnev{pb}"] = em.emit("scalar", lambda e, oh=oh, pb=pb: e.copy(
                    qnstage[:, oh * 512:oh * 512 + 512], ps[pb][:]), inc=("act", 1))
            em.wait("gpsimd", "act", em.cnt["act"])
            for pc in range(4):
                prev["qnspill"] = _dma(em, st_qnsp,
                                       _r3(qncache[:, tg])[:, 4 * pc:4 * pc + 4, :],
                                       _s3(qnstage, 512)[:, 4 * pc:4 * pc + 4, :])
            em.wait("vector", st_cs, sm_)
            qrf_mark = None
            for oc in range(8):
                pb = 2 + (oc % 2)
                pr = 4 + (oc % 2)
                em.wait("tensor", "act", prev.get(f"qrev{pb}", 0))
                for ic in range(12):
                    em.emit("tensor", lambda e, oc=oc, ic=ic: e.ldweights(
                        wqb_t[:, ic * 3072 + 2048 + oc * 128:ic * 3072 + 2048 + oc * 128 + 128]))
                    qm = em.emit("tensor", lambda e, oc=oc, ic=ic, pb=pb: e.matmul(
                        ps[pb][:], wqb_t[:, ic * 3072 + 2048 + oc * 128:ic * 3072 + 2048 + oc * 128 + 128],
                        qa_ng[:, ic * 512:ic * 512 + 512], start=(ic == 0), stop=(ic == 11)),
                        inc=(("ten", 1) if ic == 11 else None))
                em.wait("scalar", "ten", qm)
                em.wait("scalar", "vec", prev.get(f"qrf{oc % 2}", 0))
                evm = em.emit("scalar", lambda e, oc=oc, pb=pb: e.copy(
                    qr_f[:, (oc % 2) * 512:(oc % 2) * 512 + 512], ps[pb][:]), inc=("act", 1))
                prev[f"qrev{pb}"] = evm
                em.wait("tensor", "act", evm)
                em.wait("tensor", "vec", prev.get(f"qrs{oc % 2}", 0))
                rm = em.emit("tensor", lambda e, oc=oc, pr=pr: e.matmul(
                    ps[pr][:], rotT[:], qr_f[:, (oc % 2) * 512:(oc % 2) * 512 + 512],
                    start=True, stop=True), inc=("ten", 1))
                em.wait("vector", "act", evm)
                em.wait("vector", "vec", em.cnt["vec"])
                em.emit("vector", lambda e, oc=oc: e.tensor_mul(
                    qr_t1[:], qr_f[:, (oc % 2) * 512:(oc % 2) * 512 + 512], cosr2[:]), inc=("vec", 1))
                em.wait("vector", "ten", rm)
                prev[f"qrs{oc % 2}"] = em.emit("vector", lambda e, oc=oc, pr=pr: e.tensor_mul(
                    qr_t2[:], ps[pr][:], sinr2[:]), inc=("vec", 1))
                em.wait("vector", "vec", em.cnt["vec"])
                em.wait("vector", st_qrsp, prev.get("qrspill", 0) if oc < 2 else 0)
                qrf_mark = em.emit("vector", lambda e, oc=oc: e.tensor_add(
                    qrstage[:, oc * 512:oc * 512 + 512], qr_t1[:], qr_t2[:]), inc=("vec", 1))
                prev[f"qrf{oc % 2}"] = qrf_mark
            prev["qb_last"] = em.cnt["ten"]
            prev["qrf_last"] = qrf_mark
            em.wait("gpsimd", "vec", em.cnt["vec"])
            for oc in range(8):
                _dma(em, st_qrsp,
                     qrcache[:, (2 * oc) * 2048 + TG * g:(2 * oc) * 2048 + TG * g + TG],
                     qrstage[0:ROPE, oc * 512:oc * 512 + 512])
                prev["qrspill"] = _dma(
                    em, st_qrsp,
                    qrcache[:, (2 * oc + 1) * 2048 + TG * g:(2 * oc + 1) * 2048 + TG * g + TG],
                    qrstage[ROPE:128, oc * 512:oc * 512 + 512])
        em.barrier()

    with ExitStack() as a4:
        sba = lambda name, shape, dt: a4.enter_context(nc.sbuf_tensor(name, shape, dt))
        kseg = [sba(f"kseg{i}", [128, 16 * 512], BF16) for i in range(2)]
        vseg = [sba(f"vseg{i}", [128, 4 * 2048], BF16) for i in range(2)]
        qn_g = sba("qn_g", [128, 16 * 512], BF16)
        qr_g = sba("qr_g", [128, 16 * 512], BF16)
        pt = sba("pt", [128, 8 * 512], BF16)
        ptot = sba("ptot", [128, 16 * 512], BF16)
        acc = sba("acc", [128, 16 * 512], BF16)
        m_t = sba("m_t", [128, 8 * 512], BF16)
        atn = sba("atn", [128, 2 * 512], BF16)

        mm_ = _dma(em, st_mask, m_t[:], masks_d[:])
        zm = em.emit("vector", lambda e: e.memset(qr_g[ROPE:128, :], 0.0), inc=("vec", 1))
        em.wait("tensor", "vec", zm)
        prev = {}
        seg_done = {}
        pt_done = {}
        ptot_done = {}
        ptot_h = {}
        bc_marks = {}
        q_marks = {}
        kv_marks = {}
        seg_slot = {}
        slot_free = {}
        scores_h_done = {}
        segslot = [0]
        for g in range(OG):
            tg = slice(TG * g, TG * (g + 1))
            nseg = 2 * (g + 1)
            nkb_g = 8 * (g + 1)
            if g > 0:
                em.wait("tensor", "vec", em.cnt["vec"])
                em.wait("tensor", "act", em.cnt["act"])

            def load_qh(gq, h):
                stq = em.stream(f"stqg{gq % 2}", "sync")
                if gq > 0:
                    em.wait("sync", "ten", scores_h_done.get((gq - 1, h), 0))
                m1 = _dma(em, stq, _s3(qn_g, 512)[:, h:h + 1, :],
                          _r3(qncache[:, TG * gq:TG * gq + TG])[:, h:h + 1, :])
                m1 = _dma(em, stq, qr_g[0:ROPE, h * 512:h * 512 + 512],
                          qrcache[:, h * 2048 + TG * gq:h * 2048 + TG * gq + TG])
                q_marks[gq] = (stq, m1)

            def load_seg(si, g=g):
                nseg_ = 2 * (g + 1)
                if si >= nseg_:
                    g2 = g + 1
                    si2 = si - nseg_
                    if g2 >= OG or si2 >= 2 * (g2 + 1) or si2 >= 2:
                        return
                    slot = segslot[0] % 2
                    segslot[0] += 1
                    seg_slot[(g2, si2)] = slot
                    em.wait("sync", "ten", slot_free.get(slot, 0))
                    for pc in range(4):
                        _dma(em, st_seg[slot],
                             _s3(kseg[slot], 512)[:, 4 * pc:4 * pc + 4, :],
                             _r3(kcache[:, 512 * si2:512 * si2 + 512])[:, 4 * pc:4 * pc + 4, :])
                    kv_marks[(g2, si2)] = _dma(em, st_seg[slot], _s3(vseg[slot], 2048),
                                               _r3(vcache[512 * si2:512 * si2 + 512, :]))
                    return
                if (g, si) in seg_slot:
                    return
                slot = segslot[0] % 2
                segslot[0] += 1
                seg_slot[(g, si)] = slot
                em.wait("sync", "ten", slot_free.get(slot, 0))
                for pc in range(4):
                    _dma(em, st_seg[slot],
                         _s3(kseg[slot], 512)[:, 4 * pc:4 * pc + 4, :],
                         _r3(kcache[:, 512 * si:512 * si + 512])[:, 4 * pc:4 * pc + 4, :])
                kv_marks[(g, si)] = _dma(em, st_seg[slot], _s3(vseg[slot], 2048),
                                         _r3(vcache[512 * si:512 * si + 512, :]))

            if g == 0:
                for h in range(H):
                    load_qh(0, h)
            load_seg(0)
            load_seg(1)
            tiles = [(seg, h, kb) for seg in range(nseg) for h in range(H)
                     for kb in range(NKB_SEG)]
            BANKS = (2, 3, 5)
            BSZ = 3

            def scores_batch(batch):
                for t, ti in batch:
                    seg, h, kb = t
                    pscore = BANKS[ti % 3]
                    if kb == 0 and h == 0:
                        em.wait("tensor", st_seg[seg_slot[(g, seg)]], kv_marks[(g, seg)])
                    if kb == 0 and h == 0 and seg == 0:
                        em.wait("tensor", q_marks[g][0], q_marks[g][1])
                    em.wait("tensor", "act", prev.get(f"exp{pscore}", 0))
                scm = {}
                for t, ti in batch:
                    seg, h, kb = t
                    pscore = BANKS[ti % 3]
                    sl = seg_slot[(g, seg)]
                    em.emit("tensor", lambda e, sl=sl, h=h, kb=kb: e.ldweights(
                        kseg[sl][:, h * 512 + kb * 128:h * 512 + kb * 128 + 128]))
                    em.emit("tensor", lambda e, sl=sl, h=h, kb=kb, pscore=pscore: e.matmul(
                        ps[pscore][:], kseg[sl][:, h * 512 + kb * 128:h * 512 + kb * 128 + 128],
                        qn_g[:, h * 512:h * 512 + 512], start=True, stop=False))
                    em.emit("tensor", lambda e, seg=seg, kb=kb: e.ldweights(
                        kpe_t[:, seg * 512 + kb * 128:seg * 512 + kb * 128 + 128]))
                    scm[ti] = em.emit("tensor", lambda e, seg=seg, h=h, kb=kb, pscore=pscore: e.matmul(
                        ps[pscore][:], kpe_t[:, seg * 512 + kb * 128:seg * 512 + kb * 128 + 128],
                        qr_g[:, h * 512:h * 512 + 512],
                        start=False, stop=True), inc=("ten", 1))
                    if seg == nseg - 1 and kb == NKB_SEG - 1:
                        scores_h_done[(g, h)] = scm[ti]
                        if g + 1 < OG:
                            load_qh(g + 1, h)
                out = []
                for t, ti in batch:
                    seg, h, kb = t
                    jglob = seg * NKB_SEG + kb
                    r = ti % 8
                    pscore = BANKS[ti % 3]
                    em.wait("scalar", "ten", scm[ti])
                    em.wait("scalar", "ten", pt_done.get(r, 0))
                    em.wait("scalar", "vec", ptot_done.get(r, 0))
                    ex_m = em.emit("scalar", lambda e, r=r, pscore=pscore: e.activation(
                        pt[:, r * 512:r * 512 + 512], ps[pscore][:], AF.Exp, scale=SCALE),
                        inc=("act", 1))
                    prev[f"exp{pscore}"] = ex_m
                    band = jglob >= nkb_g - 8
                    mk_m = None
                    if band:
                        bp = jglob - (nkb_g - 8)
                        em.wait("vector", "act", ex_m)
                        em.wait("vector", st_mask, mm_)
                        mk_m = em.emit("vector", lambda e, r=r, bp=bp: e.tensor_mul(
                            pt[:, r * 512:r * 512 + 512], pt[:, r * 512:r * 512 + 512],
                            m_t[:, bp * 512:bp * 512 + 512]), inc=("vec", 1))
                    out.append((t, ti, ex_m, mk_m))
                return out

            def av_batch(batch):
                for t, ti, ex_m, mk_m in batch:
                    seg, h, kb = t
                    if kb == 0:
                        em.wait("tensor", "vec", prev.get(f"accv{h % 2}", 0))
                    if mk_m is not None:
                        em.wait("tensor", "vec", mk_m)
                    else:
                        em.wait("tensor", "act", ex_m)
                for t, ti, ex_m, mk_m in batch:
                    seg, h, kb = t
                    r = ti % 8
                    pa = 0 if h % 2 == 0 else 4
                    sl = seg_slot[(g, seg)]
                    em.emit("tensor", lambda e, sl=sl, h=h, kb=kb: e.ldweights(
                        vseg[sl][:, kb * 2048 + h * 128:kb * 2048 + h * 128 + 128]))
                    av_m = em.emit("tensor", lambda e, sl=sl, h=h, kb=kb, r=r, pa=pa: e.matmul(
                        ps[pa][:], vseg[sl][:, kb * 2048 + h * 128:kb * 2048 + h * 128 + 128],
                        pt[:, r * 512:r * 512 + 512], start=(kb == 0), stop=(kb == 3)),
                        inc=("ten", 1))
                    pt_done[r] = av_m
                    if mk_m is None:
                        em.wait("vector", "act", ex_m)
                    if seg == 0 and kb == 0:
                        pm = em.emit("vector", lambda e, h=h, r=r: e.tensor_copy(
                            ptot[:, h * 512:h * 512 + 512], pt[:, r * 512:r * 512 + 512]), inc=("vec", 1))
                    else:
                        em.wait("vector", "vec", ptot_h.get(h, 0))
                        pm = em.emit("vector", lambda e, h=h, r=r: e.tensor_add(
                            ptot[:, h * 512:h * 512 + 512], ptot[:, h * 512:h * 512 + 512],
                            pt[:, r * 512:r * 512 + 512]), inc=("vec", 1))
                    ptot_done[r] = pm
                    ptot_h[h] = pm
                    if kb == 3:
                        em.wait("vector", "ten", av_m)
                        if seg == 0:
                            prev[f"accv{h % 2}"] = em.emit("vector", lambda e, h=h, pa=pa: e.tensor_copy(
                                acc[:, h * 512:h * 512 + 512], ps[pa][:]), inc=("vec", 1))
                        else:
                            prev[f"accv{h % 2}"] = em.emit("vector", lambda e, h=h, pa=pa: e.tensor_add(
                                acc[:, h * 512:h * 512 + 512], acc[:, h * 512:h * 512 + 512], ps[pa][:]), inc=("vec", 1))
                        if seg == nseg - 1:
                            emit_norm1(h)
                            if len(norm_pend) > 2:
                                emit_norm2()
                    if kb == 3 and h == H - 1:
                        seg_done[(g, seg)] = em.cnt["ten"]
                        slot_free[seg_slot[(g, seg)]] = em.cnt["ten"]
                        load_seg(seg + 2)

            norm_pend = []

            def emit_norm1(h):
                pu = 1 if h % 2 == 0 else 7
                em.wait("tensor", "vec", ptot_h[h])
                em.wait("tensor", "vec", prev.get(f"rcp{h % 2}", 0))
                sm_m = em.emit("tensor", lambda e, h=h, pu=pu: e.matmul(
                    ps[pu][0:1, :], ones_b[:], ptot[:, h * 512:h * 512 + 512],
                    start=True, stop=True), inc=("ten", 1))
                em.wait("vector", "ten", sm_m)
                em.wait("vector", "ten", bc_marks.get(h % 4, 0))
                def _recip(e, h=h, pu=pu):
                    with nc.allow_low_precision(reason="softmax denominator bcast in bf16"):
                        return e.reciprocal(rowb2(h % 4), ps[pu][0:1, :])
                rc_m = em.emit("vector", _recip, inc=("vec", 1))
                prev[f"rcp{h % 2}"] = rc_m
                norm_pend.append((h, rc_m))

            def emit_norm2(g=g, tg=tg):
                if not norm_pend:
                    return
                h, rc_m = norm_pend.pop(0)
                em.wait("tensor", "vec", rc_m)
                em.wait("tensor", "vec", prev.get("nrm", 0))
                bc_m = em.emit("tensor", lambda e, h=h: e.matmul(
                    ps[6][:], ones_rb[:], rowb2(h % 4), start=True, stop=True), inc=("ten", 1))
                bc_marks[h % 4] = bc_m
                em.wait("vector", "ten", bc_m)
                em.wait("vector", st_at[h % 2], prev.get(f"atsp{h % 2}", 0))
                nm = em.emit("vector", lambda e, h=h: e.tensor_mul(
                    atn[:, (h % 2) * 512:(h % 2) * 512 + 512], acc[:, h * 512:h * 512 + 512], ps[6][:]),
                    inc=("vec", 1))
                prev["nrm"] = nm
                em.wait("gpsimd", "vec", nm)
                prev[f"atsp{h % 2}"] = _dma(
                    em, st_at[h % 2], atcache[h * 128:h * 128 + 128, tg],
                    atn[:, (h % 2) * 512:(h % 2) * 512 + 512])

            batches = [[(t, ti) for t, ti in
                        zip(tiles[i:i + BSZ], range(i, i + BSZ))]
                       for i in range(0, len(tiles), BSZ)]
            pending = None
            for bt in batches:
                done = scores_batch(bt)
                if pending is not None:
                    av_batch(pending)
                pending = done
            av_batch(pending)
            while norm_pend:
                emit_norm2()
            seg_done[(g, "last")] = em.cnt["ten"]
            prev["glast"] = em.cnt["ten"]

        em.barrier()

    with ExitStack() as a5:
        sba = lambda name, shape, dt: a5.enter_context(nc.sbuf_tensor(name, shape, dt))
        wst = sba("wst5", [128, 2 * 2048], F32)
        wst2b = sba("wst5b", [128, 2 * 1024], F32)
        wo_t = sba("wo_t", [128, 16 * 2048], BF16)
        w2q = sba("w2q", [128, 16 * 2048], FP8)
        at_g = sba("at_g", [128, 16 * 512], BF16)
        xrot = sba("xrot", [128, 2 * 512], F32)
        frot = sba("frot", [128, 2 * 512], F32)

        for qc in range(4):
            em.emit("vector", lambda e, qc=qc: e.tensor_copy(
                q1_t[:, qc * 8192:(qc + 1) * 8192].bitcast(U32),
                q1_t[:, qc * 8192:(qc + 1) * 8192].bitcast(U32)), inc=("vec", 1))
        load_weight(em, wst, wo_t, wo, 16, 2048)
        wcast5 = load_weight(em, wst2b, w2q, w2, 16, 2048, fp8_scale=sc_t[:, 1:2], queue="gpsimd")
        em.wait("vector", "vec", wcast5)
        wcast5 = em.emit("vector", lambda e: e.tensor_copy(
            w2q[:].bitcast(U32), w2q[:].bitcast(U32)), inc=("vec", 1))
        em.wait("tensor", "vec", wcast5)
        prev = {}
        for tgi in range(4):
            tg = slice(TG * tgi, TG * (tgi + 1))
            if tgi > 0:
                em.wait("sync", "ten", prev["olast"])
            am = None
            for pc in range(4):
                am = _dma(em, st_x, _s3(at_g, 512)[:, 4 * pc:4 * pc + 4, :],
                          _r3(atcache[:, tg])[:, 4 * pc:4 * pc + 4, :])
            for m in range(16):
                pb = m % 2
                em.wait("sync", "vec", prev.get(f"res{pb}", 0))
                xm = _dma(em, st_xr[pb], xrot[:, pb * 512:pb * 512 + 512],
                          xT_own[m * 128:m * 128 + 128, tg])
                em.wait("tensor", st_x, am)
                em.wait("tensor", "vec", prev.get(f"res{pb}", 0))
                em.wait("tensor", "act", prev.get(f"fsc{pb}", 0))
                for h in range(16):
                    em.emit("tensor", lambda e, h=h, m=m: e.ldweights(
                        wo_t[:, h * 2048 + m * 128:h * 2048 + m * 128 + 128]))
                    em.emit("tensor", lambda e, h=h, m=m, pb=pb: e.matmul(
                        ps[pb][:], wo_t[:, h * 2048 + m * 128:h * 2048 + m * 128 + 128],
                        at_g[:, h * 512:h * 512 + 512], start=(h == 0), stop=(h == 15)))
                for c2 in range(8):
                    wap = w2q[:, 2 * c2 * 2048:(2 * c2 + 2) * 2048].rearrange(
                        "p (a q) -> p a q", q=2048)[:, :, m * 128:m * 128 + 128]
                    xap = q1_t[:, 2 * c2 * 2048:(2 * c2 + 2) * 2048].rearrange(
                        "p (a q) -> p a q", q=2048)[:, :, tgi * 512:tgi * 512 + 512]
                    em.emit("tensor", lambda e, wap=wap: e.ldweights(wap, perf_mode=DR))
                    mm2 = em.emit("tensor", lambda e, wap=wap, xap=xap, pb=pb, c2=c2: e.matmul(
                        ps[2 + pb][:], wap, xap, start=(c2 == 0), stop=(c2 == 7), perf_mode=DR),
                        inc=(("ten", 1) if c2 == 7 else None))
                prev["olast"] = mm2
                em.wait("scalar", "ten", mm2)
                em.wait("scalar", "vec", prev.get(f"res{pb}", 0))
                em.wait("scalar", st_out[pb], prev.get(f"ost{pb}", 0))
                fm = em.emit("scalar", lambda e, pb=pb: e.mul(
                    frot[:, pb * 512:pb * 512 + 512], ps[2 + pb][:], rcol[:]), inc=("act", 1))
                prev[f"fsc{pb}"] = fm
                em.wait("vector", "act", fm)
                em.wait("vector", "ten", mm2)
                em.wait("vector", st_xr[pb], xm)
                em.wait("vector", st_out[pb], prev.get(f"ost{pb}", 0))
                em.emit("vector", lambda e, pb=pb: e.tensor_add(
                    frot[:, pb * 512:pb * 512 + 512], frot[:, pb * 512:pb * 512 + 512], ps[pb][:]),
                    inc=("vec", 1))
                em.wait("vector", "vec", em.cnt["vec"])
                rm = em.emit("vector", lambda e, pb=pb: e.tensor_add(
                    frot[:, pb * 512:pb * 512 + 512], frot[:, pb * 512:pb * 512 + 512],
                    xrot[:, pb * 512:pb * 512 + 512]), inc=("vec", 1))
                prev[f"res{pb}"] = rm
                em.wait("gpsimd", "vec", rm)
                prev[f"ost{pb}"] = _dma(em, st_out[pb], outT[m * 128:m * 128 + 128, tg],
                                        frot[:, pb * 512:pb * 512 + 512])
        for name, queue in em.dstreams.items():
            eng = "sync" if queue == "sync" else "gpsimd"
            em.wait(eng, name, em.cnt[name])
        em.flush()

    q1_stack.close()
    top.close()
    return nc



def _prep_core_inputs(c, x, shared):
    b, h = c // 2, c % 2
    xT = np.ascontiguousarray(x[b].T)
    own = np.concatenate([np.arange(1024 * g + 512 * h, 1024 * g + 512 * h + 512)
                          for g in range(4)])
    xT_own = np.ascontiguousarray(xT[:, own])
    coso_h = np.ascontiguousarray(shared["cosg"][:, own])
    sino_h = np.ascontiguousarray(shared["sing"][:, own])
    kj = np.arange(128)[:, None]
    qi = np.arange(512)[None, :]
    masks = np.concatenate(
        [(128 * bp + kj <= 512 * h + qi).astype(BF) for bp in range(8)], axis=1)
    m = dict(shared)
    m.update({"xT_all": xT, "xT_own": xT_own, "coso": coso_h, "sino": sino_h,
              "masks": masks})
    return m


def prep_all_inputs(L0_attn_hidden_states, L0_attn_q_a_w, L0_attn_q_a_ln_w,
                    L0_attn_q_b_w, L0_attn_kv_a_w, L0_attn_kv_a_ln_w,
                    L0_attn_kv_b_w, L0_attn_o_w, L0_ffn_input2, L0_ffn_scale,
                    L0_ffn_scale1):
    x = np.asarray(L0_attn_hidden_states, np.float32)
    qaw = np.asarray(L0_attn_q_a_w, np.float32)
    qbw = np.asarray(L0_attn_q_b_w, np.float32)
    kvaw = np.asarray(L0_attn_kv_a_w, np.float32)
    kvbw = np.asarray(L0_attn_kv_b_w, np.float32)
    ow = np.asarray(L0_attn_o_w, np.float32)

    qb3 = qbw.reshape(H, QHD, QA)
    kb3 = kvbw.reshape(H, NOPE + VD, CKV)

    inv = THETA ** (-np.arange(0, ROPE, 2, dtype=np.float64) / ROPE)
    t = np.arange(S, dtype=np.float64)
    ang = np.concatenate([np.outer(inv, t), np.outer(inv, t)], 0)

    p64 = np.zeros((ROPE, ROPE), np.float32)
    for i in range(32):
        p64[i, 32 + i] = -1.0
        p64[32 + i, i] = 1.0
    rotT_h = np.zeros((128, 128), np.float32)
    rotT_h[:ROPE, :ROPE] = p64.T
    rotT_h[ROPE:, ROPE:] = p64.T

    s0 = np.float32(np.asarray(L0_ffn_scale))
    s1 = np.float32(np.asarray(L0_ffn_scale1))

    shared = {
        "wqa": np.ascontiguousarray(qaw.T),
        "wqbn": np.ascontiguousarray(qb3[:, :NOPE, :].reshape(H * NOPE, QA).T),
        "wqbr": np.ascontiguousarray(qb3[:, NOPE:, :].reshape(H * ROPE, QA).T),
        "wkvc": np.ascontiguousarray(kvaw[:CKV, :].T),
        "wkvr": np.ascontiguousarray(kvaw[CKV:, :].T),
        "wkbk": np.ascontiguousarray(kb3[:, :NOPE, :].reshape(H * NOPE, CKV).T),
        "wkbv": np.ascontiguousarray(kb3[:, NOPE:, :].reshape(H * VD, CKV).T),
        "wo": np.ascontiguousarray(ow.T),
        "w2": np.ascontiguousarray(np.asarray(L0_ffn_input2, np.float32)),
        "gqa": np.ascontiguousarray(
            np.asarray(L0_attn_q_a_ln_w, np.float32).reshape(QA // 128, 128).T),
        "gkv": np.ascontiguousarray(
            np.asarray(L0_attn_kv_a_ln_w, np.float32).reshape(CKV // 128, 128).T),
        "cosg": np.cos(ang).astype(np.float32),
        "sing": np.sin(ang).astype(np.float32),
        "ones_c": np.ones((128, 1), np.float32),
        "ones_b": np.ones((128, 1), BF),
        "ones_r": np.ones((1, 128), np.float32),
        "ones_rb": np.ones((1, 128), BF),
        "rotT": rotT_h,
        "scales": np.concatenate([np.full((128, 1), s0, np.float32),
                                  np.full((128, 1), s1, np.float32)], 1),
    }
    return [_prep_core_inputs(c, x, shared) for c in range(8)]


def assemble_output(results):
    out = np.empty((B, S, D), np.float32)
    for c in range(8):
        b, h = c // 2, c % 2
        own = np.concatenate([np.arange(1024 * g + 512 * h, 1024 * g + 512 * h + 512)
                              for g in range(4)])
        out[b, own, :] = results[c]["outT"].T
    return out


def kernel(**inputs):
    in_maps = prep_all_inputs(**inputs)
    nc = build_program()
    res = run_bass_kernel_spmd(nc, in_maps, core_ids=list(range(8)))
    return assemble_output(res.results)


# version 39
# speedup vs baseline: 1.0277x; 1.0277x over previous
import numpy as np
import ml_dtypes
from contextlib import ExitStack

import concourse.bass as bass
import concourse.mybir as mybir
from concourse.bass_utils import run_bass_kernel_spmd

F32 = mybir.dt.float32
BF16 = mybir.dt.bfloat16
FP8 = mybir.dt.float8e4
AF = mybir.ActivationFunctionType
ALU = mybir.AluOpType
DR = mybir.MatmulPerfMode.DoubleRow
U32 = mybir.dt.uint32

BF = ml_dtypes.bfloat16

B, S, D = 4, 4096, 2048
H, NOPE, ROPE, VD = 16, 128, 64, 128
QHD = NOPE + ROPE
SCALE = QHD ** -0.5
THETA = 10000.0
CKV = 512
QA = 1536
EPS_LN = 1e-5
EPS_RMS = 1e-6

TG = 512
NG = S // TG
OG = 4
NKB_SEG = 4

ENGINES = ("sync", "tensor", "vector", "scalar", "gpsimd")


class Em:
    def __init__(self, nc, block):
        self.nc = nc
        self.block = block
        self._stack = ExitStack()
        self.sem = {}
        for name in ("ten", "act", "vec", "gsm", "ph"):
            self.sem[name] = self._stack.enter_context(nc.semaphore(name))
        self.cnt = {k: 0 for k in self.sem}
        self.q = {e: [] for e in ENGINES}
        self.lastwait = {}
        self.wst_marks = {}
        self.dstreams = {}

    def stream(self, name, queue):
        if name not in self.sem:
            self.sem[name] = self._stack.enter_context(self.nc.semaphore(name))
            self.cnt[name] = 0
        self.dstreams[name] = queue
        return name

    def emit(self, eng, build, inc=None):
        mark = None
        if inc is not None:
            s, a = inc
            self.cnt[s] += a
            mark = self.cnt[s]
            h = self.sem[s]

            def f(e, build=build, h=h, a=a):
                build(e).then_inc(h, a)
        else:
            def f(e, build=build):
                build(e)
        self.q[eng].append(f)
        return mark

    def wait(self, eng, s, val):
        if not val or val <= 0:
            return
        key = (eng, s)
        if self.lastwait.get(key, 0) >= val:
            return
        self.lastwait[key] = val
        h = self.sem[s]
        self.q[eng].append(lambda e, h=h, v=val: e.wait_ge(h, v))

    def flush(self):
        for eng in ENGINES:
            fns = self.q[eng]
            if not fns:
                continue
            self.q[eng] = []

            def body(e, fns=fns):
                for f in fns:
                    f(e)
            getattr(self.block, eng)(body)

    def barrier(self):
        for name, queue in self.dstreams.items():
            eng = "sync" if queue == "sync" else "gpsimd"
            self.wait(eng, name, self.cnt[name])
        for eng in ENGINES:
            self.cnt["ph"] += 1
            h = self.sem["ph"]
            self.q[eng].append(lambda e, h=h: e.drain().then_inc(h, 1))
        tgt = self.cnt["ph"]
        self.flush()
        for eng in ENGINES:
            self.wait(eng, "ph", tgt)


def _dma(em, stream, out_ap, in_ap):
    queue = em.dstreams[stream]
    return em.emit(queue, lambda e, o=out_ap, i=in_ap: e.dma_start(out=o, in_=i),
                   inc=(stream, 16))


def _r3(ap, p=128):
    return ap.rearrange("(c p) q -> p c q", p=p)


def _s3(ap, q):
    return ap.rearrange("p (c q) -> p c q", q=q)


def load_weight(em, wst, dst, src2d, C, K, fp8_scale=None, queue="sync",
                bitcast_each=False):
    key = id(wst)
    HS = wst.shape[1] // 2
    marks = em.wst_marks.setdefault(key, [])
    for c in range(C):
        for k0 in range(0, K, HS):
            kw = min(HS, K - k0)
            half = len(marks) % 2
            st = em.stream(f"w{half}" if queue == "sync" else f"wg{half}", queue)
            if len(marks) >= 2:
                em.wait("sync" if queue == "sync" else "gpsimd", "vec", marks[-2])
            m = _dma(em, st, wst[:, half * HS:half * HS + kw],
                     src2d[c * 128:(c + 1) * 128, k0:k0 + kw])
            em.wait("vector", st, m)
            d = dst[:, c * K + k0:c * K + k0 + kw]
            s = wst[:, half * HS:half * HS + kw]
            if fp8_scale is None:
                cm = em.emit("vector", lambda e, d=d, s=s: e.tensor_copy(d, s),
                             inc=("vec", 1))
            else:
                cm = em.emit("vector", lambda e, d=d, s=s, sc=fp8_scale:
                             e.tensor_scalar(d, s, sc, None, ALU.mult), inc=("vec", 1))
                if bitcast_each:
                    cm = em.emit("vector", lambda e, d=d: e.tensor_copy(
                        d.bitcast(U32), d.bitcast(U32)), inc=("vec", 1))
            marks.append(cm)
    return marks[-1]


def build_program(debug_outputs=()):
    nc = bass.Bass("TRN2", target_bir_lowering=False, debug=False)

    dk = "ExternalOutput" if "caches" in debug_outputs else "Internal"
    dq = "ExternalOutput" if "q" in debug_outputs else "Internal"
    da = "ExternalOutput" if "at" in debug_outputs else "Internal"

    xT_all = nc.dram_tensor("xT_all", [D, S], F32, kind="ExternalInput").ap()
    xT_own = nc.dram_tensor("xT_own", [D, 2048], F32, kind="ExternalInput").ap()
    wqa = nc.dram_tensor("wqa", [D, QA], F32, kind="ExternalInput").ap()
    wqbn = nc.dram_tensor("wqbn", [QA, H * NOPE], F32, kind="ExternalInput").ap()
    wqbr = nc.dram_tensor("wqbr", [QA, H * ROPE], F32, kind="ExternalInput").ap()
    wkvc = nc.dram_tensor("wkvc", [D, CKV], F32, kind="ExternalInput").ap()
    wkvr = nc.dram_tensor("wkvr", [D, ROPE], F32, kind="ExternalInput").ap()
    wkbk = nc.dram_tensor("wkbk", [CKV, H * NOPE], F32, kind="ExternalInput").ap()
    wkbv = nc.dram_tensor("wkbv", [CKV, H * VD], F32, kind="ExternalInput").ap()
    wo = nc.dram_tensor("wo", [H * VD, D], F32, kind="ExternalInput").ap()
    w2 = nc.dram_tensor("w2", [D, D], F32, kind="ExternalInput").ap()
    gqa = nc.dram_tensor("gqa", [128, QA // 128], F32, kind="ExternalInput").ap()
    gkv = nc.dram_tensor("gkv", [128, CKV // 128], F32, kind="ExternalInput").ap()
    cosg = nc.dram_tensor("cosg", [ROPE, S], F32, kind="ExternalInput").ap()
    sing = nc.dram_tensor("sing", [ROPE, S], F32, kind="ExternalInput").ap()
    coso = nc.dram_tensor("coso", [ROPE, 2048], F32, kind="ExternalInput").ap()
    sino = nc.dram_tensor("sino", [ROPE, 2048], F32, kind="ExternalInput").ap()
    masks_d = nc.dram_tensor("masks", [128, 8 * 512], BF16, kind="ExternalInput").ap()
    ones_c_d = nc.dram_tensor("ones_c", [128, 1], F32, kind="ExternalInput").ap()
    ones_b_d = nc.dram_tensor("ones_b", [128, 1], BF16, kind="ExternalInput").ap()
    ones_r_d = nc.dram_tensor("ones_r", [1, 128], F32, kind="ExternalInput").ap()
    ones_rb_d = nc.dram_tensor("ones_rb", [1, 128], BF16, kind="ExternalInput").ap()
    rotT_d = nc.dram_tensor("rotT", [128, 128], F32, kind="ExternalInput").ap()
    sc_d = nc.dram_tensor("scales", [128, 2], F32, kind="ExternalInput").ap()
    outT = nc.dram_tensor("outT", [D, 2048], F32, kind="ExternalOutput").ap()

    kcache = nc.dram_tensor("kcache", [H * NOPE, S], BF16, kind=dk).ap()
    vcache = nc.dram_tensor("vcache", [S, H * VD], BF16, kind=dk).ap()
    qacache = nc.dram_tensor("qacache", [QA, 2048], BF16, kind="Internal").ap()
    qncache = nc.dram_tensor("qncache", [H * NOPE, 2048], BF16, kind=dq).ap()
    qrcache = nc.dram_tensor("qrcache", [ROPE, H * 2048], BF16, kind=dq).ap()
    atcache = nc.dram_tensor("atcache", [H * VD, 2048], BF16, kind=da).ap()
    wo_bf = nc.dram_tensor("wo_bf", [H * VD, D], BF16, kind="Internal").ap()
    kpe_dbg = (nc.dram_tensor("kpe_dbg", [ROPE, S], BF16, kind="ExternalOutput").ap()
               if "caches" in debug_outputs else None)

    top = ExitStack()
    sb = lambda name, shape, dt: top.enter_context(nc.sbuf_tensor(name, shape, dt))
    ones_c = sb("ones_c_t", [128, 1], F32)
    ones_b = sb("ones_b_t", [128, 1], BF16)
    ones_r = sb("ones_r_t", [1, 128], F32)
    ones_rb = sb("ones_rb_t", [1, 128], BF16)
    rowsb = sb("rowsb", [1, 4 * 512], BF16)
    rotT = sb("rotT_t", [128, 128], F32)
    gqa_t = sb("gqa_t", [128, QA // 128], F32)
    gkv_t = sb("gkv_t", [128, CKV // 128], F32)
    sc_t = sb("sc_t", [128, 2], F32)
    rr_t = sb("rr_t", [128, 2], F32)
    rcol = sb("rcol", [128, 1], F32)
    kpe_t = sb("kpe_t", [128, S], BF16)
    rows = sb("rows", [1, 512 * 8], F32)

    def row(i):
        return rows[0:1, i * 512:(i + 1) * 512]

    def rowb(i):
        return rowsb[0:1, i * 512:(i + 1) * 512]

    rowb2 = rowb

    ps = [top.enter_context(nc.psum_tensor(f"ps{i}", [128, 512], F32)) for i in range(8)]

    block = top.enter_context(nc.Block())
    em = Em(nc, block)
    st_x = em.stream("stx", "sync")
    st_cs = em.stream("stcs", "sync")
    st_mask = em.stream("stmask", "sync")
    st_ksp = em.stream("stksp", "gpsimd")
    st_xbf = em.stream("stxbf", "gpsimd")
    st_vsp = em.stream("stvsp", "gpsimd")
    st_kpd = em.stream("stkpd", "gpsimd")
    st_qasp = em.stream("stqasp", "gpsimd")
    st_qnsp = em.stream("stqnsp", "gpsimd")
    st_qrsp = em.stream("stqrsp", "gpsimd")
    st_seg = [em.stream("stseg0", "sync"), em.stream("stseg1", "sync")]
    st_at = [em.stream("stat0", "gpsimd"), em.stream("stat1", "gpsimd")]
    st_xr = [em.stream("stxr0", "sync"), em.stream("stxr1", "sync")]
    st_out = [em.stream("stout0", "gpsimd"), em.stream("stout1", "gpsimd")]

    st_k = em.stream("konst", "sync")
    for dst, src in ((ones_c, ones_c_d), (ones_b, ones_b_d), (ones_r, ones_r_d),
                     (ones_rb, ones_rb_d),
                     (rotT, rotT_d), (gqa_t, gqa), (gkv_t, gkv), (sc_t, sc_d)):
        _dma(em, st_k, dst[:], src[:])
    em.wait("vector", st_k, em.cnt[st_k])
    rm0 = em.emit("vector", lambda e: e.reciprocal(rr_t[:], sc_t[:]), inc=("vec", 1))
    em.wait("vector", "vec", rm0)
    em.emit("vector", lambda e: e.tensor_mul(rcol[:], rr_t[:, 0:1], rr_t[:, 1:2]),
            inc=("vec", 1))

    def ln_rows(ndim, eps):
        inv = 1.0 / ndim
        em.emit("vector", lambda e: e.tensor_scalar(row(0), ps[6][0:1, :], inv, None, ALU.mult), inc=("vec", 1))
        em.emit("vector", lambda e: e.tensor_scalar(row(1), ps[7][0:1, :], inv, None, ALU.mult), inc=("vec", 1))
        em.wait("vector", "vec", em.cnt["vec"])
        em.emit("vector", lambda e: e.tensor_mul(row(2), row(0), row(0)), inc=("vec", 1))
        em.wait("vector", "vec", em.cnt["vec"])
        em.emit("vector", lambda e: e.tensor_sub(row(2), row(1), row(2)), inc=("vec", 1))
        em.wait("vector", "vec", em.cnt["vec"])
        vm = em.emit("vector", lambda e: e.tensor_scalar(row(2), row(2), eps, None, ALU.add), inc=("vec", 1))
        em.wait("scalar", "vec", vm)
        sm = em.emit("scalar", lambda e: e.sqrt(row(3), row(2)), inc=("act", 1))
        em.wait("vector", "act", sm)
        em.emit("vector", lambda e: e.reciprocal(row(4), row(3)), inc=("vec", 1))
        em.wait("vector", "vec", em.cnt["vec"])
        em.emit("vector", lambda e: e.tensor_mul(row(5), row(0), row(4)), inc=("vec", 1))
        em.wait("vector", "vec", em.cnt["vec"])
        return em.emit("vector", lambda e: e.tensor_scalar(row(5), row(5), -1.0, None, ALU.mult), inc=("vec", 1))

    def rms_row(ndim, eps):
        em.emit("vector", lambda e: e.tensor_scalar(row(2), ps[6][0:1, :], 1.0 / ndim, eps, ALU.mult, ALU.add), inc=("vec", 1))
        em.wait("scalar", "vec", em.cnt["vec"])
        sm = em.emit("scalar", lambda e: e.sqrt(row(3), row(2)), inc=("act", 1))
        em.wait("vector", "act", sm)
        return em.emit("vector", lambda e: e.reciprocal(row(4), row(3)), inc=("vec", 1))

    with ExitStack() as a1:
        sba = lambda name, shape, dt: a1.enter_context(nc.sbuf_tensor(name, shape, dt))
        wst = sba("wst", [128, 2 * 1024], F32)
        wkvc_t = sba("wkvc_t", [128, 16 * CKV], BF16)
        wkvr_t = sba("wkvr_t", [128, 16 * ROPE], BF16)
        wkbk_t = sba("wkbk_t", [128, 4 * 2048], BF16)
        wkbv_t = sba("wkbv_t", [128, 4 * 2048], BF16)
        xbf = sba("xbf", [128, 16 * 512], BF16)
        xsq = sba("xsq", [128, 16 * 512], BF16)
        ckv_f = sba("ckv_f", [128, 4 * 512], F32)
        cksq = sba("cksq", [128, 2 * 512], BF16)
        ckv_n = sba("ckv_n", [128, 2 * 4 * 512], BF16)
        tmpv = sba("tmpv", [128, 2 * 512], F32)
        w1kv = sba("w1kv", [128, 8], F32)
        kpa = sba("kpa", [ROPE, 512], F32)
        kpr = sba("kpr", [ROPE, 512], F32)
        kpc = sba("kpc", [ROPE, 512], F32)
        kps = sba("kps", [ROPE, 512], F32)
        cosr = sba("cosr", [ROPE, 2 * 512], F32)
        sinr = sba("sinr", [ROPE, 2 * 512], F32)
        kstage = sba("kstage", [128, 16 * 512], BF16)
        vstage = sba("vstage", [128, 4 * 2048], BF16)

        zkp = em.emit("vector", lambda e: e.memset(kpe_t[ROPE:128, :], 0.0), inc=("vec", 1))
        load_weight(em, wst, wkvc_t, wkvc, 16, CKV)
        load_weight(em, wst, wkvr_t, wkvr, 16, ROPE)
        load_weight(em, wst, wkbk_t, wkbk, 4, 2048)
        wcast = load_weight(em, wst, wkbv_t, wkbv, 4, 2048)
        em.wait("tensor", "vec", wcast)
        for oc in range(4):
            for ic in range(16):
                wm = em.emit("tensor", lambda e, oc=oc, ic=ic: e.matmul(
                    ps[oc][:, 0:1], wkvc_t[:, ic * CKV + oc * 128:ic * CKV + oc * 128 + 128],
                    ones_b[:], start=(ic == 0), stop=(ic == 15)),
                    inc=(("ten", 1) if ic == 15 else None))
            em.wait("scalar", "ten", wm)
            em.emit("scalar", lambda e, oc=oc: e.copy(w1kv[:, oc:oc + 1], ps[oc][:, 0:1]), inc=("act", 1))
        for ic in range(16):
            wm = em.emit("tensor", lambda e, ic=ic: e.matmul(
                ps[4][0:ROPE, 0:1], wkvr_t[:, ic * ROPE:(ic + 1) * ROPE],
                ones_b[:], start=(ic == 0), stop=(ic == 15)),
                inc=(("ten", 1) if ic == 15 else None))
        em.wait("scalar", "ten", wm)
        w1done = em.emit("scalar", lambda e: e.copy(w1kv[0:ROPE, 4:5], ps[4][0:ROPE, 0:1]), inc=("act", 1))
        em.wait("tensor", "act", w1done)

        prev = {}

        def emit_xload(j):
            tg = slice(TG * j, TG * (j + 1))
            em.wait("gpsimd", "ten", prev.get("kva", 0))
            em.wait("gpsimd", "act", prev.get("sq_last", 0))
            xbm = None
            for pc in range(4):
                xbm = _dma(em, st_xbf,
                           _s3(xbf, 512)[:, 4 * pc:4 * pc + 4, :],
                           _r3(xT_all[:, tg])[:, 4 * pc:4 * pc + 4, :])
            prev["xbm"] = xbm
            if j > 0:
                em.wait("sync", "vec", prev.get("kpe", 0))
            cm_ = _dma(em, st_cs, cosr[:, (j % 2) * 512:(j % 2) * 512 + 512], cosg[:, tg])
            prev["cs"] = _dma(em, st_cs, sinr[:, (j % 2) * 512:(j % 2) * 512 + 512], sing[:, tg])

        def emit_A(j):
            xbm = prev["xbm"]
            em.wait("tensor", "vec", prev.get("evac_v", 0))
            em.wait("tensor", "act", prev.get("evac_a", 0))
            sq_last = None
            for c in range(16):
                em.wait("scalar", st_xbf, xbm)
                em.wait("scalar", "ten", prev.get("xsq_mm", 0) if c < 2 else 0)
                sq_last = em.emit("scalar", lambda e, c=c: e.square(
                    xsq[:, c * 512:c * 512 + 512], xbf[:, c * 512:c * 512 + 512]), inc=("act", 1))
                prev["sq_last"] = sq_last
            for c in range(16):
                em.wait("tensor", st_xbf, xbm)
                if c == 0:
                    em.wait("tensor", "act", prev.get("evac_a", 0))
                    em.wait("tensor", "act", prev.get("vev", 0))
                    em.wait("tensor", "vec", prev.get("kpe", 0))
                em.emit("tensor", lambda e, c=c: e.matmul(
                    ps[6][0:1, :], ones_b[:], xbf[:, c * 512:c * 512 + 512],
                    start=(c == 0), stop=(c == 15)))
                for o in range(4):
                    em.emit("tensor", lambda e, c=c, o=o: e.ldweights(
                        wkvc_t[:, c * CKV + o * 128:c * CKV + o * 128 + 128]))
                    em.emit("tensor", lambda e, c=c, o=o: e.matmul(
                        ps[o][:], wkvc_t[:, c * CKV + o * 128:c * CKV + o * 128 + 128],
                        xbf[:, c * 512:c * 512 + 512], start=(c == 0), stop=(c == 15)))
                em.emit("tensor", lambda e, c=c: e.ldweights(
                    wkvr_t[:, c * ROPE:c * ROPE + ROPE]))
                prev["kva"] = em.emit("tensor", lambda e, c=c: e.matmul(
                    ps[4][0:ROPE, :], wkvr_t[:, c * ROPE:c * ROPE + ROPE],
                    xbf[:, c * 512:c * 512 + 512], start=(c == 0), stop=(c == 15)), inc=("ten", 1))
            em.wait("tensor", "act", sq_last)
            for c in range(16):
                prev["xsq_mm"] = em.emit("tensor", lambda e, c=c: e.matmul(
                    ps[7][0:1, :], ones_b[:], xsq[:, c * 512:c * 512 + 512],
                    start=(c == 0), stop=(c == 15)), inc=(("ten", 1) if c == 15 else None))
            prev["statmm"] = prev["xsq_mm"]
            em.wait("vector", "ten", prev["statmm"])
            ln_rows(D, EPS_LN)
            em.wait("tensor", "vec", em.cnt["vec"])
            em.emit("tensor", lambda e: e.matmul(ps[6][:], ones_r[:], row(4), start=True, stop=True))
            prev["bm"] = em.emit("tensor", lambda e: e.matmul(ps[7][:], ones_r[:], row(5), start=True, stop=True), inc=("ten", 1))

        def emit_E(j):
            bm = prev["bm"]
            for o in range(4):
                em.wait("scalar", "ten", bm)
                em.wait("scalar", "vec", prev.get(f"ev{o % 2}", 0))
                sc = em.emit("scalar", lambda e, o=o: e.copy(
                    tmpv[:, (o % 2) * 512:(o % 2) * 512 + 512], ps[o][:]), inc=("act", 1))
                em.wait("vector", "act", sc)
                em.wait("vector", "ten", bm)
                em.wait("vector", "vec", em.cnt["vec"])
                em.emit("vector", lambda e, o=o: e.tensor_mul(
                    tmpv[:, (o % 2) * 512:(o % 2) * 512 + 512],
                    tmpv[:, (o % 2) * 512:(o % 2) * 512 + 512], ps[6][:]), inc=("vec", 1))
                em.wait("vector", "vec", em.cnt["vec"])
                prev[f"ev{o % 2}"] = em.emit("vector", lambda e, o=o: e.scalar_tensor_tensor(
                    ckv_f[:, o * 512:o * 512 + 512], ps[7][:], w1kv[:, o:o + 1],
                    tmpv[:, (o % 2) * 512:(o % 2) * 512 + 512], ALU.mult, ALU.add), inc=("vec", 1))
            em.wait("scalar", "ten", bm)
            sc = em.emit("scalar", lambda e: e.copy(kpr[:], ps[4][0:ROPE, :]), inc=("act", 1))
            prev["kpm"] = sc
            em.wait("vector", "act", sc)
            em.emit("vector", lambda e: e.tensor_mul(kpr[:], kpr[:], ps[6][0:ROPE, :]), inc=("vec", 1))
            em.wait("vector", "vec", em.cnt["vec"])
            kpaff = em.emit("vector", lambda e: e.scalar_tensor_tensor(
                kpa[:], ps[7][0:ROPE, :], w1kv[0:ROPE, 4:5], kpr[:], ALU.mult, ALU.add), inc=("vec", 1))
            prev["kpaff"] = kpaff
            prev["evac_a"] = em.cnt["act"]
            prev["evac_v"] = em.cnt["vec"]

        def emit_R(j):
            for o in range(4):
                em.wait("scalar", "vec", prev[f"ev{o % 2}"])
                em.wait("scalar", "ten", prev.get("cksq_mm", 0))
                sqm = em.emit("scalar", lambda e, o=o: e.square(
                    cksq[:, (o % 2) * 512:(o % 2) * 512 + 512], ckv_f[:, o * 512:o * 512 + 512]), inc=("act", 1))
                em.wait("tensor", "act", sqm)
                if o == 0:
                    em.wait("tensor", "vec", prev.get("ckvn", 0))
                    em.wait("tensor", "vec", prev.get("kpe", 0))
                prev["cksq_mm"] = em.emit("tensor", lambda e, o=o: e.matmul(
                    ps[5][0:1, :], ones_b[:], cksq[:, (o % 2) * 512:(o % 2) * 512 + 512],
                    start=(o == 0), stop=(o == 3)), inc=("ten", 1))
            em.wait("vector", "ten", prev["cksq_mm"])
            em.emit("vector", lambda e: e.tensor_scalar(row(6), ps[5][0:1, :], 1.0 / CKV, EPS_RMS, ALU.mult, ALU.add), inc=("vec", 1))
            em.wait("scalar", "vec", em.cnt["vec"])
            smx = em.emit("scalar", lambda e: e.sqrt(row(7), row(6)), inc=("act", 1))
            em.wait("vector", "act", smx)
            rsm = em.emit("vector", lambda e: e.reciprocal(row(6), row(7)), inc=("vec", 1))
            em.wait("tensor", "vec", rsm)
            bmk = em.emit("tensor", lambda e: e.matmul(ps[5][:], ones_r[:], row(6), start=True, stop=True), inc=("ten", 1))
            ckvn_ready = None
            for o in range(4):
                em.wait("vector", "ten", bmk)
                em.wait("vector", "vec", em.cnt["vec"])
                em.emit("vector", lambda e, o=o: e.tensor_mul(
                    tmpv[:, 0:512], ckv_f[:, o * 512:o * 512 + 512], ps[5][:]), inc=("vec", 1))
                em.wait("vector", "vec", em.cnt["vec"])
                ckvn_ready = em.emit("vector", lambda e, o=o, j=j: e.tensor_scalar(
                    ckv_n[:, (j % 2) * 2048 + o * 512:(j % 2) * 2048 + o * 512 + 512],
                    tmpv[:, 0:512], gkv_t[:, o:o + 1], None, ALU.mult), inc=("vec", 1))
            prev["ckvn"] = ckvn_ready
            em.wait("tensor", "vec", prev["kpaff"])
            rotm = em.emit("tensor", lambda e: e.matmul(
                ps[4][0:ROPE, :], rotT[0:ROPE, 0:ROPE], kpa[:], start=True, stop=True), inc=("ten", 1))
            em.wait("vector", st_cs, prev["cs"])
            em.emit("vector", lambda e, j=j: e.tensor_mul(
                kpc[:], kpa[:], cosr[:, (j % 2) * 512:(j % 2) * 512 + 512]), inc=("vec", 1))
            em.wait("vector", "ten", rotm)
            em.emit("vector", lambda e, j=j: e.tensor_mul(
                kps[:], ps[4][0:ROPE, :], sinr[:, (j % 2) * 512:(j % 2) * 512 + 512]), inc=("vec", 1))
            em.wait("vector", "vec", em.cnt["vec"])
            kpe_done = em.emit("vector", lambda e, j=j: e.tensor_add(
                kpe_t[0:ROPE, TG * j:TG * j + TG], kpc[:], kps[:]), inc=("vec", 1))
            prev["kpe"] = kpe_done

        def emit_KB(j):
            tg = slice(TG * j, TG * (j + 1))
            ckvn_ready = prev["ckvn"]
            cb = (j % 2) * 2048
            em.wait("tensor", "act", prev.get("evac_a", 0))
            for oh in range(16):
                pb = oh % 2
                em.wait("tensor", "vec", ckvn_ready)
                em.wait("tensor", "act", prev.get(f"kev{pb}", 0))
                for ic in range(4):
                    em.emit("tensor", lambda e, oh=oh, ic=ic: e.ldweights(
                        wkbk_t[:, ic * 2048 + oh * 128:ic * 2048 + oh * 128 + 128]))
                    lm = em.emit("tensor", lambda e, oh=oh, ic=ic, pb=pb, cb=cb: e.matmul(
                        ps[pb][:], wkbk_t[:, ic * 2048 + oh * 128:ic * 2048 + oh * 128 + 128],
                        ckv_n[:, cb + ic * 512:cb + ic * 512 + 512], start=(ic == 0), stop=(ic == 3)),
                        inc=(("ten", 1) if ic == 3 else None))
                em.wait("scalar", "ten", lm)
                em.wait("scalar", st_ksp, prev.get("kspill", 0) if oh < 2 else 0)
                prev[f"kev{pb}"] = em.emit("scalar", lambda e, oh=oh, pb=pb: e.copy(
                    kstage[:, oh * 512:oh * 512 + 512], ps[pb][:]), inc=("act", 1))
            em.wait("gpsimd", "act", em.cnt["act"])
            for pc in range(4):
                prev["kspill"] = _dma(em, st_ksp,
                                      _r3(kcache[:, tg])[:, 4 * pc:4 * pc + 4, :],
                                      _s3(kstage, 512)[:, 4 * pc:4 * pc + 4, :])
            for ts_ in range(4):
                em.wait("tensor", "vec", ckvn_ready)
                em.wait("tensor", "act", prev.get("vev", 0))
                em.wait("tensor", "act", prev.get("kev0", 0))
                em.wait("tensor", "act", prev.get("kev1", 0))
                if ts_ == 0:
                    em.wait("tensor", st_vsp, prev.get("vspill", 0))
                for ic in range(4):
                    em.emit("tensor", lambda e, ts_=ts_, ic=ic, cb=cb: e.ldweights(
                        ckv_n[:, cb + ic * 512 + ts_ * 128:cb + ic * 512 + ts_ * 128 + 128]))
                    for og in range(4):
                        lm = em.emit("tensor", lambda e, ts_=ts_, ic=ic, og=og, cb=cb: e.matmul(
                            ps[og][:], ckv_n[:, cb + ic * 512 + ts_ * 128:cb + ic * 512 + ts_ * 128 + 128],
                            wkbv_t[:, ic * 2048 + og * 512:ic * 2048 + og * 512 + 512],
                            start=(ic == 0), stop=(ic == 3)),
                            inc=(("ten", 1) if (ic == 3 and og == 3) else None))
                em.wait("scalar", "ten", lm)
                for og in range(4):
                    prev["vev"] = em.emit("scalar", lambda e, ts_=ts_, og=og: e.copy(
                        vstage[:, ts_ * 2048 + og * 512:ts_ * 2048 + og * 512 + 512],
                        ps[og][:]), inc=("act", 1))
            em.wait("gpsimd", "act", em.cnt["act"])
            prev["vspill"] = _dma(em, st_vsp, _r3(vcache[tg, :]), _s3(vstage, 2048))

        for j in range(NG):
            emit_xload(j)
            if j > 0:
                emit_KB(j - 1)
            emit_A(j)
            emit_E(j)
            emit_R(j)
        emit_KB(NG - 1)
        if kpe_dbg is not None:
            em.wait("gpsimd", "vec", em.cnt["vec"])
            _dma(em, st_kpd, kpe_dbg[:], kpe_t[0:ROPE, :])
        em.barrier()

    q1_stack = ExitStack()
    q1_t = q1_stack.enter_context(nc.sbuf_tensor("q1_t", [128, 16 * 2048], FP8))
    with ExitStack() as a2:
        sba = lambda name, shape, dt: a2.enter_context(nc.sbuf_tensor(name, shape, dt))
        wst = sba("wst2", [128, 2 * 512], F32)
        wqa_t = sba("wqa_t", [128, 16 * QA], BF16)
        x_f = sba("x_f2", [128, 16 * 512], F32)
        xbf = sba("xbf2", [128, 16 * 512], BF16)
        xsq = sba("xsq2", [128, 2 * 512], F32)
        xnt = sba("xnt2", [128, 512], F32)
        qa_f = sba("qa_f", [128, 12 * 512], BF16)
        qa_n = sba("qa_n", [128, 12 * 512], BF16)
        qsq = sba("qsq", [128, 12 * 512], BF16)
        tmq = sba("tmq", [128, 2 * 512], F32)
        w1q = sba("w1q", [128, 12], F32)

        wcast = load_weight(em, wst, wqa_t, wqa, 16, QA)
        em.wait("tensor", "vec", wcast)
        w1marks = {}
        for oc in range(12):
            em.wait("tensor", "act", w1marks.get(oc % 4, 0))
            for ic in range(16):
                wm = em.emit("tensor", lambda e, oc=oc, ic=ic: e.matmul(
                    ps[oc % 4][:, 0:1], wqa_t[:, ic * QA + oc * 128:ic * QA + oc * 128 + 128],
                    ones_b[:], start=(ic == 0), stop=(ic == 15)),
                    inc=(("ten", 1) if ic == 15 else None))
            em.wait("scalar", "ten", wm)
            w1marks[oc % 4] = em.emit("scalar", lambda e, oc=oc: e.copy(w1q[:, oc:oc + 1], ps[oc % 4][:, 0:1]), inc=("act", 1))
        em.wait("tensor", "act", max(w1marks.values()))

        prev = {}
        for g in range(OG):
            tg = slice(TG * g, TG * (g + 1))
            if g > 0:
                em.wait("sync", "vec", prev["xn_last"])
            xm = None
            for pc in range(4):
                xm = _dma(em, st_x,
                          _s3(x_f, 512)[:, 4 * pc:4 * pc + 4, :],
                          _r3(xT_own[:, tg])[:, 4 * pc:4 * pc + 4, :])
            em.wait("gpsimd", "ten", prev.get("qa_mm", 0))
            xbm = None
            for pc in range(4):
                xbm = _dma(em, st_xbf,
                           _s3(xbf, 512)[:, 4 * pc:4 * pc + 4, :],
                           _r3(xT_own[:, tg])[:, 4 * pc:4 * pc + 4, :])
            sq_last = None
            for c in range(16):
                em.wait("scalar", st_x, xm)
                em.wait("scalar", "ten", prev.get("statmm", 0))
                sq_last = em.emit("scalar", lambda e, c=c: e.square(
                    xsq[:, (c % 2) * 512:(c % 2) * 512 + 512], x_f[:, c * 512:c * 512 + 512]), inc=("act", 1))
                em.wait("tensor", st_x, xm)
                em.wait("tensor", "act", sq_last)
                if c == 0:
                    em.wait("tensor", "vec", prev.get("xn_last", 0))
                    em.wait("tensor", "vec", prev.get("qan", 0))
                em.emit("tensor", lambda e, c=c: e.matmul(
                    ps[6][0:1, :], ones_c[:], x_f[:, c * 512:c * 512 + 512], start=(c == 0), stop=(c == 15)))
                prev["statmm"] = em.emit("tensor", lambda e, c=c: e.matmul(
                    ps[7][0:1, :], ones_c[:], xsq[:, (c % 2) * 512:(c % 2) * 512 + 512],
                    start=(c == 0), stop=(c == 15)), inc=("ten", 1))
            em.wait("vector", "ten", prev["statmm"])
            ln_rows(D, EPS_LN)
            em.wait("tensor", "vec", em.cnt["vec"])
            em.emit("tensor", lambda e: e.matmul(ps[6][:], ones_r[:], row(4), start=True, stop=True))
            bm = em.emit("tensor", lambda e: e.matmul(ps[7][:], ones_r[:], row(5), start=True, stop=True), inc=("ten", 1))
            def emit_xn_chunk(c, g=g):
                em.wait("vector", "ten", bm)
                em.wait("vector", "vec", em.cnt["vec"])
                em.emit("vector", lambda e, c=c: e.tensor_mul(
                    xnt[:, 0:512], x_f[:, c * 512:c * 512 + 512], ps[6][:]), inc=("vec", 1))
                em.wait("vector", "vec", em.cnt["vec"])
                em.emit("vector", lambda e, c=c: e.tensor_add(
                    xnt[:, 0:512], xnt[:, 0:512], ps[7][:]), inc=("vec", 1))
                em.wait("vector", "vec", em.cnt["vec"])
                prev["xn_last"] = em.emit("vector", lambda e, c=c, g=g: e.tensor_scalar(
                    q1_t[:, c * 2048 + g * 512:c * 2048 + g * 512 + 512],
                    xnt[:, 0:512], sc_t[:, 0:1], None, ALU.mult), inc=("vec", 1))

            sqq = None
            for oc in range(12):
                pb = oc % 4
                ts2 = oc % 2
                em.wait("tensor", st_xbf, xbm)
                em.wait("tensor", "act", prev.get(f"qaev{pb}", 0))
                for c in range(16):
                    em.emit("tensor", lambda e, oc=oc, c=c: e.ldweights(
                        wqa_t[:, c * QA + oc * 128:c * QA + oc * 128 + 128]))
                    qm = em.emit("tensor", lambda e, oc=oc, c=c, pb=pb: e.matmul(
                        ps[pb][:], wqa_t[:, c * QA + oc * 128:c * QA + oc * 128 + 128],
                        xbf[:, c * 512:c * 512 + 512], start=(c == 0), stop=(c == 15)),
                        inc=(("ten", 1) if c == 15 else None))
                prev["qa_mm"] = qm
                em.wait("scalar", "ten", qm)
                em.wait("scalar", "ten", bm)
                em.wait("scalar", "vec", prev.get(f"qmul{ts2}", 0))
                cpm = em.emit("scalar", lambda e, pb=pb, ts2=ts2: e.copy(
                    tmq[:, ts2 * 512:ts2 * 512 + 512], ps[pb][:]), inc=("act", 1))
                prev[f"qaev{pb}"] = cpm
                em.wait("vector", "act", cpm)
                em.wait("vector", "ten", bm)
                em.wait("vector", "vec", em.cnt["vec"])
                prev[f"qmul{ts2}"] = em.emit("vector", lambda e, ts2=ts2: e.tensor_mul(
                    tmq[:, ts2 * 512:ts2 * 512 + 512], tmq[:, ts2 * 512:ts2 * 512 + 512], ps[6][:]), inc=("vec", 1))
                em.wait("vector", "vec", em.cnt["vec"])
                qafm = em.emit("vector", lambda e, oc=oc, ts2=ts2: e.scalar_tensor_tensor(
                    qa_f[:, oc * 512:oc * 512 + 512], ps[7][:], w1q[:, oc:oc + 1],
                    tmq[:, ts2 * 512:ts2 * 512 + 512], ALU.mult, ALU.add), inc=("vec", 1))
                em.wait("scalar", "vec", qafm)
                em.wait("scalar", "ten", prev.get("qsq_mm", 0))
                sqq = em.emit("scalar", lambda e, oc=oc: e.square(
                    qsq[:, oc * 512:oc * 512 + 512], qa_f[:, oc * 512:oc * 512 + 512]), inc=("act", 1))
                emit_xn_chunk(oc)
            for c in range(12, 16):
                emit_xn_chunk(c)
            em.wait("tensor", "act", sqq)
            for oc in range(12):
                prev["qsq_mm"] = em.emit("tensor", lambda e, oc=oc: e.matmul(
                    ps[4][0:1, :], ones_b[:], qsq[:, oc * 512:oc * 512 + 512],
                    start=(oc == 0), stop=(oc == 11)), inc=(("ten", 1) if oc == 11 else None))
            em.wait("vector", "ten", prev["qsq_mm"])
            em.emit("vector", lambda e: e.tensor_scalar(row(6), ps[4][0:1, :], 1.0 / QA, EPS_RMS, ALU.mult, ALU.add), inc=("vec", 1))
            em.wait("scalar", "vec", em.cnt["vec"])
            smx = em.emit("scalar", lambda e: e.sqrt(row(7), row(6)), inc=("act", 1))
            em.wait("vector", "act", smx)
            rsm = em.emit("vector", lambda e: e.reciprocal(row(6), row(7)), inc=("vec", 1))
            em.wait("tensor", "vec", rsm)
            bmq = em.emit("tensor", lambda e: e.matmul(ps[5][:], ones_r[:], row(6), start=True, stop=True), inc=("ten", 1))
            qan = None
            for oc in range(12):
                em.wait("vector", "ten", bmq)
                em.wait("vector", st_qasp, prev.get("qaspill", 0) if oc < 2 else 0)
                em.wait("vector", "vec", em.cnt["vec"])
                em.emit("vector", lambda e, oc=oc: e.tensor_mul(
                    tmq[:, 0:512], qa_f[:, oc * 512:oc * 512 + 512], ps[5][:]), inc=("vec", 1))
                em.wait("vector", "vec", em.cnt["vec"])
                qan = em.emit("vector", lambda e, oc=oc: e.tensor_scalar(
                    qa_n[:, oc * 512:oc * 512 + 512], tmq[:, 0:512], gqa_t[:, oc:oc + 1], None, ALU.mult), inc=("vec", 1))
            prev["qan"] = qan
            em.wait("gpsimd", "vec", qan)
            for pc in range(3):
                prev["qaspill"] = _dma(em, st_qasp,
                                       _r3(qacache[:, tg])[:, 4 * pc:4 * pc + 4, :],
                                       _s3(qa_n, 512)[:, 4 * pc:4 * pc + 4, :])
        em.barrier()

    with ExitStack() as a3:
        sba = lambda name, shape, dt: a3.enter_context(nc.sbuf_tensor(name, shape, dt))
        wst = sba("wst3", [128, 2 * 2048], F32)
        wqb_t = sba("wqb_t", [128, 12 * 3072], BF16)
        qa_ng = sba("qa_ng", [128, 12 * 512], BF16)
        qnstage = sba("qnstage", [128, 16 * 512], BF16)
        qr_f = sba("qr_f", [128, 2 * 512], F32)
        qr_t1 = sba("qr_t1", [128, 512], F32)
        qr_t2 = sba("qr_t2", [128, 512], F32)
        qrstage = sba("qrstage", [128, 8 * 512], BF16)
        cosr2 = sba("cosr2", [128, 512], F32)
        sinr2 = sba("sinr2", [128, 512], F32)

        for ic in range(12):
            load_weight(em, wst,
                        wqb_t[:, ic * 3072:ic * 3072 + 2048],
                        wqbn[ic * 128:(ic + 1) * 128, :], 1, 2048)
        for ic in range(12):
            load_weight(em, wst,
                        wqb_t[:, ic * 3072 + 2048:ic * 3072 + 3072],
                        wqbr[ic * 128:(ic + 1) * 128, :], 1, 1024)
        prev = {}
        for g in range(OG):
            tg = slice(TG * g, TG * (g + 1))
            em.wait("tensor", "vec", em.cnt["vec"])
            em.wait("tensor", "act", em.cnt["act"])
            if g > 0:
                em.wait("sync", "ten", prev["qb_last"])
                em.wait("sync", "vec", prev["qrf_last"])
            qam = None
            for pc in range(3):
                qam = _dma(em, st_x,
                           _s3(qa_ng, 512)[:, 4 * pc:4 * pc + 4, :],
                           _r3(qacache[:, tg])[:, 4 * pc:4 * pc + 4, :])
            _dma(em, st_cs, cosr2[0:ROPE, :], coso[:, tg])
            _dma(em, st_cs, cosr2[ROPE:128, :], coso[:, tg])
            _dma(em, st_cs, sinr2[0:ROPE, :], sino[:, tg])
            sm_ = _dma(em, st_cs, sinr2[ROPE:128, :], sino[:, tg])
            em.wait("tensor", st_x, qam)
            for oh in range(16):
                pb = oh % 2
                em.wait("tensor", "act", prev.get(f"qnev{pb}", 0))
                for ic in range(12):
                    em.emit("tensor", lambda e, oh=oh, ic=ic: e.ldweights(
                        wqb_t[:, ic * 3072 + oh * 128:ic * 3072 + oh * 128 + 128]))
                    qm = em.emit("tensor", lambda e, oh=oh, ic=ic, pb=pb: e.matmul(
                        ps[pb][:], wqb_t[:, ic * 3072 + oh * 128:ic * 3072 + oh * 128 + 128],
                        qa_ng[:, ic * 512:ic * 512 + 512], start=(ic == 0), stop=(ic == 11)),
                        inc=(("ten", 1) if ic == 11 else None))
                em.wait("scalar", "ten", qm)
                em.wait("scalar", st_qnsp, prev.get("qnspill", 0) if oh < 2 else 0)
                prev[f"qnev{pb}"] = em.emit("scalar", lambda e, oh=oh, pb=pb: e.copy(
                    qnstage[:, oh * 512:oh * 512 + 512], ps[pb][:]), inc=("act", 1))
            em.wait("gpsimd", "act", em.cnt["act"])
            for pc in range(4):
                prev["qnspill"] = _dma(em, st_qnsp,
                                       _r3(qncache[:, tg])[:, 4 * pc:4 * pc + 4, :],
                                       _s3(qnstage, 512)[:, 4 * pc:4 * pc + 4, :])
            em.wait("vector", st_cs, sm_)
            qrf_mark = None
            for oc in range(8):
                pb = 2 + (oc % 2)
                pr = 4 + (oc % 2)
                em.wait("tensor", "act", prev.get(f"qrev{pb}", 0))
                for ic in range(12):
                    em.emit("tensor", lambda e, oc=oc, ic=ic: e.ldweights(
                        wqb_t[:, ic * 3072 + 2048 + oc * 128:ic * 3072 + 2048 + oc * 128 + 128]))
                    qm = em.emit("tensor", lambda e, oc=oc, ic=ic, pb=pb: e.matmul(
                        ps[pb][:], wqb_t[:, ic * 3072 + 2048 + oc * 128:ic * 3072 + 2048 + oc * 128 + 128],
                        qa_ng[:, ic * 512:ic * 512 + 512], start=(ic == 0), stop=(ic == 11)),
                        inc=(("ten", 1) if ic == 11 else None))
                em.wait("scalar", "ten", qm)
                em.wait("scalar", "vec", prev.get(f"qrf{oc % 2}", 0))
                evm = em.emit("scalar", lambda e, oc=oc, pb=pb: e.copy(
                    qr_f[:, (oc % 2) * 512:(oc % 2) * 512 + 512], ps[pb][:]), inc=("act", 1))
                prev[f"qrev{pb}"] = evm
                em.wait("tensor", "act", evm)
                em.wait("tensor", "vec", prev.get(f"qrs{oc % 2}", 0))
                rm = em.emit("tensor", lambda e, oc=oc, pr=pr: e.matmul(
                    ps[pr][:], rotT[:], qr_f[:, (oc % 2) * 512:(oc % 2) * 512 + 512],
                    start=True, stop=True), inc=("ten", 1))
                em.wait("vector", "act", evm)
                em.wait("vector", "vec", em.cnt["vec"])
                em.emit("vector", lambda e, oc=oc: e.tensor_mul(
                    qr_t1[:], qr_f[:, (oc % 2) * 512:(oc % 2) * 512 + 512], cosr2[:]), inc=("vec", 1))
                em.wait("vector", "ten", rm)
                prev[f"qrs{oc % 2}"] = em.emit("vector", lambda e, oc=oc, pr=pr: e.tensor_mul(
                    qr_t2[:], ps[pr][:], sinr2[:]), inc=("vec", 1))
                em.wait("vector", "vec", em.cnt["vec"])
                em.wait("vector", st_qrsp, prev.get("qrspill", 0) if oc < 2 else 0)
                qrf_mark = em.emit("vector", lambda e, oc=oc: e.tensor_add(
                    qrstage[:, oc * 512:oc * 512 + 512], qr_t1[:], qr_t2[:]), inc=("vec", 1))
                prev[f"qrf{oc % 2}"] = qrf_mark
            prev["qb_last"] = em.cnt["ten"]
            prev["qrf_last"] = qrf_mark
            em.wait("gpsimd", "vec", em.cnt["vec"])
            for oc in range(8):
                _dma(em, st_qrsp,
                     qrcache[:, (2 * oc) * 2048 + TG * g:(2 * oc) * 2048 + TG * g + TG],
                     qrstage[0:ROPE, oc * 512:oc * 512 + 512])
                prev["qrspill"] = _dma(
                    em, st_qrsp,
                    qrcache[:, (2 * oc + 1) * 2048 + TG * g:(2 * oc + 1) * 2048 + TG * g + TG],
                    qrstage[ROPE:128, oc * 512:oc * 512 + 512])
        em.barrier()

    with ExitStack() as a4:
        sba = lambda name, shape, dt: a4.enter_context(nc.sbuf_tensor(name, shape, dt))
        kseg = [sba(f"kseg{i}", [128, 16 * 512], BF16) for i in range(2)]
        vseg = [sba(f"vseg{i}", [128, 4 * 2048], BF16) for i in range(2)]
        qn_g = sba("qn_g", [128, 16 * 512], BF16)
        qr_g = sba("qr_g", [128, 16 * 512], BF16)
        pt = sba("pt", [128, 8 * 512], BF16)
        ptot = sba("ptot", [128, 16 * 512], BF16)
        acc = sba("acc", [128, 16 * 512], BF16)
        m_t = sba("m_t", [128, 8 * 512], BF16)
        atn = sba("atn", [128, 2 * 512], BF16)

        mm_ = _dma(em, st_mask, m_t[:], masks_d[:])
        zm = em.emit("vector", lambda e: e.memset(qr_g[ROPE:128, :], 0.0), inc=("vec", 1))
        em.wait("tensor", "vec", zm)

        st_wpre = em.stream("stwpre", "gpsimd")
        for c in range(16):
            _dma(em, st_wpre, wo_bf[c * 128:(c + 1) * 128, :],
                 wo[c * 128:(c + 1) * 128, :])
        prev = {}
        seg_done = {}
        pt_done = {}
        ptot_done = {}
        ptot_h = {}
        bc_marks = {}
        q_marks = {}
        kv_marks = {}
        seg_slot = {}
        slot_free = {}
        scores_h_done = {}
        segslot = [0]
        for g in range(OG):
            tg = slice(TG * g, TG * (g + 1))
            nseg = 2 * (g + 1)
            nkb_g = 8 * (g + 1)
            if g > 0:
                em.wait("tensor", "vec", em.cnt["vec"])
                em.wait("tensor", "act", em.cnt["act"])

            def load_qh(gq, h):
                stq = em.stream(f"stqg{gq % 2}", "sync")
                if gq > 0:
                    em.wait("sync", "ten", scores_h_done.get((gq - 1, h), 0))
                m1 = _dma(em, stq, _s3(qn_g, 512)[:, h:h + 1, :],
                          _r3(qncache[:, TG * gq:TG * gq + TG])[:, h:h + 1, :])
                m1 = _dma(em, stq, qr_g[0:ROPE, h * 512:h * 512 + 512],
                          qrcache[:, h * 2048 + TG * gq:h * 2048 + TG * gq + TG])
                q_marks[gq] = (stq, m1)

            def load_seg(si, g=g):
                nseg_ = 2 * (g + 1)
                if si >= nseg_:
                    g2 = g + 1
                    si2 = si - nseg_
                    if g2 >= OG or si2 >= 2 * (g2 + 1) or si2 >= 2:
                        return
                    slot = segslot[0] % 2
                    segslot[0] += 1
                    seg_slot[(g2, si2)] = slot
                    em.wait("sync", "ten", slot_free.get(slot, 0))
                    for pc in range(4):
                        _dma(em, st_seg[slot],
                             _s3(kseg[slot], 512)[:, 4 * pc:4 * pc + 4, :],
                             _r3(kcache[:, 512 * si2:512 * si2 + 512])[:, 4 * pc:4 * pc + 4, :])
                    kv_marks[(g2, si2)] = _dma(em, st_seg[slot], _s3(vseg[slot], 2048),
                                               _r3(vcache[512 * si2:512 * si2 + 512, :]))
                    return
                if (g, si) in seg_slot:
                    return
                slot = segslot[0] % 2
                segslot[0] += 1
                seg_slot[(g, si)] = slot
                em.wait("sync", "ten", slot_free.get(slot, 0))
                for pc in range(4):
                    _dma(em, st_seg[slot],
                         _s3(kseg[slot], 512)[:, 4 * pc:4 * pc + 4, :],
                         _r3(kcache[:, 512 * si:512 * si + 512])[:, 4 * pc:4 * pc + 4, :])
                kv_marks[(g, si)] = _dma(em, st_seg[slot], _s3(vseg[slot], 2048),
                                         _r3(vcache[512 * si:512 * si + 512, :]))

            if g == 0:
                for h in range(H):
                    load_qh(0, h)
            load_seg(0)
            load_seg(1)
            tiles = [(seg, h, kb) for seg in range(nseg) for h in range(H)
                     for kb in range(NKB_SEG)]
            BANKS = (2, 3, 5)
            BSZ = 2

            def scores_batch(batch):
                for t, ti in batch:
                    seg, h, kb = t
                    pscore = BANKS[ti % 3]
                    if kb == 0 and h == 0:
                        em.wait("tensor", st_seg[seg_slot[(g, seg)]], kv_marks[(g, seg)])
                    if kb == 0 and h == 0 and seg == 0:
                        em.wait("tensor", q_marks[g][0], q_marks[g][1])
                    em.wait("tensor", "act", prev.get(f"exp{pscore}", 0))
                scm = {}
                for t, ti in batch:
                    seg, h, kb = t
                    pscore = BANKS[ti % 3]
                    sl = seg_slot[(g, seg)]
                    em.emit("tensor", lambda e, sl=sl, h=h, kb=kb: e.ldweights(
                        kseg[sl][:, h * 512 + kb * 128:h * 512 + kb * 128 + 128]))
                    em.emit("tensor", lambda e, sl=sl, h=h, kb=kb, pscore=pscore: e.matmul(
                        ps[pscore][:], kseg[sl][:, h * 512 + kb * 128:h * 512 + kb * 128 + 128],
                        qn_g[:, h * 512:h * 512 + 512], start=True, stop=False))
                    em.emit("tensor", lambda e, seg=seg, kb=kb: e.ldweights(
                        kpe_t[:, seg * 512 + kb * 128:seg * 512 + kb * 128 + 128]))
                    scm[ti] = em.emit("tensor", lambda e, seg=seg, h=h, kb=kb, pscore=pscore: e.matmul(
                        ps[pscore][:], kpe_t[:, seg * 512 + kb * 128:seg * 512 + kb * 128 + 128],
                        qr_g[:, h * 512:h * 512 + 512],
                        start=False, stop=True), inc=("ten", 1))
                    if seg == nseg - 1 and kb == NKB_SEG - 1:
                        scores_h_done[(g, h)] = scm[ti]
                        if g + 1 < OG:
                            load_qh(g + 1, h)
                out = []
                for t, ti in batch:
                    seg, h, kb = t
                    jglob = seg * NKB_SEG + kb
                    r = ti % 8
                    pscore = BANKS[ti % 3]
                    em.wait("scalar", "ten", scm[ti])
                    em.wait("scalar", "ten", pt_done.get(r, 0))
                    em.wait("scalar", "vec", ptot_done.get(r, 0))
                    ex_m = em.emit("scalar", lambda e, r=r, pscore=pscore: e.activation(
                        pt[:, r * 512:r * 512 + 512], ps[pscore][:], AF.Exp, scale=SCALE),
                        inc=("act", 1))
                    prev[f"exp{pscore}"] = ex_m
                    band = jglob >= nkb_g - 8
                    mk_m = None
                    if band:
                        bp = jglob - (nkb_g - 8)
                        em.wait("vector", "act", ex_m)
                        em.wait("vector", st_mask, mm_)
                        mk_m = em.emit("vector", lambda e, r=r, bp=bp: e.tensor_mul(
                            pt[:, r * 512:r * 512 + 512], pt[:, r * 512:r * 512 + 512],
                            m_t[:, bp * 512:bp * 512 + 512]), inc=("vec", 1))
                    out.append((t, ti, ex_m, mk_m))
                return out

            def av_batch(batch):
                for t, ti, ex_m, mk_m in batch:
                    seg, h, kb = t
                    if kb == 0:
                        em.wait("tensor", "vec", prev.get(f"accv{h % 2}", 0))
                    if mk_m is not None:
                        em.wait("tensor", "vec", mk_m)
                    else:
                        em.wait("tensor", "act", ex_m)
                for t, ti, ex_m, mk_m in batch:
                    seg, h, kb = t
                    r = ti % 8
                    pa = 0 if h % 2 == 0 else 4
                    sl = seg_slot[(g, seg)]
                    em.emit("tensor", lambda e, sl=sl, h=h, kb=kb: e.ldweights(
                        vseg[sl][:, kb * 2048 + h * 128:kb * 2048 + h * 128 + 128]))
                    av_m = em.emit("tensor", lambda e, sl=sl, h=h, kb=kb, r=r, pa=pa: e.matmul(
                        ps[pa][:], vseg[sl][:, kb * 2048 + h * 128:kb * 2048 + h * 128 + 128],
                        pt[:, r * 512:r * 512 + 512], start=(kb == 0), stop=(kb == 3)),
                        inc=("ten", 1))
                    pt_done[r] = av_m
                    if mk_m is None:
                        em.wait("vector", "act", ex_m)
                    if seg == 0 and kb == 0:
                        pm = em.emit("vector", lambda e, h=h, r=r: e.tensor_copy(
                            ptot[:, h * 512:h * 512 + 512], pt[:, r * 512:r * 512 + 512]), inc=("vec", 1))
                    else:
                        em.wait("vector", "vec", ptot_h.get(h, 0))
                        pm = em.emit("vector", lambda e, h=h, r=r: e.tensor_add(
                            ptot[:, h * 512:h * 512 + 512], ptot[:, h * 512:h * 512 + 512],
                            pt[:, r * 512:r * 512 + 512]), inc=("vec", 1))
                    ptot_done[r] = pm
                    ptot_h[h] = pm
                    if kb == 3:
                        em.wait("vector", "ten", av_m)
                        if seg == 0:
                            prev[f"accv{h % 2}"] = em.emit("vector", lambda e, h=h, pa=pa: e.tensor_copy(
                                acc[:, h * 512:h * 512 + 512], ps[pa][:]), inc=("vec", 1))
                        else:
                            prev[f"accv{h % 2}"] = em.emit("vector", lambda e, h=h, pa=pa: e.tensor_add(
                                acc[:, h * 512:h * 512 + 512], acc[:, h * 512:h * 512 + 512], ps[pa][:]), inc=("vec", 1))
                        if seg == nseg - 1:
                            emit_norm1(h)
                            if len(norm_pend) > 2:
                                emit_norm2()
                    if kb == 3 and h == H - 1:
                        seg_done[(g, seg)] = em.cnt["ten"]
                        slot_free[seg_slot[(g, seg)]] = em.cnt["ten"]
                        load_seg(seg + 2)

            norm_pend = []

            def emit_norm1(h):
                pu = 1 if h % 2 == 0 else 7
                em.wait("tensor", "vec", ptot_h[h])
                em.wait("tensor", "vec", prev.get(f"rcp{h % 2}", 0))
                sm_m = em.emit("tensor", lambda e, h=h, pu=pu: e.matmul(
                    ps[pu][0:1, :], ones_b[:], ptot[:, h * 512:h * 512 + 512],
                    start=True, stop=True), inc=("ten", 1))
                em.wait("vector", "ten", sm_m)
                em.wait("vector", "ten", bc_marks.get(h % 4, 0))
                def _recip(e, h=h, pu=pu):
                    with nc.allow_low_precision(reason="softmax denominator bcast in bf16"):
                        return e.reciprocal(rowb2(h % 4), ps[pu][0:1, :])
                rc_m = em.emit("vector", _recip, inc=("vec", 1))
                prev[f"rcp{h % 2}"] = rc_m
                norm_pend.append((h, rc_m))

            def emit_norm2(g=g, tg=tg):
                if not norm_pend:
                    return
                h, rc_m = norm_pend.pop(0)
                em.wait("tensor", "vec", rc_m)
                em.wait("tensor", "vec", prev.get("nrm", 0))
                bc_m = em.emit("tensor", lambda e, h=h: e.matmul(
                    ps[6][:], ones_rb[:], rowb2(h % 4), start=True, stop=True), inc=("ten", 1))
                bc_marks[h % 4] = bc_m
                em.wait("vector", "ten", bc_m)
                em.wait("vector", st_at[h % 2], prev.get(f"atsp{h % 2}", 0))
                nm = em.emit("vector", lambda e, h=h: e.tensor_mul(
                    atn[:, (h % 2) * 512:(h % 2) * 512 + 512], acc[:, h * 512:h * 512 + 512], ps[6][:]),
                    inc=("vec", 1))
                prev["nrm"] = nm
                em.wait("gpsimd", "vec", nm)
                prev[f"atsp{h % 2}"] = _dma(
                    em, st_at[h % 2], atcache[h * 128:h * 128 + 128, tg],
                    atn[:, (h % 2) * 512:(h % 2) * 512 + 512])

            batches = [[(t, ti) for t, ti in
                        zip(tiles[i:i + BSZ], range(i, i + BSZ))]
                       for i in range(0, len(tiles), BSZ)]
            pending = None
            for bt in batches:
                done = scores_batch(bt)
                if pending is not None:
                    av_batch(pending)
                pending = done
            av_batch(pending)
            while norm_pend:
                emit_norm2()
            seg_done[(g, "last")] = em.cnt["ten"]
            prev["glast"] = em.cnt["ten"]

        em.barrier()

    with ExitStack() as a5:
        sba = lambda name, shape, dt: a5.enter_context(nc.sbuf_tensor(name, shape, dt))
        wst2b = sba("wst5b", [128, 2 * 1024], F32)
        wo_t = sba("wo_t", [128, 16 * 2048], BF16)
        w2q = sba("w2q", [128, 16 * 2048], FP8)
        at_g = sba("at_g", [128, 16 * 512], BF16)
        xrot = sba("xrot", [128, 2 * 512], F32)
        frot = sba("frot", [128, 2 * 512], F32)

        for qc in range(4):
            em.emit("vector", lambda e, qc=qc: e.tensor_copy(
                q1_t[:, qc * 8192:(qc + 1) * 8192].bitcast(U32),
                q1_t[:, qc * 8192:(qc + 1) * 8192].bitcast(U32)), inc=("vec", 1))
        st_wo = em.stream("stwo", "sync")
        wom = 0
        for c in range(16):
            wom = _dma(em, st_wo, wo_t[:, c * 2048:(c + 1) * 2048],
                       wo_bf[c * 128:(c + 1) * 128, :])
        em.wait("tensor", st_wo, wom)
        wcast5 = load_weight(em, wst2b, w2q, w2, 16, 2048, fp8_scale=sc_t[:, 1:2],
                             queue="gpsimd", bitcast_each=True)
        em.wait("tensor", "vec", wcast5)
        prev = {}
        for tgi in range(4):
            tg = slice(TG * tgi, TG * (tgi + 1))
            if tgi > 0:
                em.wait("sync", "ten", prev["olast"])
            am = None
            for pc in range(4):
                am = _dma(em, st_x, _s3(at_g, 512)[:, 4 * pc:4 * pc + 4, :],
                          _r3(atcache[:, tg])[:, 4 * pc:4 * pc + 4, :])
            for m in range(16):
                pb = m % 2
                em.wait("sync", "vec", prev.get(f"res{pb}", 0))
                xm = _dma(em, st_xr[pb], xrot[:, pb * 512:pb * 512 + 512],
                          xT_own[m * 128:m * 128 + 128, tg])
                em.wait("tensor", st_x, am)
                em.wait("tensor", "vec", prev.get(f"res{pb}", 0))
                em.wait("tensor", "act", prev.get(f"fsc{pb}", 0))
                for h in range(16):
                    em.emit("tensor", lambda e, h=h, m=m: e.ldweights(
                        wo_t[:, h * 2048 + m * 128:h * 2048 + m * 128 + 128]))
                    em.emit("tensor", lambda e, h=h, m=m, pb=pb: e.matmul(
                        ps[pb][:], wo_t[:, h * 2048 + m * 128:h * 2048 + m * 128 + 128],
                        at_g[:, h * 512:h * 512 + 512], start=(h == 0), stop=(h == 15)))
                for c2 in range(8):
                    wap = w2q[:, 2 * c2 * 2048:(2 * c2 + 2) * 2048].rearrange(
                        "p (a q) -> p a q", q=2048)[:, :, m * 128:m * 128 + 128]
                    xap = q1_t[:, 2 * c2 * 2048:(2 * c2 + 2) * 2048].rearrange(
                        "p (a q) -> p a q", q=2048)[:, :, tgi * 512:tgi * 512 + 512]
                    em.emit("tensor", lambda e, wap=wap: e.ldweights(wap, perf_mode=DR))
                    mm2 = em.emit("tensor", lambda e, wap=wap, xap=xap, pb=pb, c2=c2: e.matmul(
                        ps[2 + pb][:], wap, xap, start=(c2 == 0), stop=(c2 == 7), perf_mode=DR),
                        inc=(("ten", 1) if c2 == 7 else None))
                prev["olast"] = mm2
                em.wait("scalar", "ten", mm2)
                em.wait("scalar", "vec", prev.get(f"res{pb}", 0))
                em.wait("scalar", st_out[pb], prev.get(f"ost{pb}", 0))
                fm = em.emit("scalar", lambda e, pb=pb: e.mul(
                    frot[:, pb * 512:pb * 512 + 512], ps[2 + pb][:], rcol[:]), inc=("act", 1))
                prev[f"fsc{pb}"] = fm
                em.wait("vector", "act", fm)
                em.wait("vector", "ten", mm2)
                em.wait("vector", st_xr[pb], xm)
                em.wait("vector", st_out[pb], prev.get(f"ost{pb}", 0))
                em.emit("vector", lambda e, pb=pb: e.tensor_add(
                    frot[:, pb * 512:pb * 512 + 512], frot[:, pb * 512:pb * 512 + 512], ps[pb][:]),
                    inc=("vec", 1))
                em.wait("vector", "vec", em.cnt["vec"])
                rm = em.emit("vector", lambda e, pb=pb: e.tensor_add(
                    frot[:, pb * 512:pb * 512 + 512], frot[:, pb * 512:pb * 512 + 512],
                    xrot[:, pb * 512:pb * 512 + 512]), inc=("vec", 1))
                prev[f"res{pb}"] = rm
                em.wait("gpsimd", "vec", rm)
                prev[f"ost{pb}"] = _dma(em, st_out[pb], outT[m * 128:m * 128 + 128, tg],
                                        frot[:, pb * 512:pb * 512 + 512])
        for name, queue in em.dstreams.items():
            eng = "sync" if queue == "sync" else "gpsimd"
            em.wait(eng, name, em.cnt[name])
        em.flush()

    q1_stack.close()
    top.close()
    return nc



def _prep_core_inputs(c, x, shared):
    b, h = c // 2, c % 2
    xT = np.ascontiguousarray(x[b].T)
    own = np.concatenate([np.arange(1024 * g + 512 * h, 1024 * g + 512 * h + 512)
                          for g in range(4)])
    xT_own = np.ascontiguousarray(xT[:, own])
    coso_h = np.ascontiguousarray(shared["cosg"][:, own])
    sino_h = np.ascontiguousarray(shared["sing"][:, own])
    kj = np.arange(128)[:, None]
    qi = np.arange(512)[None, :]
    masks = np.concatenate(
        [(128 * bp + kj <= 512 * h + qi).astype(BF) for bp in range(8)], axis=1)
    m = dict(shared)
    m.update({"xT_all": xT, "xT_own": xT_own, "coso": coso_h, "sino": sino_h,
              "masks": masks})
    return m


def prep_all_inputs(L0_attn_hidden_states, L0_attn_q_a_w, L0_attn_q_a_ln_w,
                    L0_attn_q_b_w, L0_attn_kv_a_w, L0_attn_kv_a_ln_w,
                    L0_attn_kv_b_w, L0_attn_o_w, L0_ffn_input2, L0_ffn_scale,
                    L0_ffn_scale1):
    x = np.asarray(L0_attn_hidden_states, np.float32)
    qaw = np.asarray(L0_attn_q_a_w, np.float32)
    qbw = np.asarray(L0_attn_q_b_w, np.float32)
    kvaw = np.asarray(L0_attn_kv_a_w, np.float32)
    kvbw = np.asarray(L0_attn_kv_b_w, np.float32)
    ow = np.asarray(L0_attn_o_w, np.float32)

    qb3 = qbw.reshape(H, QHD, QA)
    kb3 = kvbw.reshape(H, NOPE + VD, CKV)

    inv = THETA ** (-np.arange(0, ROPE, 2, dtype=np.float64) / ROPE)
    t = np.arange(S, dtype=np.float64)
    ang = np.concatenate([np.outer(inv, t), np.outer(inv, t)], 0)

    p64 = np.zeros((ROPE, ROPE), np.float32)
    for i in range(32):
        p64[i, 32 + i] = -1.0
        p64[32 + i, i] = 1.0
    rotT_h = np.zeros((128, 128), np.float32)
    rotT_h[:ROPE, :ROPE] = p64.T
    rotT_h[ROPE:, ROPE:] = p64.T

    s0 = np.float32(np.asarray(L0_ffn_scale))
    s1 = np.float32(np.asarray(L0_ffn_scale1))

    shared = {
        "wqa": np.ascontiguousarray(qaw.T),
        "wqbn": np.ascontiguousarray(qb3[:, :NOPE, :].reshape(H * NOPE, QA).T),
        "wqbr": np.ascontiguousarray(qb3[:, NOPE:, :].reshape(H * ROPE, QA).T),
        "wkvc": np.ascontiguousarray(kvaw[:CKV, :].T),
        "wkvr": np.ascontiguousarray(kvaw[CKV:, :].T),
        "wkbk": np.ascontiguousarray(kb3[:, :NOPE, :].reshape(H * NOPE, CKV).T),
        "wkbv": np.ascontiguousarray(kb3[:, NOPE:, :].reshape(H * VD, CKV).T),
        "wo": np.ascontiguousarray(ow.T),
        "w2": np.ascontiguousarray(np.asarray(L0_ffn_input2, np.float32)),
        "gqa": np.ascontiguousarray(
            np.asarray(L0_attn_q_a_ln_w, np.float32).reshape(QA // 128, 128).T),
        "gkv": np.ascontiguousarray(
            np.asarray(L0_attn_kv_a_ln_w, np.float32).reshape(CKV // 128, 128).T),
        "cosg": np.cos(ang).astype(np.float32),
        "sing": np.sin(ang).astype(np.float32),
        "ones_c": np.ones((128, 1), np.float32),
        "ones_b": np.ones((128, 1), BF),
        "ones_r": np.ones((1, 128), np.float32),
        "ones_rb": np.ones((1, 128), BF),
        "rotT": rotT_h,
        "scales": np.concatenate([np.full((128, 1), s0, np.float32),
                                  np.full((128, 1), s1, np.float32)], 1),
    }
    return [_prep_core_inputs(c, x, shared) for c in range(8)]


def assemble_output(results):
    out = np.empty((B, S, D), np.float32)
    for c in range(8):
        b, h = c // 2, c % 2
        own = np.concatenate([np.arange(1024 * g + 512 * h, 1024 * g + 512 * h + 512)
                              for g in range(4)])
        out[b, own, :] = results[c]["outT"].T
    return out


def kernel(**inputs):
    in_maps = prep_all_inputs(**inputs)
    nc = build_program()
    res = run_bass_kernel_spmd(nc, in_maps, core_ids=list(range(8)))
    return assemble_output(res.results)


# version 40
# speedup vs baseline: 1.0492x; 1.0209x over previous
import numpy as np
import ml_dtypes
from contextlib import ExitStack

import concourse.bass as bass
import concourse.mybir as mybir
from concourse.bass_utils import run_bass_kernel_spmd

F32 = mybir.dt.float32
BF16 = mybir.dt.bfloat16
FP8 = mybir.dt.float8e4
AF = mybir.ActivationFunctionType
ALU = mybir.AluOpType
DR = mybir.MatmulPerfMode.DoubleRow
U32 = mybir.dt.uint32

BF = ml_dtypes.bfloat16

B, S, D = 4, 4096, 2048
H, NOPE, ROPE, VD = 16, 128, 64, 128
QHD = NOPE + ROPE
SCALE = QHD ** -0.5
THETA = 10000.0
CKV = 512
QA = 1536
EPS_LN = 1e-5
EPS_RMS = 1e-6

TG = 512
NG = S // TG
OG = 4
NKB_SEG = 4

ENGINES = ("sync", "tensor", "vector", "scalar", "gpsimd")


class Em:
    def __init__(self, nc, block):
        self.nc = nc
        self.block = block
        self._stack = ExitStack()
        self.sem = {}
        for name in ("ten", "act", "vec", "gsm", "ph"):
            self.sem[name] = self._stack.enter_context(nc.semaphore(name))
        self.cnt = {k: 0 for k in self.sem}
        self.q = {e: [] for e in ENGINES}
        self.lastwait = {}
        self.wst_marks = {}
        self.dstreams = {}

    def stream(self, name, queue):
        if name not in self.sem:
            self.sem[name] = self._stack.enter_context(self.nc.semaphore(name))
            self.cnt[name] = 0
        self.dstreams[name] = queue
        return name

    def emit(self, eng, build, inc=None):
        mark = None
        if inc is not None:
            s, a = inc
            self.cnt[s] += a
            mark = self.cnt[s]
            h = self.sem[s]

            def f(e, build=build, h=h, a=a):
                build(e).then_inc(h, a)
        else:
            def f(e, build=build):
                build(e)
        self.q[eng].append(f)
        return mark

    def wait(self, eng, s, val):
        if not val or val <= 0:
            return
        key = (eng, s)
        if self.lastwait.get(key, 0) >= val:
            return
        self.lastwait[key] = val
        h = self.sem[s]
        self.q[eng].append(lambda e, h=h, v=val: e.wait_ge(h, v))

    def flush(self):
        for eng in ENGINES:
            fns = self.q[eng]
            if not fns:
                continue
            self.q[eng] = []

            def body(e, fns=fns):
                for f in fns:
                    f(e)
            getattr(self.block, eng)(body)

    def barrier(self):
        for name, queue in self.dstreams.items():
            eng = "sync" if queue == "sync" else "gpsimd"
            self.wait(eng, name, self.cnt[name])
        for eng in ENGINES:
            self.cnt["ph"] += 1
            h = self.sem["ph"]
            self.q[eng].append(lambda e, h=h: e.drain().then_inc(h, 1))
        tgt = self.cnt["ph"]
        self.flush()
        for eng in ENGINES:
            self.wait(eng, "ph", tgt)


def _dma(em, stream, out_ap, in_ap):
    queue = em.dstreams[stream]
    return em.emit(queue, lambda e, o=out_ap, i=in_ap: e.dma_start(out=o, in_=i),
                   inc=(stream, 16))


def _r3(ap, p=128):
    return ap.rearrange("(c p) q -> p c q", p=p)


def _s3(ap, q):
    return ap.rearrange("p (c q) -> p c q", q=q)


def load_weight(em, wst, dst, src2d, C, K, fp8_scale=None, queue="sync",
                bitcast_each=False):
    key = id(wst)
    HS = wst.shape[1] // 2
    marks = em.wst_marks.setdefault(key, [])
    for c in range(C):
        for k0 in range(0, K, HS):
            kw = min(HS, K - k0)
            half = len(marks) % 2
            st = em.stream(f"w{half}" if queue == "sync" else f"wg{half}", queue)
            if len(marks) >= 2:
                em.wait("sync" if queue == "sync" else "gpsimd", "vec", marks[-2])
            m = _dma(em, st, wst[:, half * HS:half * HS + kw],
                     src2d[c * 128:(c + 1) * 128, k0:k0 + kw])
            em.wait("vector", st, m)
            d = dst[:, c * K + k0:c * K + k0 + kw]
            s = wst[:, half * HS:half * HS + kw]
            if fp8_scale is None:
                cm = em.emit("vector", lambda e, d=d, s=s: e.tensor_copy(d, s),
                             inc=("vec", 1))
            else:
                cm = em.emit("vector", lambda e, d=d, s=s, sc=fp8_scale:
                             e.tensor_scalar(d, s, sc, None, ALU.mult), inc=("vec", 1))
                if bitcast_each:
                    cm = em.emit("vector", lambda e, d=d: e.tensor_copy(
                        d.bitcast(U32), d.bitcast(U32)), inc=("vec", 1))
            marks.append(cm)
    return marks[-1]


def build_program(debug_outputs=()):
    nc = bass.Bass("TRN2", target_bir_lowering=False, debug=False)

    dk = "ExternalOutput" if "caches" in debug_outputs else "Internal"
    dq = "ExternalOutput" if "q" in debug_outputs else "Internal"
    da = "ExternalOutput" if "at" in debug_outputs else "Internal"

    xT_all = nc.dram_tensor("xT_all", [D, S], F32, kind="ExternalInput").ap()
    xT_own = nc.dram_tensor("xT_own", [D, 2048], F32, kind="ExternalInput").ap()
    wqa = nc.dram_tensor("wqa", [D, QA], F32, kind="ExternalInput").ap()
    wqbn = nc.dram_tensor("wqbn", [QA, H * NOPE], F32, kind="ExternalInput").ap()
    wqbr = nc.dram_tensor("wqbr", [QA, H * ROPE], F32, kind="ExternalInput").ap()
    wkvc = nc.dram_tensor("wkvc", [D, CKV], F32, kind="ExternalInput").ap()
    wkvr = nc.dram_tensor("wkvr", [D, ROPE], F32, kind="ExternalInput").ap()
    wkbk = nc.dram_tensor("wkbk", [CKV, H * NOPE], F32, kind="ExternalInput").ap()
    wkbv = nc.dram_tensor("wkbv", [CKV, H * VD], F32, kind="ExternalInput").ap()
    wo = nc.dram_tensor("wo", [H * VD, D], F32, kind="ExternalInput").ap()
    w2 = nc.dram_tensor("w2", [D, D], F32, kind="ExternalInput").ap()
    gqa = nc.dram_tensor("gqa", [128, QA // 128], F32, kind="ExternalInput").ap()
    gkv = nc.dram_tensor("gkv", [128, CKV // 128], F32, kind="ExternalInput").ap()
    cosg = nc.dram_tensor("cosg", [ROPE, S], F32, kind="ExternalInput").ap()
    sing = nc.dram_tensor("sing", [ROPE, S], F32, kind="ExternalInput").ap()
    coso = nc.dram_tensor("coso", [ROPE, 2048], F32, kind="ExternalInput").ap()
    sino = nc.dram_tensor("sino", [ROPE, 2048], F32, kind="ExternalInput").ap()
    masks_d = nc.dram_tensor("masks", [128, 8 * 512], BF16, kind="ExternalInput").ap()
    ones_c_d = nc.dram_tensor("ones_c", [128, 1], F32, kind="ExternalInput").ap()
    ones_b_d = nc.dram_tensor("ones_b", [128, 1], BF16, kind="ExternalInput").ap()
    ones_r_d = nc.dram_tensor("ones_r", [1, 128], F32, kind="ExternalInput").ap()
    ones_rb_d = nc.dram_tensor("ones_rb", [1, 128], BF16, kind="ExternalInput").ap()
    rotT_d = nc.dram_tensor("rotT", [128, 128], F32, kind="ExternalInput").ap()
    sc_d = nc.dram_tensor("scales", [128, 2], F32, kind="ExternalInput").ap()
    outT = nc.dram_tensor("outT", [D, 2048], F32, kind="ExternalOutput").ap()

    kcache = nc.dram_tensor("kcache", [H * NOPE, S], BF16, kind=dk).ap()
    vcache = nc.dram_tensor("vcache", [S, H * VD], BF16, kind=dk).ap()
    qacache = nc.dram_tensor("qacache", [QA, 2048], BF16, kind="Internal").ap()
    qncache = nc.dram_tensor("qncache", [H * NOPE, 2048], BF16, kind=dq).ap()
    qrcache = nc.dram_tensor("qrcache", [ROPE, H * 2048], BF16, kind=dq).ap()
    atcache = nc.dram_tensor("atcache", [H * VD, 2048], BF16, kind=da).ap()
    wo_bf = nc.dram_tensor("wo_bf", [H * VD, D], BF16, kind="Internal").ap()
    kpe_dbg = (nc.dram_tensor("kpe_dbg", [ROPE, S], BF16, kind="ExternalOutput").ap()
               if "caches" in debug_outputs else None)

    top = ExitStack()
    sb = lambda name, shape, dt: top.enter_context(nc.sbuf_tensor(name, shape, dt))
    ones_c = sb("ones_c_t", [128, 1], F32)
    ones_b = sb("ones_b_t", [128, 1], BF16)
    ones_r = sb("ones_r_t", [1, 128], F32)
    ones_rb = sb("ones_rb_t", [1, 128], BF16)
    rowsb = sb("rowsb", [1, 4 * 512], BF16)
    rotT = sb("rotT_t", [128, 128], F32)
    gqa_t = sb("gqa_t", [128, QA // 128], F32)
    gkv_t = sb("gkv_t", [128, CKV // 128], F32)
    sc_t = sb("sc_t", [128, 2], F32)
    rr_t = sb("rr_t", [128, 2], F32)
    rcol = sb("rcol", [128, 1], F32)
    kpe_t = sb("kpe_t", [128, S], BF16)
    rows = sb("rows", [1, 512 * 8], F32)

    def row(i):
        return rows[0:1, i * 512:(i + 1) * 512]

    def rowb(i):
        return rowsb[0:1, i * 512:(i + 1) * 512]

    rowb2 = rowb

    ps = [top.enter_context(nc.psum_tensor(f"ps{i}", [128, 512], F32)) for i in range(8)]

    block = top.enter_context(nc.Block())
    em = Em(nc, block)
    st_x = em.stream("stx", "sync")
    st_cs = em.stream("stcs", "sync")
    st_mask = em.stream("stmask", "sync")
    st_ksp = em.stream("stksp", "gpsimd")
    st_xbf = em.stream("stxbf", "gpsimd")
    st_vsp = em.stream("stvsp", "gpsimd")
    st_kpd = em.stream("stkpd", "gpsimd")
    st_qasp = em.stream("stqasp", "gpsimd")
    st_qnsp = em.stream("stqnsp", "gpsimd")
    st_qrsp = em.stream("stqrsp", "gpsimd")
    st_seg = [em.stream("stseg0", "sync"), em.stream("stseg1", "sync")]
    st_at = [em.stream("stat0", "gpsimd"), em.stream("stat1", "gpsimd")]
    st_xr = [em.stream("stxr0", "sync"), em.stream("stxr1", "sync")]
    st_out = [em.stream("stout0", "gpsimd"), em.stream("stout1", "gpsimd")]

    st_k = em.stream("konst", "sync")
    for dst, src in ((ones_c, ones_c_d), (ones_b, ones_b_d), (ones_r, ones_r_d),
                     (ones_rb, ones_rb_d),
                     (rotT, rotT_d), (gqa_t, gqa), (gkv_t, gkv), (sc_t, sc_d)):
        _dma(em, st_k, dst[:], src[:])
    em.wait("vector", st_k, em.cnt[st_k])
    rm0 = em.emit("vector", lambda e: e.reciprocal(rr_t[:], sc_t[:]), inc=("vec", 1))
    em.wait("vector", "vec", rm0)
    em.emit("vector", lambda e: e.tensor_mul(rcol[:], rr_t[:, 0:1], rr_t[:, 1:2]),
            inc=("vec", 1))

    def ln_rows(ndim, eps):
        inv = 1.0 / ndim
        em.emit("vector", lambda e: e.tensor_scalar(row(0), ps[6][0:1, :], inv, None, ALU.mult), inc=("vec", 1))
        em.emit("vector", lambda e: e.tensor_scalar(row(1), ps[7][0:1, :], inv, None, ALU.mult), inc=("vec", 1))
        em.wait("vector", "vec", em.cnt["vec"])
        em.emit("vector", lambda e: e.tensor_mul(row(2), row(0), row(0)), inc=("vec", 1))
        em.wait("vector", "vec", em.cnt["vec"])
        em.emit("vector", lambda e: e.tensor_sub(row(2), row(1), row(2)), inc=("vec", 1))
        em.wait("vector", "vec", em.cnt["vec"])
        vm = em.emit("vector", lambda e: e.tensor_scalar(row(2), row(2), eps, None, ALU.add), inc=("vec", 1))
        em.wait("scalar", "vec", vm)
        sm = em.emit("scalar", lambda e: e.sqrt(row(3), row(2)), inc=("act", 1))
        em.wait("vector", "act", sm)
        em.emit("vector", lambda e: e.reciprocal(row(4), row(3)), inc=("vec", 1))
        em.wait("vector", "vec", em.cnt["vec"])
        em.emit("vector", lambda e: e.tensor_mul(row(5), row(0), row(4)), inc=("vec", 1))
        em.wait("vector", "vec", em.cnt["vec"])
        return em.emit("vector", lambda e: e.tensor_scalar(row(5), row(5), -1.0, None, ALU.mult), inc=("vec", 1))

    def rms_row(ndim, eps):
        em.emit("vector", lambda e: e.tensor_scalar(row(2), ps[6][0:1, :], 1.0 / ndim, eps, ALU.mult, ALU.add), inc=("vec", 1))
        em.wait("scalar", "vec", em.cnt["vec"])
        sm = em.emit("scalar", lambda e: e.sqrt(row(3), row(2)), inc=("act", 1))
        em.wait("vector", "act", sm)
        return em.emit("vector", lambda e: e.reciprocal(row(4), row(3)), inc=("vec", 1))

    with ExitStack() as a1:
        sba = lambda name, shape, dt: a1.enter_context(nc.sbuf_tensor(name, shape, dt))
        wst = sba("wst", [128, 2 * 1024], F32)
        wkvc_t = sba("wkvc_t", [128, 16 * CKV], BF16)
        wkvr_t = sba("wkvr_t", [128, 16 * ROPE], BF16)
        wkbk_t = sba("wkbk_t", [128, 4 * 2048], BF16)
        wkbv_t = sba("wkbv_t", [128, 4 * 2048], BF16)
        xbf = sba("xbf", [128, 16 * 512], BF16)
        xsq = sba("xsq", [128, 16 * 512], BF16)
        ckv_f = sba("ckv_f", [128, 4 * 512], F32)
        cksq = sba("cksq", [128, 2 * 512], BF16)
        ckv_n = sba("ckv_n", [128, 2 * 4 * 512], BF16)
        tmpv = sba("tmpv", [128, 2 * 512], F32)
        w1kv = sba("w1kv", [128, 8], F32)
        kpa = sba("kpa", [ROPE, 512], F32)
        kpr = sba("kpr", [ROPE, 512], F32)
        kpc = sba("kpc", [ROPE, 512], F32)
        kps = sba("kps", [ROPE, 512], F32)
        cosr = sba("cosr", [ROPE, 2 * 512], F32)
        sinr = sba("sinr", [ROPE, 2 * 512], F32)
        kstage = sba("kstage", [128, 16 * 512], BF16)
        vstage = sba("vstage", [128, 4 * 2048], BF16)

        zkp = em.emit("vector", lambda e: e.memset(kpe_t[ROPE:128, :], 0.0), inc=("vec", 1))
        load_weight(em, wst, wkvc_t, wkvc, 16, CKV)
        load_weight(em, wst, wkvr_t, wkvr, 16, ROPE)
        load_weight(em, wst, wkbk_t, wkbk, 4, 2048)
        wcast = load_weight(em, wst, wkbv_t, wkbv, 4, 2048)
        em.wait("tensor", "vec", wcast)
        for oc in range(4):
            for ic in range(16):
                wm = em.emit("tensor", lambda e, oc=oc, ic=ic: e.matmul(
                    ps[oc][:, 0:1], wkvc_t[:, ic * CKV + oc * 128:ic * CKV + oc * 128 + 128],
                    ones_b[:], start=(ic == 0), stop=(ic == 15)),
                    inc=(("ten", 1) if ic == 15 else None))
            em.wait("scalar", "ten", wm)
            em.emit("scalar", lambda e, oc=oc: e.copy(w1kv[:, oc:oc + 1], ps[oc][:, 0:1]), inc=("act", 1))
        for ic in range(16):
            wm = em.emit("tensor", lambda e, ic=ic: e.matmul(
                ps[4][0:ROPE, 0:1], wkvr_t[:, ic * ROPE:(ic + 1) * ROPE],
                ones_b[:], start=(ic == 0), stop=(ic == 15)),
                inc=(("ten", 1) if ic == 15 else None))
        em.wait("scalar", "ten", wm)
        w1done = em.emit("scalar", lambda e: e.copy(w1kv[0:ROPE, 4:5], ps[4][0:ROPE, 0:1]), inc=("act", 1))
        em.wait("tensor", "act", w1done)

        prev = {}

        def emit_xload(j):
            tg = slice(TG * j, TG * (j + 1))
            em.wait("gpsimd", "ten", prev.get("kva", 0))
            em.wait("gpsimd", "act", prev.get("sq_last", 0))
            xbm = None
            for pc in range(4):
                xbm = _dma(em, st_xbf,
                           _s3(xbf, 512)[:, 4 * pc:4 * pc + 4, :],
                           _r3(xT_all[:, tg])[:, 4 * pc:4 * pc + 4, :])
            prev["xbm"] = xbm
            if j > 0:
                em.wait("sync", "vec", prev.get("kpe", 0))
            cm_ = _dma(em, st_cs, cosr[:, (j % 2) * 512:(j % 2) * 512 + 512], cosg[:, tg])
            prev["cs"] = _dma(em, st_cs, sinr[:, (j % 2) * 512:(j % 2) * 512 + 512], sing[:, tg])

        def emit_A(j):
            xbm = prev["xbm"]
            em.wait("tensor", "vec", prev.get("evac_v", 0))
            em.wait("tensor", "act", prev.get("evac_a", 0))
            sq_last = None
            for c in range(16):
                em.wait("scalar", st_xbf, xbm)
                em.wait("scalar", "ten", prev.get("xsq_mm", 0) if c < 2 else 0)
                sq_last = em.emit("scalar", lambda e, c=c: e.square(
                    xsq[:, c * 512:c * 512 + 512], xbf[:, c * 512:c * 512 + 512]), inc=("act", 1))
                prev["sq_last"] = sq_last
            for c in range(16):
                em.wait("tensor", st_xbf, xbm)
                if c == 0:
                    em.wait("tensor", "act", prev.get("evac_a", 0))
                    em.wait("tensor", "act", prev.get("vev", 0))
                    em.wait("tensor", "vec", prev.get("kpe", 0))
                em.emit("tensor", lambda e, c=c: e.matmul(
                    ps[6][0:1, :], ones_b[:], xbf[:, c * 512:c * 512 + 512],
                    start=(c == 0), stop=(c == 15)))
                for o in range(4):
                    em.emit("tensor", lambda e, c=c, o=o: e.ldweights(
                        wkvc_t[:, c * CKV + o * 128:c * CKV + o * 128 + 128]))
                    em.emit("tensor", lambda e, c=c, o=o: e.matmul(
                        ps[o][:], wkvc_t[:, c * CKV + o * 128:c * CKV + o * 128 + 128],
                        xbf[:, c * 512:c * 512 + 512], start=(c == 0), stop=(c == 15)))
                em.emit("tensor", lambda e, c=c: e.ldweights(
                    wkvr_t[:, c * ROPE:c * ROPE + ROPE]))
                prev["kva"] = em.emit("tensor", lambda e, c=c: e.matmul(
                    ps[4][0:ROPE, :], wkvr_t[:, c * ROPE:c * ROPE + ROPE],
                    xbf[:, c * 512:c * 512 + 512], start=(c == 0), stop=(c == 15)), inc=("ten", 1))
            em.wait("tensor", "act", sq_last)
            for c in range(16):
                prev["xsq_mm"] = em.emit("tensor", lambda e, c=c: e.matmul(
                    ps[7][0:1, :], ones_b[:], xsq[:, c * 512:c * 512 + 512],
                    start=(c == 0), stop=(c == 15)), inc=(("ten", 1) if c == 15 else None))
            prev["statmm"] = prev["xsq_mm"]
            em.wait("vector", "ten", prev["statmm"])
            ln_rows(D, EPS_LN)
            em.wait("tensor", "vec", em.cnt["vec"])
            em.emit("tensor", lambda e: e.matmul(ps[6][:], ones_r[:], row(4), start=True, stop=True))
            prev["bm"] = em.emit("tensor", lambda e: e.matmul(ps[7][:], ones_r[:], row(5), start=True, stop=True), inc=("ten", 1))

        def emit_E(j):
            bm = prev["bm"]
            for o in range(4):
                em.wait("scalar", "ten", bm)
                em.wait("scalar", "vec", prev.get(f"ev{o % 2}", 0))
                sc = em.emit("scalar", lambda e, o=o: e.copy(
                    tmpv[:, (o % 2) * 512:(o % 2) * 512 + 512], ps[o][:]), inc=("act", 1))
                em.wait("vector", "act", sc)
                em.wait("vector", "ten", bm)
                em.wait("vector", "vec", em.cnt["vec"])
                em.emit("vector", lambda e, o=o: e.tensor_mul(
                    tmpv[:, (o % 2) * 512:(o % 2) * 512 + 512],
                    tmpv[:, (o % 2) * 512:(o % 2) * 512 + 512], ps[6][:]), inc=("vec", 1))
                em.wait("vector", "vec", em.cnt["vec"])
                prev[f"ev{o % 2}"] = em.emit("vector", lambda e, o=o: e.scalar_tensor_tensor(
                    ckv_f[:, o * 512:o * 512 + 512], ps[7][:], w1kv[:, o:o + 1],
                    tmpv[:, (o % 2) * 512:(o % 2) * 512 + 512], ALU.mult, ALU.add), inc=("vec", 1))
            em.wait("scalar", "ten", bm)
            sc = em.emit("scalar", lambda e: e.copy(kpr[:], ps[4][0:ROPE, :]), inc=("act", 1))
            prev["kpm"] = sc
            em.wait("vector", "act", sc)
            em.emit("vector", lambda e: e.tensor_mul(kpr[:], kpr[:], ps[6][0:ROPE, :]), inc=("vec", 1))
            em.wait("vector", "vec", em.cnt["vec"])
            kpaff = em.emit("vector", lambda e: e.scalar_tensor_tensor(
                kpa[:], ps[7][0:ROPE, :], w1kv[0:ROPE, 4:5], kpr[:], ALU.mult, ALU.add), inc=("vec", 1))
            prev["kpaff"] = kpaff
            prev["evac_a"] = em.cnt["act"]
            prev["evac_v"] = em.cnt["vec"]

        def emit_R(j):
            for o in range(4):
                em.wait("scalar", "vec", prev[f"ev{o % 2}"])
                em.wait("scalar", "ten", prev.get("cksq_mm", 0))
                sqm = em.emit("scalar", lambda e, o=o: e.square(
                    cksq[:, (o % 2) * 512:(o % 2) * 512 + 512], ckv_f[:, o * 512:o * 512 + 512]), inc=("act", 1))
                em.wait("tensor", "act", sqm)
                if o == 0:
                    em.wait("tensor", "vec", prev.get("ckvn", 0))
                    em.wait("tensor", "vec", prev.get("kpe", 0))
                prev["cksq_mm"] = em.emit("tensor", lambda e, o=o: e.matmul(
                    ps[5][0:1, :], ones_b[:], cksq[:, (o % 2) * 512:(o % 2) * 512 + 512],
                    start=(o == 0), stop=(o == 3)), inc=("ten", 1))
            em.wait("vector", "ten", prev["cksq_mm"])
            em.emit("vector", lambda e: e.tensor_scalar(row(6), ps[5][0:1, :], 1.0 / CKV, EPS_RMS, ALU.mult, ALU.add), inc=("vec", 1))
            em.wait("scalar", "vec", em.cnt["vec"])
            smx = em.emit("scalar", lambda e: e.sqrt(row(7), row(6)), inc=("act", 1))
            em.wait("vector", "act", smx)
            rsm = em.emit("vector", lambda e: e.reciprocal(row(6), row(7)), inc=("vec", 1))
            em.wait("tensor", "vec", rsm)
            bmk = em.emit("tensor", lambda e: e.matmul(ps[5][:], ones_r[:], row(6), start=True, stop=True), inc=("ten", 1))
            ckvn_ready = None
            for o in range(4):
                em.wait("vector", "ten", bmk)
                em.wait("vector", "vec", em.cnt["vec"])
                em.emit("vector", lambda e, o=o: e.tensor_mul(
                    tmpv[:, 0:512], ckv_f[:, o * 512:o * 512 + 512], ps[5][:]), inc=("vec", 1))
                em.wait("vector", "vec", em.cnt["vec"])
                ckvn_ready = em.emit("vector", lambda e, o=o, j=j: e.tensor_scalar(
                    ckv_n[:, (j % 2) * 2048 + o * 512:(j % 2) * 2048 + o * 512 + 512],
                    tmpv[:, 0:512], gkv_t[:, o:o + 1], None, ALU.mult), inc=("vec", 1))
            prev["ckvn"] = ckvn_ready
            em.wait("tensor", "vec", prev["kpaff"])
            rotm = em.emit("tensor", lambda e: e.matmul(
                ps[4][0:ROPE, :], rotT[0:ROPE, 0:ROPE], kpa[:], start=True, stop=True), inc=("ten", 1))
            em.wait("vector", st_cs, prev["cs"])
            em.emit("vector", lambda e, j=j: e.tensor_mul(
                kpc[:], kpa[:], cosr[:, (j % 2) * 512:(j % 2) * 512 + 512]), inc=("vec", 1))
            em.wait("vector", "ten", rotm)
            em.emit("vector", lambda e, j=j: e.tensor_mul(
                kps[:], ps[4][0:ROPE, :], sinr[:, (j % 2) * 512:(j % 2) * 512 + 512]), inc=("vec", 1))
            em.wait("vector", "vec", em.cnt["vec"])
            kpe_done = em.emit("vector", lambda e, j=j: e.tensor_add(
                kpe_t[0:ROPE, TG * j:TG * j + TG], kpc[:], kps[:]), inc=("vec", 1))
            prev["kpe"] = kpe_done

        def emit_KB(j):
            tg = slice(TG * j, TG * (j + 1))
            ckvn_ready = prev["ckvn"]
            cb = (j % 2) * 2048
            em.wait("tensor", "act", prev.get("evac_a", 0))
            for oh in range(16):
                pb = oh % 2
                em.wait("tensor", "vec", ckvn_ready)
                em.wait("tensor", "act", prev.get(f"kev{pb}", 0))
                for ic in range(4):
                    em.emit("tensor", lambda e, oh=oh, ic=ic: e.ldweights(
                        wkbk_t[:, ic * 2048 + oh * 128:ic * 2048 + oh * 128 + 128]))
                    lm = em.emit("tensor", lambda e, oh=oh, ic=ic, pb=pb, cb=cb: e.matmul(
                        ps[pb][:], wkbk_t[:, ic * 2048 + oh * 128:ic * 2048 + oh * 128 + 128],
                        ckv_n[:, cb + ic * 512:cb + ic * 512 + 512], start=(ic == 0), stop=(ic == 3)),
                        inc=(("ten", 1) if ic == 3 else None))
                em.wait("scalar", "ten", lm)
                em.wait("scalar", st_ksp, prev.get("kspill", 0) if oh < 2 else 0)
                prev[f"kev{pb}"] = em.emit("scalar", lambda e, oh=oh, pb=pb: e.copy(
                    kstage[:, oh * 512:oh * 512 + 512], ps[pb][:]), inc=("act", 1))
            em.wait("gpsimd", "act", em.cnt["act"])
            for pc in range(4):
                prev["kspill"] = _dma(em, st_ksp,
                                      _r3(kcache[:, tg])[:, 4 * pc:4 * pc + 4, :],
                                      _s3(kstage, 512)[:, 4 * pc:4 * pc + 4, :])
            for ts_ in range(4):
                em.wait("tensor", "vec", ckvn_ready)
                em.wait("tensor", "act", prev.get("vev", 0))
                em.wait("tensor", "act", prev.get("kev0", 0))
                em.wait("tensor", "act", prev.get("kev1", 0))
                if ts_ == 0:
                    em.wait("tensor", st_vsp, prev.get("vspill", 0))
                for ic in range(4):
                    em.emit("tensor", lambda e, ts_=ts_, ic=ic, cb=cb: e.ldweights(
                        ckv_n[:, cb + ic * 512 + ts_ * 128:cb + ic * 512 + ts_ * 128 + 128]))
                    for og in range(4):
                        lm = em.emit("tensor", lambda e, ts_=ts_, ic=ic, og=og, cb=cb: e.matmul(
                            ps[og][:], ckv_n[:, cb + ic * 512 + ts_ * 128:cb + ic * 512 + ts_ * 128 + 128],
                            wkbv_t[:, ic * 2048 + og * 512:ic * 2048 + og * 512 + 512],
                            start=(ic == 0), stop=(ic == 3)),
                            inc=(("ten", 1) if (ic == 3 and og == 3) else None))
                em.wait("scalar", "ten", lm)
                for og in range(4):
                    prev["vev"] = em.emit("scalar", lambda e, ts_=ts_, og=og: e.copy(
                        vstage[:, ts_ * 2048 + og * 512:ts_ * 2048 + og * 512 + 512],
                        ps[og][:]), inc=("act", 1))
            em.wait("gpsimd", "act", em.cnt["act"])
            prev["vspill"] = _dma(em, st_vsp, _r3(vcache[tg, :]), _s3(vstage, 2048))

        for j in range(NG):
            emit_xload(j)
            if j > 0:
                emit_KB(j - 1)
            emit_A(j)
            emit_E(j)
            emit_R(j)
        emit_KB(NG - 1)
        if kpe_dbg is not None:
            em.wait("gpsimd", "vec", em.cnt["vec"])
            _dma(em, st_kpd, kpe_dbg[:], kpe_t[0:ROPE, :])
        em.barrier()

    q1_stack = ExitStack()
    q1_t = q1_stack.enter_context(nc.sbuf_tensor("q1_t", [128, 16 * 2048], FP8))
    with ExitStack() as a2:
        sba = lambda name, shape, dt: a2.enter_context(nc.sbuf_tensor(name, shape, dt))
        wst = sba("wst2", [128, 2 * 512], F32)
        wqa_t = sba("wqa_t", [128, 16 * QA], BF16)
        x_f = sba("x_f2", [128, 16 * 512], F32)
        xbf = sba("xbf2", [128, 16 * 512], BF16)
        xsq = sba("xsq2", [128, 2 * 512], F32)
        xnt = sba("xnt2", [128, 512], F32)
        qa_f = sba("qa_f", [128, 12 * 512], BF16)
        qa_n = sba("qa_n", [128, 12 * 512], BF16)
        qsq = sba("qsq", [128, 12 * 512], BF16)
        tmq = sba("tmq", [128, 2 * 512], F32)
        w1q = sba("w1q", [128, 12], F32)

        wcast = load_weight(em, wst, wqa_t, wqa, 16, QA)
        em.wait("tensor", "vec", wcast)
        w1marks = {}
        for oc in range(12):
            em.wait("tensor", "act", w1marks.get(oc % 4, 0))
            for ic in range(16):
                wm = em.emit("tensor", lambda e, oc=oc, ic=ic: e.matmul(
                    ps[oc % 4][:, 0:1], wqa_t[:, ic * QA + oc * 128:ic * QA + oc * 128 + 128],
                    ones_b[:], start=(ic == 0), stop=(ic == 15)),
                    inc=(("ten", 1) if ic == 15 else None))
            em.wait("scalar", "ten", wm)
            w1marks[oc % 4] = em.emit("scalar", lambda e, oc=oc: e.copy(w1q[:, oc:oc + 1], ps[oc % 4][:, 0:1]), inc=("act", 1))
        em.wait("tensor", "act", max(w1marks.values()))

        prev = {}
        for g in range(OG):
            tg = slice(TG * g, TG * (g + 1))
            if g > 0:
                em.wait("sync", "vec", prev["xn_last"])
            xm = None
            for pc in range(4):
                xm = _dma(em, st_x,
                          _s3(x_f, 512)[:, 4 * pc:4 * pc + 4, :],
                          _r3(xT_own[:, tg])[:, 4 * pc:4 * pc + 4, :])
            em.wait("gpsimd", "ten", prev.get("qa_mm", 0))
            xbm = None
            for pc in range(4):
                xbm = _dma(em, st_xbf,
                           _s3(xbf, 512)[:, 4 * pc:4 * pc + 4, :],
                           _r3(xT_own[:, tg])[:, 4 * pc:4 * pc + 4, :])
            sq_last = None
            for c in range(16):
                em.wait("scalar", st_x, xm)
                em.wait("scalar", "ten", prev.get("statmm", 0))
                sq_last = em.emit("scalar", lambda e, c=c: e.square(
                    xsq[:, (c % 2) * 512:(c % 2) * 512 + 512], x_f[:, c * 512:c * 512 + 512]), inc=("act", 1))
                em.wait("tensor", st_x, xm)
                em.wait("tensor", "act", sq_last)
                if c == 0:
                    em.wait("tensor", "vec", prev.get("xn_last", 0))
                    em.wait("tensor", "vec", prev.get("qan", 0))
                em.emit("tensor", lambda e, c=c: e.matmul(
                    ps[6][0:1, :], ones_c[:], x_f[:, c * 512:c * 512 + 512], start=(c == 0), stop=(c == 15)))
                prev["statmm"] = em.emit("tensor", lambda e, c=c: e.matmul(
                    ps[7][0:1, :], ones_c[:], xsq[:, (c % 2) * 512:(c % 2) * 512 + 512],
                    start=(c == 0), stop=(c == 15)), inc=("ten", 1))
            em.wait("vector", "ten", prev["statmm"])
            ln_rows(D, EPS_LN)
            em.wait("tensor", "vec", em.cnt["vec"])
            em.emit("tensor", lambda e: e.matmul(ps[6][:], ones_r[:], row(4), start=True, stop=True))
            bm = em.emit("tensor", lambda e: e.matmul(ps[7][:], ones_r[:], row(5), start=True, stop=True), inc=("ten", 1))
            def emit_xn_chunk(c, g=g):
                em.wait("vector", "ten", bm)
                em.wait("vector", "vec", em.cnt["vec"])
                em.emit("vector", lambda e, c=c: e.tensor_mul(
                    xnt[:, 0:512], x_f[:, c * 512:c * 512 + 512], ps[6][:]), inc=("vec", 1))
                em.wait("vector", "vec", em.cnt["vec"])
                em.emit("vector", lambda e, c=c: e.tensor_add(
                    xnt[:, 0:512], xnt[:, 0:512], ps[7][:]), inc=("vec", 1))
                em.wait("vector", "vec", em.cnt["vec"])
                prev["xn_last"] = em.emit("vector", lambda e, c=c, g=g: e.tensor_scalar(
                    q1_t[:, c * 2048 + g * 512:c * 2048 + g * 512 + 512],
                    xnt[:, 0:512], sc_t[:, 0:1], None, ALU.mult), inc=("vec", 1))

            sqq = None
            for oc in range(12):
                pb = oc % 4
                ts2 = oc % 2
                em.wait("tensor", st_xbf, xbm)
                em.wait("tensor", "act", prev.get(f"qaev{pb}", 0))
                for c in range(16):
                    em.emit("tensor", lambda e, oc=oc, c=c: e.ldweights(
                        wqa_t[:, c * QA + oc * 128:c * QA + oc * 128 + 128]))
                    qm = em.emit("tensor", lambda e, oc=oc, c=c, pb=pb: e.matmul(
                        ps[pb][:], wqa_t[:, c * QA + oc * 128:c * QA + oc * 128 + 128],
                        xbf[:, c * 512:c * 512 + 512], start=(c == 0), stop=(c == 15)),
                        inc=(("ten", 1) if c == 15 else None))
                prev["qa_mm"] = qm
                em.wait("scalar", "ten", qm)
                em.wait("scalar", "ten", bm)
                em.wait("scalar", "vec", prev.get(f"qmul{ts2}", 0))
                cpm = em.emit("scalar", lambda e, pb=pb, ts2=ts2: e.copy(
                    tmq[:, ts2 * 512:ts2 * 512 + 512], ps[pb][:]), inc=("act", 1))
                prev[f"qaev{pb}"] = cpm
                em.wait("vector", "act", cpm)
                em.wait("vector", "ten", bm)
                em.wait("vector", "vec", em.cnt["vec"])
                prev[f"qmul{ts2}"] = em.emit("vector", lambda e, ts2=ts2: e.tensor_mul(
                    tmq[:, ts2 * 512:ts2 * 512 + 512], tmq[:, ts2 * 512:ts2 * 512 + 512], ps[6][:]), inc=("vec", 1))
                em.wait("vector", "vec", em.cnt["vec"])
                qafm = em.emit("vector", lambda e, oc=oc, ts2=ts2: e.scalar_tensor_tensor(
                    qa_f[:, oc * 512:oc * 512 + 512], ps[7][:], w1q[:, oc:oc + 1],
                    tmq[:, ts2 * 512:ts2 * 512 + 512], ALU.mult, ALU.add), inc=("vec", 1))
                em.wait("scalar", "vec", qafm)
                em.wait("scalar", "ten", prev.get("qsq_mm", 0))
                sqq = em.emit("scalar", lambda e, oc=oc: e.square(
                    qsq[:, oc * 512:oc * 512 + 512], qa_f[:, oc * 512:oc * 512 + 512]), inc=("act", 1))
                emit_xn_chunk(oc)
            for c in range(12, 16):
                emit_xn_chunk(c)
            em.wait("tensor", "act", sqq)
            for oc in range(12):
                prev["qsq_mm"] = em.emit("tensor", lambda e, oc=oc: e.matmul(
                    ps[4][0:1, :], ones_b[:], qsq[:, oc * 512:oc * 512 + 512],
                    start=(oc == 0), stop=(oc == 11)), inc=(("ten", 1) if oc == 11 else None))
            em.wait("vector", "ten", prev["qsq_mm"])
            em.emit("vector", lambda e: e.tensor_scalar(row(6), ps[4][0:1, :], 1.0 / QA, EPS_RMS, ALU.mult, ALU.add), inc=("vec", 1))
            em.wait("scalar", "vec", em.cnt["vec"])
            smx = em.emit("scalar", lambda e: e.sqrt(row(7), row(6)), inc=("act", 1))
            em.wait("vector", "act", smx)
            rsm = em.emit("vector", lambda e: e.reciprocal(row(6), row(7)), inc=("vec", 1))
            em.wait("tensor", "vec", rsm)
            bmq = em.emit("tensor", lambda e: e.matmul(ps[5][:], ones_r[:], row(6), start=True, stop=True), inc=("ten", 1))
            qan = None
            for oc in range(12):
                em.wait("vector", "ten", bmq)
                em.wait("vector", st_qasp, prev.get("qaspill", 0) if oc < 2 else 0)
                em.wait("vector", "vec", em.cnt["vec"])
                em.emit("vector", lambda e, oc=oc: e.tensor_mul(
                    tmq[:, 0:512], qa_f[:, oc * 512:oc * 512 + 512], ps[5][:]), inc=("vec", 1))
                em.wait("vector", "vec", em.cnt["vec"])
                qan = em.emit("vector", lambda e, oc=oc: e.tensor_scalar(
                    qa_n[:, oc * 512:oc * 512 + 512], tmq[:, 0:512], gqa_t[:, oc:oc + 1], None, ALU.mult), inc=("vec", 1))
            prev["qan"] = qan
            em.wait("gpsimd", "vec", qan)
            for pc in range(3):
                prev["qaspill"] = _dma(em, st_qasp,
                                       _r3(qacache[:, tg])[:, 4 * pc:4 * pc + 4, :],
                                       _s3(qa_n, 512)[:, 4 * pc:4 * pc + 4, :])
        em.barrier()

    with ExitStack() as a3:
        sba = lambda name, shape, dt: a3.enter_context(nc.sbuf_tensor(name, shape, dt))
        wst = sba("wst3", [128, 2 * 2048], F32)
        wqb_t = sba("wqb_t", [128, 12 * 3072], BF16)
        qa_ng = sba("qa_ng", [128, 12 * 512], BF16)
        qnstage = sba("qnstage", [128, 16 * 512], BF16)
        qr_f = sba("qr_f", [128, 2 * 512], F32)
        qr_t1 = sba("qr_t1", [128, 512], F32)
        qr_t2 = sba("qr_t2", [128, 512], F32)
        qrstage = sba("qrstage", [128, 8 * 512], BF16)
        cosr2 = sba("cosr2", [128, 512], F32)
        sinr2 = sba("sinr2", [128, 512], F32)

        for ic in range(12):
            load_weight(em, wst,
                        wqb_t[:, ic * 3072:ic * 3072 + 2048],
                        wqbn[ic * 128:(ic + 1) * 128, :], 1, 2048)
        for ic in range(12):
            load_weight(em, wst,
                        wqb_t[:, ic * 3072 + 2048:ic * 3072 + 3072],
                        wqbr[ic * 128:(ic + 1) * 128, :], 1, 1024)
        prev = {}
        for g in range(OG):
            tg = slice(TG * g, TG * (g + 1))
            em.wait("tensor", "vec", em.cnt["vec"])
            em.wait("tensor", "act", em.cnt["act"])
            if g > 0:
                em.wait("sync", "ten", prev["qb_last"])
                em.wait("sync", "vec", prev["qrf_last"])
            qam = None
            for pc in range(3):
                qam = _dma(em, st_x,
                           _s3(qa_ng, 512)[:, 4 * pc:4 * pc + 4, :],
                           _r3(qacache[:, tg])[:, 4 * pc:4 * pc + 4, :])
            _dma(em, st_cs, cosr2[0:ROPE, :], coso[:, tg])
            _dma(em, st_cs, cosr2[ROPE:128, :], coso[:, tg])
            _dma(em, st_cs, sinr2[0:ROPE, :], sino[:, tg])
            sm_ = _dma(em, st_cs, sinr2[ROPE:128, :], sino[:, tg])
            em.wait("tensor", st_x, qam)
            for oh in range(16):
                pb = oh % 2
                em.wait("tensor", "act", prev.get(f"qnev{pb}", 0))
                for ic in range(12):
                    em.emit("tensor", lambda e, oh=oh, ic=ic: e.ldweights(
                        wqb_t[:, ic * 3072 + oh * 128:ic * 3072 + oh * 128 + 128]))
                    qm = em.emit("tensor", lambda e, oh=oh, ic=ic, pb=pb: e.matmul(
                        ps[pb][:], wqb_t[:, ic * 3072 + oh * 128:ic * 3072 + oh * 128 + 128],
                        qa_ng[:, ic * 512:ic * 512 + 512], start=(ic == 0), stop=(ic == 11)),
                        inc=(("ten", 1) if ic == 11 else None))
                em.wait("scalar", "ten", qm)
                em.wait("scalar", st_qnsp, prev.get("qnspill", 0) if oh < 2 else 0)
                prev[f"qnev{pb}"] = em.emit("scalar", lambda e, oh=oh, pb=pb: e.copy(
                    qnstage[:, oh * 512:oh * 512 + 512], ps[pb][:]), inc=("act", 1))
            em.wait("gpsimd", "act", em.cnt["act"])
            for pc in range(4):
                prev["qnspill"] = _dma(em, st_qnsp,
                                       _r3(qncache[:, tg])[:, 4 * pc:4 * pc + 4, :],
                                       _s3(qnstage, 512)[:, 4 * pc:4 * pc + 4, :])
            em.wait("vector", st_cs, sm_)
            qrf_mark = None
            for oc in range(8):
                pb = 2 + (oc % 2)
                pr = 4 + (oc % 2)
                em.wait("tensor", "act", prev.get(f"qrev{pb}", 0))
                for ic in range(12):
                    em.emit("tensor", lambda e, oc=oc, ic=ic: e.ldweights(
                        wqb_t[:, ic * 3072 + 2048 + oc * 128:ic * 3072 + 2048 + oc * 128 + 128]))
                    qm = em.emit("tensor", lambda e, oc=oc, ic=ic, pb=pb: e.matmul(
                        ps[pb][:], wqb_t[:, ic * 3072 + 2048 + oc * 128:ic * 3072 + 2048 + oc * 128 + 128],
                        qa_ng[:, ic * 512:ic * 512 + 512], start=(ic == 0), stop=(ic == 11)),
                        inc=(("ten", 1) if ic == 11 else None))
                em.wait("scalar", "ten", qm)
                em.wait("scalar", "vec", prev.get(f"qrf{oc % 2}", 0))
                evm = em.emit("scalar", lambda e, oc=oc, pb=pb: e.copy(
                    qr_f[:, (oc % 2) * 512:(oc % 2) * 512 + 512], ps[pb][:]), inc=("act", 1))
                prev[f"qrev{pb}"] = evm
                em.wait("tensor", "act", evm)
                em.wait("tensor", "vec", prev.get(f"qrs{oc % 2}", 0))
                rm = em.emit("tensor", lambda e, oc=oc, pr=pr: e.matmul(
                    ps[pr][:], rotT[:], qr_f[:, (oc % 2) * 512:(oc % 2) * 512 + 512],
                    start=True, stop=True), inc=("ten", 1))
                em.wait("vector", "act", evm)
                em.wait("vector", "vec", em.cnt["vec"])
                em.emit("vector", lambda e, oc=oc: e.tensor_mul(
                    qr_t1[:], qr_f[:, (oc % 2) * 512:(oc % 2) * 512 + 512], cosr2[:]), inc=("vec", 1))
                em.wait("vector", "ten", rm)
                prev[f"qrs{oc % 2}"] = em.emit("vector", lambda e, oc=oc, pr=pr: e.tensor_mul(
                    qr_t2[:], ps[pr][:], sinr2[:]), inc=("vec", 1))
                em.wait("vector", "vec", em.cnt["vec"])
                em.wait("vector", st_qrsp, prev.get("qrspill", 0) if oc < 2 else 0)
                qrf_mark = em.emit("vector", lambda e, oc=oc: e.tensor_add(
                    qrstage[:, oc * 512:oc * 512 + 512], qr_t1[:], qr_t2[:]), inc=("vec", 1))
                prev[f"qrf{oc % 2}"] = qrf_mark
            prev["qb_last"] = em.cnt["ten"]
            prev["qrf_last"] = qrf_mark
            em.wait("gpsimd", "vec", em.cnt["vec"])
            for oc in range(8):
                _dma(em, st_qrsp,
                     qrcache[:, (2 * oc) * 2048 + TG * g:(2 * oc) * 2048 + TG * g + TG],
                     qrstage[0:ROPE, oc * 512:oc * 512 + 512])
                prev["qrspill"] = _dma(
                    em, st_qrsp,
                    qrcache[:, (2 * oc + 1) * 2048 + TG * g:(2 * oc + 1) * 2048 + TG * g + TG],
                    qrstage[ROPE:128, oc * 512:oc * 512 + 512])
        em.barrier()

    with ExitStack() as a4:
        sba = lambda name, shape, dt: a4.enter_context(nc.sbuf_tensor(name, shape, dt))
        kseg = [sba(f"kseg{i}", [128, 16 * 512], BF16) for i in range(2)]
        vseg = [sba(f"vseg{i}", [128, 4 * 2048], BF16) for i in range(2)]
        qn_g = sba("qn_g", [128, 16 * 512], BF16)
        qr_g = sba("qr_g", [128, 16 * 512], BF16)
        pt = sba("pt", [128, 8 * 512], BF16)
        ptot = sba("ptot", [128, 16 * 512], BF16)
        acc = sba("acc", [128, 16 * 512], BF16)
        m_t = sba("m_t", [128, 8 * 512], BF16)
        atn = sba("atn", [128, 2 * 512], BF16)

        mm_ = _dma(em, st_mask, m_t[:], masks_d[:])
        zm = em.emit("vector", lambda e: e.memset(qr_g[ROPE:128, :], 0.0), inc=("vec", 1))
        em.wait("tensor", "vec", zm)

        st_wpre = em.stream("stwpre", "gpsimd")
        wo_pre = [0]

        def prestage_wo(upto):
            if wo_pre[0] >= upto:
                return
            em.wait("gpsimd", "ten", em.cnt["ten"])
            for c in range(wo_pre[0], upto):
                _dma(em, st_wpre, wo_bf[c * 128:(c + 1) * 128, :],
                     wo[c * 128:(c + 1) * 128, :])
            wo_pre[0] = upto
        prev = {}
        seg_done = {}
        pt_done = {}
        ptot_done = {}
        ptot_h = {}
        bc_marks = {}
        q_marks = {}
        kv_marks = {}
        seg_slot = {}
        slot_free = {}
        scores_h_done = {}
        segslot = [0]
        for g in range(OG):
            tg = slice(TG * g, TG * (g + 1))
            nseg = 2 * (g + 1)
            nkb_g = 8 * (g + 1)
            if g >= 1:
                prestage_wo((6, 11, 16)[g - 1])
            if g > 0:
                em.wait("tensor", "vec", em.cnt["vec"])
                em.wait("tensor", "act", em.cnt["act"])

            def load_qh(gq, h):
                stq = em.stream(f"stqg{gq % 2}", "sync")
                if gq > 0:
                    em.wait("sync", "ten", scores_h_done.get((gq - 1, h), 0))
                m1 = _dma(em, stq, _s3(qn_g, 512)[:, h:h + 1, :],
                          _r3(qncache[:, TG * gq:TG * gq + TG])[:, h:h + 1, :])
                m1 = _dma(em, stq, qr_g[0:ROPE, h * 512:h * 512 + 512],
                          qrcache[:, h * 2048 + TG * gq:h * 2048 + TG * gq + TG])
                q_marks[gq] = (stq, m1)

            def load_seg(si, g=g):
                nseg_ = 2 * (g + 1)
                if si >= nseg_:
                    g2 = g + 1
                    si2 = si - nseg_
                    if g2 >= OG or si2 >= 2 * (g2 + 1) or si2 >= 2:
                        return
                    slot = segslot[0] % 2
                    segslot[0] += 1
                    seg_slot[(g2, si2)] = slot
                    em.wait("sync", "ten", slot_free.get(slot, 0))
                    for pc in range(4):
                        _dma(em, st_seg[slot],
                             _s3(kseg[slot], 512)[:, 4 * pc:4 * pc + 4, :],
                             _r3(kcache[:, 512 * si2:512 * si2 + 512])[:, 4 * pc:4 * pc + 4, :])
                    kv_marks[(g2, si2)] = _dma(em, st_seg[slot], _s3(vseg[slot], 2048),
                                               _r3(vcache[512 * si2:512 * si2 + 512, :]))
                    return
                if (g, si) in seg_slot:
                    return
                slot = segslot[0] % 2
                segslot[0] += 1
                seg_slot[(g, si)] = slot
                em.wait("sync", "ten", slot_free.get(slot, 0))
                for pc in range(4):
                    _dma(em, st_seg[slot],
                         _s3(kseg[slot], 512)[:, 4 * pc:4 * pc + 4, :],
                         _r3(kcache[:, 512 * si:512 * si + 512])[:, 4 * pc:4 * pc + 4, :])
                kv_marks[(g, si)] = _dma(em, st_seg[slot], _s3(vseg[slot], 2048),
                                         _r3(vcache[512 * si:512 * si + 512, :]))

            if g == 0:
                for h in range(H):
                    load_qh(0, h)
            load_seg(0)
            load_seg(1)
            tiles = [(seg, h, kb) for seg in range(nseg) for h in range(H)
                     for kb in range(NKB_SEG)]
            BANKS = (2, 3, 5)
            BSZ = 2

            def scores_batch(batch):
                for t, ti in batch:
                    seg, h, kb = t
                    pscore = BANKS[ti % 3]
                    if kb == 0 and h == 0:
                        em.wait("tensor", st_seg[seg_slot[(g, seg)]], kv_marks[(g, seg)])
                    if kb == 0 and h == 0 and seg == 0:
                        em.wait("tensor", q_marks[g][0], q_marks[g][1])
                    em.wait("tensor", "act", prev.get(f"exp{pscore}", 0))
                scm = {}
                for t, ti in batch:
                    seg, h, kb = t
                    pscore = BANKS[ti % 3]
                    sl = seg_slot[(g, seg)]
                    em.emit("tensor", lambda e, sl=sl, h=h, kb=kb: e.ldweights(
                        kseg[sl][:, h * 512 + kb * 128:h * 512 + kb * 128 + 128]))
                    em.emit("tensor", lambda e, sl=sl, h=h, kb=kb, pscore=pscore: e.matmul(
                        ps[pscore][:], kseg[sl][:, h * 512 + kb * 128:h * 512 + kb * 128 + 128],
                        qn_g[:, h * 512:h * 512 + 512], start=True, stop=False))
                    em.emit("tensor", lambda e, seg=seg, kb=kb: e.ldweights(
                        kpe_t[:, seg * 512 + kb * 128:seg * 512 + kb * 128 + 128]))
                    scm[ti] = em.emit("tensor", lambda e, seg=seg, h=h, kb=kb, pscore=pscore: e.matmul(
                        ps[pscore][:], kpe_t[:, seg * 512 + kb * 128:seg * 512 + kb * 128 + 128],
                        qr_g[:, h * 512:h * 512 + 512],
                        start=False, stop=True), inc=("ten", 1))
                    if seg == nseg - 1 and kb == NKB_SEG - 1:
                        scores_h_done[(g, h)] = scm[ti]
                        if g + 1 < OG:
                            load_qh(g + 1, h)
                out = []
                for t, ti in batch:
                    seg, h, kb = t
                    jglob = seg * NKB_SEG + kb
                    r = ti % 8
                    pscore = BANKS[ti % 3]
                    em.wait("scalar", "ten", scm[ti])
                    em.wait("scalar", "ten", pt_done.get(r, 0))
                    em.wait("scalar", "vec", ptot_done.get(r, 0))
                    ex_m = em.emit("scalar", lambda e, r=r, pscore=pscore: e.activation(
                        pt[:, r * 512:r * 512 + 512], ps[pscore][:], AF.Exp, scale=SCALE),
                        inc=("act", 1))
                    prev[f"exp{pscore}"] = ex_m
                    band = jglob >= nkb_g - 8
                    mk_m = None
                    if band:
                        bp = jglob - (nkb_g - 8)
                        em.wait("vector", "act", ex_m)
                        em.wait("vector", st_mask, mm_)
                        mk_m = em.emit("vector", lambda e, r=r, bp=bp: e.tensor_mul(
                            pt[:, r * 512:r * 512 + 512], pt[:, r * 512:r * 512 + 512],
                            m_t[:, bp * 512:bp * 512 + 512]), inc=("vec", 1))
                    out.append((t, ti, ex_m, mk_m))
                return out

            def av_batch(batch):
                for t, ti, ex_m, mk_m in batch:
                    seg, h, kb = t
                    if kb == 0:
                        em.wait("tensor", "vec", prev.get(f"accv{h % 2}", 0))
                    if mk_m is not None:
                        em.wait("tensor", "vec", mk_m)
                    else:
                        em.wait("tensor", "act", ex_m)
                for t, ti, ex_m, mk_m in batch:
                    seg, h, kb = t
                    r = ti % 8
                    pa = 0 if h % 2 == 0 else 4
                    sl = seg_slot[(g, seg)]
                    em.emit("tensor", lambda e, sl=sl, h=h, kb=kb: e.ldweights(
                        vseg[sl][:, kb * 2048 + h * 128:kb * 2048 + h * 128 + 128]))
                    av_m = em.emit("tensor", lambda e, sl=sl, h=h, kb=kb, r=r, pa=pa: e.matmul(
                        ps[pa][:], vseg[sl][:, kb * 2048 + h * 128:kb * 2048 + h * 128 + 128],
                        pt[:, r * 512:r * 512 + 512], start=(kb == 0), stop=(kb == 3)),
                        inc=("ten", 1))
                    pt_done[r] = av_m
                    if mk_m is None:
                        em.wait("vector", "act", ex_m)
                    if seg == 0 and kb == 0:
                        pm = em.emit("vector", lambda e, h=h, r=r: e.tensor_copy(
                            ptot[:, h * 512:h * 512 + 512], pt[:, r * 512:r * 512 + 512]), inc=("vec", 1))
                    else:
                        em.wait("vector", "vec", ptot_h.get(h, 0))
                        pm = em.emit("vector", lambda e, h=h, r=r: e.tensor_add(
                            ptot[:, h * 512:h * 512 + 512], ptot[:, h * 512:h * 512 + 512],
                            pt[:, r * 512:r * 512 + 512]), inc=("vec", 1))
                    ptot_done[r] = pm
                    ptot_h[h] = pm
                    if kb == 3:
                        em.wait("vector", "ten", av_m)
                        if seg == 0:
                            prev[f"accv{h % 2}"] = em.emit("vector", lambda e, h=h, pa=pa: e.tensor_copy(
                                acc[:, h * 512:h * 512 + 512], ps[pa][:]), inc=("vec", 1))
                        else:
                            prev[f"accv{h % 2}"] = em.emit("vector", lambda e, h=h, pa=pa: e.tensor_add(
                                acc[:, h * 512:h * 512 + 512], acc[:, h * 512:h * 512 + 512], ps[pa][:]), inc=("vec", 1))
                        if seg == nseg - 1:
                            emit_norm1(h)
                            if len(norm_pend) > 2:
                                emit_norm2()
                    if kb == 3 and h == H - 1:
                        seg_done[(g, seg)] = em.cnt["ten"]
                        slot_free[seg_slot[(g, seg)]] = em.cnt["ten"]
                        load_seg(seg + 2)

            norm_pend = []

            def emit_norm1(h):
                pu = 1 if h % 2 == 0 else 7
                em.wait("tensor", "vec", ptot_h[h])
                em.wait("tensor", "vec", prev.get(f"rcp{h % 2}", 0))
                sm_m = em.emit("tensor", lambda e, h=h, pu=pu: e.matmul(
                    ps[pu][0:1, :], ones_b[:], ptot[:, h * 512:h * 512 + 512],
                    start=True, stop=True), inc=("ten", 1))
                em.wait("vector", "ten", sm_m)
                em.wait("vector", "ten", bc_marks.get(h % 4, 0))
                def _recip(e, h=h, pu=pu):
                    with nc.allow_low_precision(reason="softmax denominator bcast in bf16"):
                        return e.reciprocal(rowb2(h % 4), ps[pu][0:1, :])
                rc_m = em.emit("vector", _recip, inc=("vec", 1))
                prev[f"rcp{h % 2}"] = rc_m
                norm_pend.append((h, rc_m))

            def emit_norm2(g=g, tg=tg):
                if not norm_pend:
                    return
                h, rc_m = norm_pend.pop(0)
                em.wait("tensor", "vec", rc_m)
                em.wait("tensor", "vec", prev.get("nrm", 0))
                bc_m = em.emit("tensor", lambda e, h=h: e.matmul(
                    ps[6][:], ones_rb[:], rowb2(h % 4), start=True, stop=True), inc=("ten", 1))
                bc_marks[h % 4] = bc_m
                em.wait("vector", "ten", bc_m)
                em.wait("vector", st_at[h % 2], prev.get(f"atsp{h % 2}", 0))
                nm = em.emit("vector", lambda e, h=h: e.tensor_mul(
                    atn[:, (h % 2) * 512:(h % 2) * 512 + 512], acc[:, h * 512:h * 512 + 512], ps[6][:]),
                    inc=("vec", 1))
                prev["nrm"] = nm
                em.wait("gpsimd", "vec", nm)
                prev[f"atsp{h % 2}"] = _dma(
                    em, st_at[h % 2], atcache[h * 128:h * 128 + 128, tg],
                    atn[:, (h % 2) * 512:(h % 2) * 512 + 512])

            batches = [[(t, ti) for t, ti in
                        zip(tiles[i:i + BSZ], range(i, i + BSZ))]
                       for i in range(0, len(tiles), BSZ)]
            pending = None
            for bt in batches:
                done = scores_batch(bt)
                if pending is not None:
                    av_batch(pending)
                pending = done
            av_batch(pending)
            while norm_pend:
                emit_norm2()
            seg_done[(g, "last")] = em.cnt["ten"]
            prev["glast"] = em.cnt["ten"]

        em.barrier()

    with ExitStack() as a5:
        sba = lambda name, shape, dt: a5.enter_context(nc.sbuf_tensor(name, shape, dt))
        wst2b = sba("wst5b", [128, 2 * 1024], F32)
        wo_t = sba("wo_t", [128, 16 * 2048], BF16)
        w2q = sba("w2q", [128, 16 * 2048], FP8)
        at_g = sba("at_g", [128, 16 * 512], BF16)
        xrot = sba("xrot", [128, 2 * 512], F32)
        frot = sba("frot", [128, 2 * 512], F32)

        for qc in range(4):
            em.emit("vector", lambda e, qc=qc: e.tensor_copy(
                q1_t[:, qc * 8192:(qc + 1) * 8192].bitcast(U32),
                q1_t[:, qc * 8192:(qc + 1) * 8192].bitcast(U32)), inc=("vec", 1))
        st_wo = em.stream("stwo", "sync")
        wom = 0
        for c in range(16):
            wom = _dma(em, st_wo, wo_t[:, c * 2048:(c + 1) * 2048],
                       wo_bf[c * 128:(c + 1) * 128, :])
        em.wait("tensor", st_wo, wom)
        wcast5 = load_weight(em, wst2b, w2q, w2, 16, 2048, fp8_scale=sc_t[:, 1:2],
                             queue="gpsimd", bitcast_each=True)
        em.wait("tensor", "vec", wcast5)
        prev = {}
        for tgi in range(4):
            tg = slice(TG * tgi, TG * (tgi + 1))
            if tgi > 0:
                em.wait("sync", "ten", prev["olast"])
            am = None
            for pc in range(4):
                am = _dma(em, st_x, _s3(at_g, 512)[:, 4 * pc:4 * pc + 4, :],
                          _r3(atcache[:, tg])[:, 4 * pc:4 * pc + 4, :])
            for m in range(16):
                pb = m % 2
                em.wait("sync", "vec", prev.get(f"res{pb}", 0))
                xm = _dma(em, st_xr[pb], xrot[:, pb * 512:pb * 512 + 512],
                          xT_own[m * 128:m * 128 + 128, tg])
                em.wait("tensor", st_x, am)
                em.wait("tensor", "vec", prev.get(f"res{pb}", 0))
                em.wait("tensor", "act", prev.get(f"fsc{pb}", 0))
                for h in range(16):
                    em.emit("tensor", lambda e, h=h, m=m: e.ldweights(
                        wo_t[:, h * 2048 + m * 128:h * 2048 + m * 128 + 128]))
                    em.emit("tensor", lambda e, h=h, m=m, pb=pb: e.matmul(
                        ps[pb][:], wo_t[:, h * 2048 + m * 128:h * 2048 + m * 128 + 128],
                        at_g[:, h * 512:h * 512 + 512], start=(h == 0), stop=(h == 15)))
                for c2 in range(8):
                    wap = w2q[:, 2 * c2 * 2048:(2 * c2 + 2) * 2048].rearrange(
                        "p (a q) -> p a q", q=2048)[:, :, m * 128:m * 128 + 128]
                    xap = q1_t[:, 2 * c2 * 2048:(2 * c2 + 2) * 2048].rearrange(
                        "p (a q) -> p a q", q=2048)[:, :, tgi * 512:tgi * 512 + 512]
                    em.emit("tensor", lambda e, wap=wap: e.ldweights(wap, perf_mode=DR))
                    mm2 = em.emit("tensor", lambda e, wap=wap, xap=xap, pb=pb, c2=c2: e.matmul(
                        ps[2 + pb][:], wap, xap, start=(c2 == 0), stop=(c2 == 7), perf_mode=DR),
                        inc=(("ten", 1) if c2 == 7 else None))
                prev["olast"] = mm2
                em.wait("scalar", "ten", mm2)
                em.wait("scalar", "vec", prev.get(f"res{pb}", 0))
                em.wait("scalar", st_out[pb], prev.get(f"ost{pb}", 0))
                fm = em.emit("scalar", lambda e, pb=pb: e.mul(
                    frot[:, pb * 512:pb * 512 + 512], ps[2 + pb][:], rcol[:]), inc=("act", 1))
                prev[f"fsc{pb}"] = fm
                em.wait("vector", "act", fm)
                em.wait("vector", "ten", mm2)
                em.wait("vector", st_xr[pb], xm)
                em.wait("vector", st_out[pb], prev.get(f"ost{pb}", 0))
                em.emit("vector", lambda e, pb=pb: e.tensor_add(
                    frot[:, pb * 512:pb * 512 + 512], frot[:, pb * 512:pb * 512 + 512], ps[pb][:]),
                    inc=("vec", 1))
                em.wait("vector", "vec", em.cnt["vec"])
                rm = em.emit("vector", lambda e, pb=pb: e.tensor_add(
                    frot[:, pb * 512:pb * 512 + 512], frot[:, pb * 512:pb * 512 + 512],
                    xrot[:, pb * 512:pb * 512 + 512]), inc=("vec", 1))
                prev[f"res{pb}"] = rm
                em.wait("gpsimd", "vec", rm)
                prev[f"ost{pb}"] = _dma(em, st_out[pb], outT[m * 128:m * 128 + 128, tg],
                                        frot[:, pb * 512:pb * 512 + 512])
        for name, queue in em.dstreams.items():
            eng = "sync" if queue == "sync" else "gpsimd"
            em.wait(eng, name, em.cnt[name])
        em.flush()

    q1_stack.close()
    top.close()
    return nc



def _prep_core_inputs(c, x, shared):
    b, h = c // 2, c % 2
    xT = np.ascontiguousarray(x[b].T)
    own = np.concatenate([np.arange(1024 * g + 512 * h, 1024 * g + 512 * h + 512)
                          for g in range(4)])
    xT_own = np.ascontiguousarray(xT[:, own])
    coso_h = np.ascontiguousarray(shared["cosg"][:, own])
    sino_h = np.ascontiguousarray(shared["sing"][:, own])
    kj = np.arange(128)[:, None]
    qi = np.arange(512)[None, :]
    masks = np.concatenate(
        [(128 * bp + kj <= 512 * h + qi).astype(BF) for bp in range(8)], axis=1)
    m = dict(shared)
    m.update({"xT_all": xT, "xT_own": xT_own, "coso": coso_h, "sino": sino_h,
              "masks": masks})
    return m


def prep_all_inputs(L0_attn_hidden_states, L0_attn_q_a_w, L0_attn_q_a_ln_w,
                    L0_attn_q_b_w, L0_attn_kv_a_w, L0_attn_kv_a_ln_w,
                    L0_attn_kv_b_w, L0_attn_o_w, L0_ffn_input2, L0_ffn_scale,
                    L0_ffn_scale1):
    x = np.asarray(L0_attn_hidden_states, np.float32)
    qaw = np.asarray(L0_attn_q_a_w, np.float32)
    qbw = np.asarray(L0_attn_q_b_w, np.float32)
    kvaw = np.asarray(L0_attn_kv_a_w, np.float32)
    kvbw = np.asarray(L0_attn_kv_b_w, np.float32)
    ow = np.asarray(L0_attn_o_w, np.float32)

    qb3 = qbw.reshape(H, QHD, QA)
    kb3 = kvbw.reshape(H, NOPE + VD, CKV)

    inv = THETA ** (-np.arange(0, ROPE, 2, dtype=np.float64) / ROPE)
    t = np.arange(S, dtype=np.float64)
    ang = np.concatenate([np.outer(inv, t), np.outer(inv, t)], 0)

    p64 = np.zeros((ROPE, ROPE), np.float32)
    for i in range(32):
        p64[i, 32 + i] = -1.0
        p64[32 + i, i] = 1.0
    rotT_h = np.zeros((128, 128), np.float32)
    rotT_h[:ROPE, :ROPE] = p64.T
    rotT_h[ROPE:, ROPE:] = p64.T

    s0 = np.float32(np.asarray(L0_ffn_scale))
    s1 = np.float32(np.asarray(L0_ffn_scale1))

    shared = {
        "wqa": np.ascontiguousarray(qaw.T),
        "wqbn": np.ascontiguousarray(qb3[:, :NOPE, :].reshape(H * NOPE, QA).T),
        "wqbr": np.ascontiguousarray(qb3[:, NOPE:, :].reshape(H * ROPE, QA).T),
        "wkvc": np.ascontiguousarray(kvaw[:CKV, :].T),
        "wkvr": np.ascontiguousarray(kvaw[CKV:, :].T),
        "wkbk": np.ascontiguousarray(kb3[:, :NOPE, :].reshape(H * NOPE, CKV).T),
        "wkbv": np.ascontiguousarray(kb3[:, NOPE:, :].reshape(H * VD, CKV).T),
        "wo": np.ascontiguousarray(ow.T),
        "w2": np.ascontiguousarray(np.asarray(L0_ffn_input2, np.float32)),
        "gqa": np.ascontiguousarray(
            np.asarray(L0_attn_q_a_ln_w, np.float32).reshape(QA // 128, 128).T),
        "gkv": np.ascontiguousarray(
            np.asarray(L0_attn_kv_a_ln_w, np.float32).reshape(CKV // 128, 128).T),
        "cosg": np.cos(ang).astype(np.float32),
        "sing": np.sin(ang).astype(np.float32),
        "ones_c": np.ones((128, 1), np.float32),
        "ones_b": np.ones((128, 1), BF),
        "ones_r": np.ones((1, 128), np.float32),
        "ones_rb": np.ones((1, 128), BF),
        "rotT": rotT_h,
        "scales": np.concatenate([np.full((128, 1), s0, np.float32),
                                  np.full((128, 1), s1, np.float32)], 1),
    }
    return [_prep_core_inputs(c, x, shared) for c in range(8)]


def assemble_output(results):
    out = np.empty((B, S, D), np.float32)
    for c in range(8):
        b, h = c // 2, c % 2
        own = np.concatenate([np.arange(1024 * g + 512 * h, 1024 * g + 512 * h + 512)
                              for g in range(4)])
        out[b, own, :] = results[c]["outT"].T
    return out


def kernel(**inputs):
    in_maps = prep_all_inputs(**inputs)
    nc = build_program()
    res = run_bass_kernel_spmd(nc, in_maps, core_ids=list(range(8)))
    return assemble_output(res.results)


# version 41
# speedup vs baseline: 1.0501x; 1.0009x over previous
import numpy as np
import ml_dtypes
from contextlib import ExitStack

import concourse.bass as bass
import concourse.mybir as mybir
from concourse.bass_utils import run_bass_kernel_spmd

F32 = mybir.dt.float32
BF16 = mybir.dt.bfloat16
FP8 = mybir.dt.float8e4
AF = mybir.ActivationFunctionType
ALU = mybir.AluOpType
DR = mybir.MatmulPerfMode.DoubleRow
U32 = mybir.dt.uint32

BF = ml_dtypes.bfloat16

B, S, D = 4, 4096, 2048
H, NOPE, ROPE, VD = 16, 128, 64, 128
QHD = NOPE + ROPE
SCALE = QHD ** -0.5
THETA = 10000.0
CKV = 512
QA = 1536
EPS_LN = 1e-5
EPS_RMS = 1e-6

TG = 512
NG = S // TG
OG = 4
NKB_SEG = 4

ENGINES = ("sync", "tensor", "vector", "scalar", "gpsimd")


class Em:
    def __init__(self, nc, block):
        self.nc = nc
        self.block = block
        self._stack = ExitStack()
        self.sem = {}
        for name in ("ten", "act", "vec", "gsm", "ph"):
            self.sem[name] = self._stack.enter_context(nc.semaphore(name))
        self.cnt = {k: 0 for k in self.sem}
        self.q = {e: [] for e in ENGINES}
        self.lastwait = {}
        self.wst_marks = {}
        self.dstreams = {}

    def stream(self, name, queue):
        if name not in self.sem:
            self.sem[name] = self._stack.enter_context(self.nc.semaphore(name))
            self.cnt[name] = 0
        self.dstreams[name] = queue
        return name

    def emit(self, eng, build, inc=None):
        mark = None
        if inc is not None:
            s, a = inc
            self.cnt[s] += a
            mark = self.cnt[s]
            h = self.sem[s]

            def f(e, build=build, h=h, a=a):
                build(e).then_inc(h, a)
        else:
            def f(e, build=build):
                build(e)
        self.q[eng].append(f)
        return mark

    def wait(self, eng, s, val):
        if not val or val <= 0:
            return
        key = (eng, s)
        if self.lastwait.get(key, 0) >= val:
            return
        self.lastwait[key] = val
        h = self.sem[s]
        self.q[eng].append(lambda e, h=h, v=val: e.wait_ge(h, v))

    def flush(self):
        for eng in ENGINES:
            fns = self.q[eng]
            if not fns:
                continue
            self.q[eng] = []

            def body(e, fns=fns):
                for f in fns:
                    f(e)
            getattr(self.block, eng)(body)

    def barrier(self):
        for name, queue in self.dstreams.items():
            eng = "sync" if queue == "sync" else "gpsimd"
            self.wait(eng, name, self.cnt[name])
        for eng in ENGINES:
            self.cnt["ph"] += 1
            h = self.sem["ph"]
            self.q[eng].append(lambda e, h=h: e.drain().then_inc(h, 1))
        tgt = self.cnt["ph"]
        self.flush()
        for eng in ENGINES:
            self.wait(eng, "ph", tgt)


def _dma(em, stream, out_ap, in_ap):
    queue = em.dstreams[stream]
    return em.emit(queue, lambda e, o=out_ap, i=in_ap: e.dma_start(out=o, in_=i),
                   inc=(stream, 16))


def _r3(ap, p=128):
    return ap.rearrange("(c p) q -> p c q", p=p)


def _s3(ap, q):
    return ap.rearrange("p (c q) -> p c q", q=q)


def load_weight(em, wst, dst, src2d, C, K, fp8_scale=None, queue="sync",
                bitcast_each=False):
    key = id(wst)
    HS = wst.shape[1] // 2
    marks = em.wst_marks.setdefault(key, [])
    for c in range(C):
        for k0 in range(0, K, HS):
            kw = min(HS, K - k0)
            half = len(marks) % 2
            st = em.stream(f"w{half}" if queue == "sync" else f"wg{half}", queue)
            if len(marks) >= 2:
                em.wait("sync" if queue == "sync" else "gpsimd", "vec", marks[-2])
            m = _dma(em, st, wst[:, half * HS:half * HS + kw],
                     src2d[c * 128:(c + 1) * 128, k0:k0 + kw])
            em.wait("vector", st, m)
            d = dst[:, c * K + k0:c * K + k0 + kw]
            s = wst[:, half * HS:half * HS + kw]
            if fp8_scale is None:
                cm = em.emit("vector", lambda e, d=d, s=s: e.tensor_copy(d, s),
                             inc=("vec", 1))
            else:
                cm = em.emit("vector", lambda e, d=d, s=s, sc=fp8_scale:
                             e.tensor_scalar(d, s, sc, None, ALU.mult), inc=("vec", 1))
                if bitcast_each:
                    cm = em.emit("vector", lambda e, d=d: e.tensor_copy(
                        d.bitcast(U32), d.bitcast(U32)), inc=("vec", 1))
            marks.append(cm)
    return marks[-1]


def build_program(debug_outputs=()):
    nc = bass.Bass("TRN2", target_bir_lowering=False, debug=False)

    dk = "ExternalOutput" if "caches" in debug_outputs else "Internal"
    dq = "ExternalOutput" if "q" in debug_outputs else "Internal"
    da = "ExternalOutput" if "at" in debug_outputs else "Internal"

    xT_all = nc.dram_tensor("xT_all", [D, S], F32, kind="ExternalInput").ap()
    xT_own = nc.dram_tensor("xT_own", [D, 2048], F32, kind="ExternalInput").ap()
    wqa = nc.dram_tensor("wqa", [D, QA], F32, kind="ExternalInput").ap()
    wqbn = nc.dram_tensor("wqbn", [QA, H * NOPE], F32, kind="ExternalInput").ap()
    wqbr = nc.dram_tensor("wqbr", [QA, H * ROPE], F32, kind="ExternalInput").ap()
    wkvc = nc.dram_tensor("wkvc", [D, CKV], F32, kind="ExternalInput").ap()
    wkvr = nc.dram_tensor("wkvr", [D, ROPE], F32, kind="ExternalInput").ap()
    wkbk = nc.dram_tensor("wkbk", [CKV, H * NOPE], F32, kind="ExternalInput").ap()
    wkbv = nc.dram_tensor("wkbv", [CKV, H * VD], F32, kind="ExternalInput").ap()
    wo = nc.dram_tensor("wo", [H * VD, D], F32, kind="ExternalInput").ap()
    w2 = nc.dram_tensor("w2", [D, D], F32, kind="ExternalInput").ap()
    gqa = nc.dram_tensor("gqa", [128, QA // 128], F32, kind="ExternalInput").ap()
    gkv = nc.dram_tensor("gkv", [128, CKV // 128], F32, kind="ExternalInput").ap()
    cosg = nc.dram_tensor("cosg", [ROPE, S], F32, kind="ExternalInput").ap()
    sing = nc.dram_tensor("sing", [ROPE, S], F32, kind="ExternalInput").ap()
    coso = nc.dram_tensor("coso", [ROPE, 2048], F32, kind="ExternalInput").ap()
    sino = nc.dram_tensor("sino", [ROPE, 2048], F32, kind="ExternalInput").ap()
    masks_d = nc.dram_tensor("masks", [128, 8 * 512], BF16, kind="ExternalInput").ap()
    ones_c_d = nc.dram_tensor("ones_c", [128, 1], F32, kind="ExternalInput").ap()
    ones_b_d = nc.dram_tensor("ones_b", [128, 1], BF16, kind="ExternalInput").ap()
    ones_r_d = nc.dram_tensor("ones_r", [1, 128], F32, kind="ExternalInput").ap()
    ones_rb_d = nc.dram_tensor("ones_rb", [1, 128], BF16, kind="ExternalInput").ap()
    rotT_d = nc.dram_tensor("rotT", [128, 128], F32, kind="ExternalInput").ap()
    sc_d = nc.dram_tensor("scales", [128, 2], F32, kind="ExternalInput").ap()
    outT = nc.dram_tensor("outT", [D, 2048], F32, kind="ExternalOutput").ap()

    kcache = nc.dram_tensor("kcache", [H * NOPE, S], BF16, kind=dk).ap()
    vcache = nc.dram_tensor("vcache", [S, H * VD], BF16, kind=dk).ap()
    qacache = nc.dram_tensor("qacache", [QA, 2048], BF16, kind="Internal").ap()
    qncache = nc.dram_tensor("qncache", [H * NOPE, 2048], BF16, kind=dq).ap()
    qrcache = nc.dram_tensor("qrcache", [ROPE, H * 2048], BF16, kind=dq).ap()
    atcache = nc.dram_tensor("atcache", [H * VD, 2048], BF16, kind=da).ap()
    wo_bf = nc.dram_tensor("wo_bf", [H * VD, D], BF16, kind="Internal").ap()
    kpe_dbg = (nc.dram_tensor("kpe_dbg", [ROPE, S], BF16, kind="ExternalOutput").ap()
               if "caches" in debug_outputs else None)

    top = ExitStack()
    sb = lambda name, shape, dt: top.enter_context(nc.sbuf_tensor(name, shape, dt))
    ones_c = sb("ones_c_t", [128, 1], F32)
    ones_b = sb("ones_b_t", [128, 1], BF16)
    ones_r = sb("ones_r_t", [1, 128], F32)
    ones_rb = sb("ones_rb_t", [1, 128], BF16)
    rowsb = sb("rowsb", [1, 4 * 512], BF16)
    rotT = sb("rotT_t", [128, 128], F32)
    gqa_t = sb("gqa_t", [128, QA // 128], F32)
    gkv_t = sb("gkv_t", [128, CKV // 128], F32)
    sc_t = sb("sc_t", [128, 2], F32)
    rr_t = sb("rr_t", [128, 2], F32)
    rcol = sb("rcol", [128, 1], F32)
    kpe_t = sb("kpe_t", [128, S], BF16)
    rows = sb("rows", [1, 512 * 8], F32)

    def row(i):
        return rows[0:1, i * 512:(i + 1) * 512]

    def rowb(i):
        return rowsb[0:1, i * 512:(i + 1) * 512]

    rowb2 = rowb

    ps = [top.enter_context(nc.psum_tensor(f"ps{i}", [128, 512], F32)) for i in range(8)]

    block = top.enter_context(nc.Block())
    em = Em(nc, block)
    st_x = em.stream("stx", "sync")
    st_cs = em.stream("stcs", "sync")
    st_mask = em.stream("stmask", "sync")
    st_ksp = em.stream("stksp", "gpsimd")
    st_xbf = em.stream("stxbf", "gpsimd")
    st_vsp = em.stream("stvsp", "gpsimd")
    st_kpd = em.stream("stkpd", "gpsimd")
    st_qasp = em.stream("stqasp", "gpsimd")
    st_qnsp = em.stream("stqnsp", "gpsimd")
    st_qrsp = em.stream("stqrsp", "gpsimd")
    st_seg = [em.stream("stseg0", "sync"), em.stream("stseg1", "sync")]
    st_at = [em.stream("stat0", "gpsimd"), em.stream("stat1", "gpsimd")]
    st_xr = [em.stream("stxr0", "sync"), em.stream("stxr1", "sync")]
    st_out = [em.stream("stout0", "gpsimd"), em.stream("stout1", "gpsimd")]

    st_k = em.stream("konst", "sync")
    for dst, src in ((ones_c, ones_c_d), (ones_b, ones_b_d), (ones_r, ones_r_d),
                     (ones_rb, ones_rb_d),
                     (rotT, rotT_d), (gqa_t, gqa), (gkv_t, gkv), (sc_t, sc_d)):
        _dma(em, st_k, dst[:], src[:])
    em.wait("vector", st_k, em.cnt[st_k])
    rm0 = em.emit("vector", lambda e: e.reciprocal(rr_t[:], sc_t[:]), inc=("vec", 1))
    em.wait("vector", "vec", rm0)
    em.emit("vector", lambda e: e.tensor_mul(rcol[:], rr_t[:, 0:1], rr_t[:, 1:2]),
            inc=("vec", 1))

    def ln_rows(ndim, eps):
        inv = 1.0 / ndim
        em.emit("vector", lambda e: e.tensor_scalar(row(0), ps[6][0:1, :], inv, None, ALU.mult), inc=("vec", 1))
        em.emit("vector", lambda e: e.tensor_scalar(row(1), ps[7][0:1, :], inv, None, ALU.mult), inc=("vec", 1))
        em.wait("vector", "vec", em.cnt["vec"])
        em.emit("vector", lambda e: e.tensor_mul(row(2), row(0), row(0)), inc=("vec", 1))
        em.wait("vector", "vec", em.cnt["vec"])
        em.emit("vector", lambda e: e.tensor_sub(row(2), row(1), row(2)), inc=("vec", 1))
        em.wait("vector", "vec", em.cnt["vec"])
        vm = em.emit("vector", lambda e: e.tensor_scalar(row(2), row(2), eps, None, ALU.add), inc=("vec", 1))
        em.wait("scalar", "vec", vm)
        sm = em.emit("scalar", lambda e: e.sqrt(row(3), row(2)), inc=("act", 1))
        em.wait("vector", "act", sm)
        em.emit("vector", lambda e: e.reciprocal(row(4), row(3)), inc=("vec", 1))
        em.wait("vector", "vec", em.cnt["vec"])
        em.emit("vector", lambda e: e.tensor_mul(row(5), row(0), row(4)), inc=("vec", 1))
        em.wait("vector", "vec", em.cnt["vec"])
        return em.emit("vector", lambda e: e.tensor_scalar(row(5), row(5), -1.0, None, ALU.mult), inc=("vec", 1))

    def rms_row(ndim, eps):
        em.emit("vector", lambda e: e.tensor_scalar(row(2), ps[6][0:1, :], 1.0 / ndim, eps, ALU.mult, ALU.add), inc=("vec", 1))
        em.wait("scalar", "vec", em.cnt["vec"])
        sm = em.emit("scalar", lambda e: e.sqrt(row(3), row(2)), inc=("act", 1))
        em.wait("vector", "act", sm)
        return em.emit("vector", lambda e: e.reciprocal(row(4), row(3)), inc=("vec", 1))

    with ExitStack() as a1:
        sba = lambda name, shape, dt: a1.enter_context(nc.sbuf_tensor(name, shape, dt))
        wst = sba("wst", [128, 2 * 1024], F32)
        wkvc_t = sba("wkvc_t", [128, 16 * CKV], BF16)
        wkvr_t = sba("wkvr_t", [128, 16 * ROPE], BF16)
        wkbk_t = sba("wkbk_t", [128, 4 * 2048], BF16)
        wkbv_t = sba("wkbv_t", [128, 4 * 2048], BF16)
        xbf = sba("xbf", [128, 16 * 512], BF16)
        xsq = sba("xsq", [128, 16 * 512], BF16)
        ckv_f = sba("ckv_f", [128, 4 * 512], F32)
        cksq = sba("cksq", [128, 2 * 512], BF16)
        ckv_n = sba("ckv_n", [128, 2 * 4 * 512], BF16)
        tmpv = sba("tmpv", [128, 2 * 512], F32)
        w1kv = sba("w1kv", [128, 8], F32)
        kpa = sba("kpa", [ROPE, 512], F32)
        kpr = sba("kpr", [ROPE, 512], F32)
        kpc = sba("kpc", [ROPE, 512], F32)
        kps = sba("kps", [ROPE, 512], F32)
        cosr = sba("cosr", [ROPE, 2 * 512], F32)
        sinr = sba("sinr", [ROPE, 2 * 512], F32)
        kstage = sba("kstage", [128, 16 * 512], BF16)
        vstage = sba("vstage", [128, 4 * 2048], BF16)

        zkp = em.emit("vector", lambda e: e.memset(kpe_t[ROPE:128, :], 0.0), inc=("vec", 1))
        load_weight(em, wst, wkvc_t, wkvc, 16, CKV)
        load_weight(em, wst, wkvr_t, wkvr, 16, ROPE)
        load_weight(em, wst, wkbk_t, wkbk, 4, 2048)
        wcast = load_weight(em, wst, wkbv_t, wkbv, 4, 2048)
        em.wait("tensor", "vec", wcast)
        for oc in range(4):
            for ic in range(16):
                wm = em.emit("tensor", lambda e, oc=oc, ic=ic: e.matmul(
                    ps[oc][:, 0:1], wkvc_t[:, ic * CKV + oc * 128:ic * CKV + oc * 128 + 128],
                    ones_b[:], start=(ic == 0), stop=(ic == 15)),
                    inc=(("ten", 1) if ic == 15 else None))
            em.wait("scalar", "ten", wm)
            em.emit("scalar", lambda e, oc=oc: e.copy(w1kv[:, oc:oc + 1], ps[oc][:, 0:1]), inc=("act", 1))
        for ic in range(16):
            wm = em.emit("tensor", lambda e, ic=ic: e.matmul(
                ps[4][0:ROPE, 0:1], wkvr_t[:, ic * ROPE:(ic + 1) * ROPE],
                ones_b[:], start=(ic == 0), stop=(ic == 15)),
                inc=(("ten", 1) if ic == 15 else None))
        em.wait("scalar", "ten", wm)
        w1done = em.emit("scalar", lambda e: e.copy(w1kv[0:ROPE, 4:5], ps[4][0:ROPE, 0:1]), inc=("act", 1))
        em.wait("tensor", "act", w1done)

        prev = {}

        def emit_xload(j):
            tg = slice(TG * j, TG * (j + 1))
            em.wait("gpsimd", "ten", prev.get("kva", 0))
            em.wait("gpsimd", "act", prev.get("sq_last", 0))
            xbm = None
            for pc in range(4):
                xbm = _dma(em, st_xbf,
                           _s3(xbf, 512)[:, 4 * pc:4 * pc + 4, :],
                           _r3(xT_all[:, tg])[:, 4 * pc:4 * pc + 4, :])
            prev["xbm"] = xbm
            if j > 0:
                em.wait("sync", "vec", prev.get("kpe", 0))
            cm_ = _dma(em, st_cs, cosr[:, (j % 2) * 512:(j % 2) * 512 + 512], cosg[:, tg])
            prev["cs"] = _dma(em, st_cs, sinr[:, (j % 2) * 512:(j % 2) * 512 + 512], sing[:, tg])

        def emit_A(j):
            xbm = prev["xbm"]
            em.wait("tensor", "vec", prev.get("evac_v", 0))
            em.wait("tensor", "act", prev.get("evac_a", 0))
            sq_last = None
            for c in range(16):
                em.wait("scalar", st_xbf, xbm)
                em.wait("scalar", "ten", prev.get("xsq_mm", 0) if c < 2 else 0)
                sq_last = em.emit("scalar", lambda e, c=c: e.square(
                    xsq[:, c * 512:c * 512 + 512], xbf[:, c * 512:c * 512 + 512]), inc=("act", 1))
                prev["sq_last"] = sq_last
            for c in range(16):
                em.wait("tensor", st_xbf, xbm)
                if c == 0:
                    em.wait("tensor", "act", prev.get("evac_a", 0))
                    em.wait("tensor", "act", prev.get("vev", 0))
                    em.wait("tensor", "vec", prev.get("kpe", 0))
                em.emit("tensor", lambda e, c=c: e.matmul(
                    ps[6][0:1, :], ones_b[:], xbf[:, c * 512:c * 512 + 512],
                    start=(c == 0), stop=(c == 15)))
                for o in range(4):
                    em.emit("tensor", lambda e, c=c, o=o: e.ldweights(
                        wkvc_t[:, c * CKV + o * 128:c * CKV + o * 128 + 128]))
                    em.emit("tensor", lambda e, c=c, o=o: e.matmul(
                        ps[o][:], wkvc_t[:, c * CKV + o * 128:c * CKV + o * 128 + 128],
                        xbf[:, c * 512:c * 512 + 512], start=(c == 0), stop=(c == 15)))
                em.emit("tensor", lambda e, c=c: e.ldweights(
                    wkvr_t[:, c * ROPE:c * ROPE + ROPE]))
                prev["kva"] = em.emit("tensor", lambda e, c=c: e.matmul(
                    ps[4][0:ROPE, :], wkvr_t[:, c * ROPE:c * ROPE + ROPE],
                    xbf[:, c * 512:c * 512 + 512], start=(c == 0), stop=(c == 15)), inc=("ten", 1))
            em.wait("tensor", "act", sq_last)
            for c in range(16):
                prev["xsq_mm"] = em.emit("tensor", lambda e, c=c: e.matmul(
                    ps[7][0:1, :], ones_b[:], xsq[:, c * 512:c * 512 + 512],
                    start=(c == 0), stop=(c == 15)), inc=(("ten", 1) if c == 15 else None))
            prev["statmm"] = prev["xsq_mm"]
            em.wait("vector", "ten", prev["statmm"])
            ln_rows(D, EPS_LN)
            em.wait("tensor", "vec", em.cnt["vec"])
            em.emit("tensor", lambda e: e.matmul(ps[6][:], ones_r[:], row(4), start=True, stop=True))
            prev["bm"] = em.emit("tensor", lambda e: e.matmul(ps[7][:], ones_r[:], row(5), start=True, stop=True), inc=("ten", 1))

        def emit_E(j):
            bm = prev["bm"]
            for o in range(4):
                em.wait("scalar", "ten", bm)
                em.wait("scalar", "vec", prev.get(f"ev{o % 2}", 0))
                sc = em.emit("scalar", lambda e, o=o: e.copy(
                    tmpv[:, (o % 2) * 512:(o % 2) * 512 + 512], ps[o][:]), inc=("act", 1))
                em.wait("vector", "act", sc)
                em.wait("vector", "ten", bm)
                em.wait("vector", "vec", em.cnt["vec"])
                em.emit("vector", lambda e, o=o: e.tensor_mul(
                    tmpv[:, (o % 2) * 512:(o % 2) * 512 + 512],
                    tmpv[:, (o % 2) * 512:(o % 2) * 512 + 512], ps[6][:]), inc=("vec", 1))
                em.wait("vector", "vec", em.cnt["vec"])
                prev[f"ev{o % 2}"] = em.emit("vector", lambda e, o=o: e.scalar_tensor_tensor(
                    ckv_f[:, o * 512:o * 512 + 512], ps[7][:], w1kv[:, o:o + 1],
                    tmpv[:, (o % 2) * 512:(o % 2) * 512 + 512], ALU.mult, ALU.add), inc=("vec", 1))
            em.wait("scalar", "ten", bm)
            sc = em.emit("scalar", lambda e: e.copy(kpr[:], ps[4][0:ROPE, :]), inc=("act", 1))
            prev["kpm"] = sc
            em.wait("vector", "act", sc)
            em.emit("vector", lambda e: e.tensor_mul(kpr[:], kpr[:], ps[6][0:ROPE, :]), inc=("vec", 1))
            em.wait("vector", "vec", em.cnt["vec"])
            kpaff = em.emit("vector", lambda e: e.scalar_tensor_tensor(
                kpa[:], ps[7][0:ROPE, :], w1kv[0:ROPE, 4:5], kpr[:], ALU.mult, ALU.add), inc=("vec", 1))
            prev["kpaff"] = kpaff
            prev["evac_a"] = em.cnt["act"]
            prev["evac_v"] = em.cnt["vec"]

        def emit_R(j):
            for o in range(4):
                em.wait("scalar", "vec", prev[f"ev{o % 2}"])
                em.wait("scalar", "ten", prev.get("cksq_mm", 0))
                sqm = em.emit("scalar", lambda e, o=o: e.square(
                    cksq[:, (o % 2) * 512:(o % 2) * 512 + 512], ckv_f[:, o * 512:o * 512 + 512]), inc=("act", 1))
                em.wait("tensor", "act", sqm)
                if o == 0:
                    em.wait("tensor", "vec", prev.get("ckvn", 0))
                    em.wait("tensor", "vec", prev.get("kpe", 0))
                prev["cksq_mm"] = em.emit("tensor", lambda e, o=o: e.matmul(
                    ps[5][0:1, :], ones_b[:], cksq[:, (o % 2) * 512:(o % 2) * 512 + 512],
                    start=(o == 0), stop=(o == 3)), inc=("ten", 1))
            em.wait("vector", "ten", prev["cksq_mm"])
            em.emit("vector", lambda e: e.tensor_scalar(row(6), ps[5][0:1, :], 1.0 / CKV, EPS_RMS, ALU.mult, ALU.add), inc=("vec", 1))
            em.wait("scalar", "vec", em.cnt["vec"])
            smx = em.emit("scalar", lambda e: e.sqrt(row(7), row(6)), inc=("act", 1))
            em.wait("vector", "act", smx)
            rsm = em.emit("vector", lambda e: e.reciprocal(row(6), row(7)), inc=("vec", 1))
            em.wait("tensor", "vec", rsm)
            bmk = em.emit("tensor", lambda e: e.matmul(ps[5][:], ones_r[:], row(6), start=True, stop=True), inc=("ten", 1))
            ckvn_ready = None
            for o in range(4):
                em.wait("vector", "ten", bmk)
                em.wait("vector", "vec", em.cnt["vec"])
                em.emit("vector", lambda e, o=o: e.tensor_mul(
                    tmpv[:, 0:512], ckv_f[:, o * 512:o * 512 + 512], ps[5][:]), inc=("vec", 1))
                em.wait("vector", "vec", em.cnt["vec"])
                ckvn_ready = em.emit("vector", lambda e, o=o, j=j: e.tensor_scalar(
                    ckv_n[:, (j % 2) * 2048 + o * 512:(j % 2) * 2048 + o * 512 + 512],
                    tmpv[:, 0:512], gkv_t[:, o:o + 1], None, ALU.mult), inc=("vec", 1))
            prev["ckvn"] = ckvn_ready
            em.wait("tensor", "vec", prev["kpaff"])
            rotm = em.emit("tensor", lambda e: e.matmul(
                ps[4][0:ROPE, :], rotT[0:ROPE, 0:ROPE], kpa[:], start=True, stop=True), inc=("ten", 1))
            em.wait("vector", st_cs, prev["cs"])
            em.emit("vector", lambda e, j=j: e.tensor_mul(
                kpc[:], kpa[:], cosr[:, (j % 2) * 512:(j % 2) * 512 + 512]), inc=("vec", 1))
            em.wait("vector", "ten", rotm)
            em.emit("vector", lambda e, j=j: e.tensor_mul(
                kps[:], ps[4][0:ROPE, :], sinr[:, (j % 2) * 512:(j % 2) * 512 + 512]), inc=("vec", 1))
            em.wait("vector", "vec", em.cnt["vec"])
            kpe_done = em.emit("vector", lambda e, j=j: e.tensor_add(
                kpe_t[0:ROPE, TG * j:TG * j + TG], kpc[:], kps[:]), inc=("vec", 1))
            prev["kpe"] = kpe_done

        def emit_KB(j):
            tg = slice(TG * j, TG * (j + 1))
            ckvn_ready = prev["ckvn"]
            cb = (j % 2) * 2048
            em.wait("tensor", "act", prev.get("evac_a", 0))
            for oh in range(16):
                pb = oh % 2
                em.wait("tensor", "vec", ckvn_ready)
                em.wait("tensor", "act", prev.get(f"kev{pb}", 0))
                for ic in range(4):
                    em.emit("tensor", lambda e, oh=oh, ic=ic: e.ldweights(
                        wkbk_t[:, ic * 2048 + oh * 128:ic * 2048 + oh * 128 + 128]))
                    lm = em.emit("tensor", lambda e, oh=oh, ic=ic, pb=pb, cb=cb: e.matmul(
                        ps[pb][:], wkbk_t[:, ic * 2048 + oh * 128:ic * 2048 + oh * 128 + 128],
                        ckv_n[:, cb + ic * 512:cb + ic * 512 + 512], start=(ic == 0), stop=(ic == 3)),
                        inc=(("ten", 1) if ic == 3 else None))
                em.wait("scalar", "ten", lm)
                em.wait("scalar", st_ksp, prev.get("kspill", 0) if oh < 2 else 0)
                prev[f"kev{pb}"] = em.emit("scalar", lambda e, oh=oh, pb=pb: e.copy(
                    kstage[:, oh * 512:oh * 512 + 512], ps[pb][:]), inc=("act", 1))
            em.wait("gpsimd", "act", em.cnt["act"])
            for pc in range(4):
                prev["kspill"] = _dma(em, st_ksp,
                                      _r3(kcache[:, tg])[:, 4 * pc:4 * pc + 4, :],
                                      _s3(kstage, 512)[:, 4 * pc:4 * pc + 4, :])
            for ts_ in range(4):
                em.wait("tensor", "vec", ckvn_ready)
                em.wait("tensor", "act", prev.get("vev", 0))
                em.wait("tensor", "act", prev.get("kev0", 0))
                em.wait("tensor", "act", prev.get("kev1", 0))
                if ts_ == 0:
                    em.wait("tensor", st_vsp, prev.get("vspill", 0))
                for ic in range(4):
                    em.emit("tensor", lambda e, ts_=ts_, ic=ic, cb=cb: e.ldweights(
                        ckv_n[:, cb + ic * 512 + ts_ * 128:cb + ic * 512 + ts_ * 128 + 128]))
                    for og in range(4):
                        lm = em.emit("tensor", lambda e, ts_=ts_, ic=ic, og=og, cb=cb: e.matmul(
                            ps[og][:], ckv_n[:, cb + ic * 512 + ts_ * 128:cb + ic * 512 + ts_ * 128 + 128],
                            wkbv_t[:, ic * 2048 + og * 512:ic * 2048 + og * 512 + 512],
                            start=(ic == 0), stop=(ic == 3)),
                            inc=(("ten", 1) if (ic == 3 and og == 3) else None))
                em.wait("scalar", "ten", lm)
                for og in range(4):
                    prev["vev"] = em.emit("scalar", lambda e, ts_=ts_, og=og: e.copy(
                        vstage[:, ts_ * 2048 + og * 512:ts_ * 2048 + og * 512 + 512],
                        ps[og][:]), inc=("act", 1))
            em.wait("gpsimd", "act", em.cnt["act"])
            prev["vspill"] = _dma(em, st_vsp, _r3(vcache[tg, :]), _s3(vstage, 2048))

        for j in range(NG):
            emit_xload(j)
            if j > 0:
                emit_KB(j - 1)
            emit_A(j)
            emit_E(j)
            emit_R(j)
        emit_KB(NG - 1)
        if kpe_dbg is not None:
            em.wait("gpsimd", "vec", em.cnt["vec"])
            _dma(em, st_kpd, kpe_dbg[:], kpe_t[0:ROPE, :])
        em.barrier()

    q1_stack = ExitStack()
    q1_t = q1_stack.enter_context(nc.sbuf_tensor("q1_t", [128, 16 * 2048], FP8))
    with ExitStack() as a2:
        sba = lambda name, shape, dt: a2.enter_context(nc.sbuf_tensor(name, shape, dt))
        wst = sba("wst2", [128, 2 * 512], F32)
        wqa_t = sba("wqa_t", [128, 16 * QA], BF16)
        x_f = sba("x_f2", [128, 16 * 512], F32)
        xbf = sba("xbf2", [128, 16 * 512], BF16)
        xsq = sba("xsq2", [128, 2 * 512], F32)
        xnt = sba("xnt2", [128, 512], F32)
        qa_f = sba("qa_f", [128, 12 * 512], BF16)
        qa_n = sba("qa_n", [128, 12 * 512], BF16)
        qsq = sba("qsq", [128, 12 * 512], BF16)
        tmq = sba("tmq", [128, 2 * 512], F32)
        w1q = sba("w1q", [128, 12], F32)

        wcast = load_weight(em, wst, wqa_t, wqa, 16, QA)
        em.wait("tensor", "vec", wcast)
        w1marks = {}
        for oc in range(12):
            em.wait("tensor", "act", w1marks.get(oc % 4, 0))
            for ic in range(16):
                wm = em.emit("tensor", lambda e, oc=oc, ic=ic: e.matmul(
                    ps[oc % 4][:, 0:1], wqa_t[:, ic * QA + oc * 128:ic * QA + oc * 128 + 128],
                    ones_b[:], start=(ic == 0), stop=(ic == 15)),
                    inc=(("ten", 1) if ic == 15 else None))
            em.wait("scalar", "ten", wm)
            w1marks[oc % 4] = em.emit("scalar", lambda e, oc=oc: e.copy(w1q[:, oc:oc + 1], ps[oc % 4][:, 0:1]), inc=("act", 1))
        em.wait("tensor", "act", max(w1marks.values()))

        prev = {}
        for g in range(OG):
            tg = slice(TG * g, TG * (g + 1))
            if g > 0:
                em.wait("sync", "vec", prev["xn_last"])
            xm = None
            for pc in range(4):
                xm = _dma(em, st_x,
                          _s3(x_f, 512)[:, 4 * pc:4 * pc + 4, :],
                          _r3(xT_own[:, tg])[:, 4 * pc:4 * pc + 4, :])
            em.wait("gpsimd", "ten", prev.get("qa_mm", 0))
            xbm = None
            for pc in range(4):
                xbm = _dma(em, st_xbf,
                           _s3(xbf, 512)[:, 4 * pc:4 * pc + 4, :],
                           _r3(xT_own[:, tg])[:, 4 * pc:4 * pc + 4, :])
            sq_last = None
            for c in range(16):
                em.wait("scalar", st_x, xm)
                em.wait("scalar", "ten", prev.get("statmm", 0))
                sq_last = em.emit("scalar", lambda e, c=c: e.square(
                    xsq[:, (c % 2) * 512:(c % 2) * 512 + 512], x_f[:, c * 512:c * 512 + 512]), inc=("act", 1))
                em.wait("tensor", st_x, xm)
                em.wait("tensor", "act", sq_last)
                if c == 0:
                    em.wait("tensor", "vec", prev.get("xn_last", 0))
                    em.wait("tensor", "vec", prev.get("qan", 0))
                em.emit("tensor", lambda e, c=c: e.matmul(
                    ps[6][0:1, :], ones_c[:], x_f[:, c * 512:c * 512 + 512], start=(c == 0), stop=(c == 15)))
                prev["statmm"] = em.emit("tensor", lambda e, c=c: e.matmul(
                    ps[7][0:1, :], ones_c[:], xsq[:, (c % 2) * 512:(c % 2) * 512 + 512],
                    start=(c == 0), stop=(c == 15)), inc=("ten", 1))
            em.wait("vector", "ten", prev["statmm"])
            ln_rows(D, EPS_LN)
            em.wait("tensor", "vec", em.cnt["vec"])
            em.emit("tensor", lambda e: e.matmul(ps[6][:], ones_r[:], row(4), start=True, stop=True))
            bm = em.emit("tensor", lambda e: e.matmul(ps[7][:], ones_r[:], row(5), start=True, stop=True), inc=("ten", 1))
            def emit_xn_chunk(c, g=g):
                em.wait("vector", "ten", bm)
                em.wait("vector", "vec", em.cnt["vec"])
                em.emit("vector", lambda e, c=c: e.tensor_mul(
                    xnt[:, 0:512], x_f[:, c * 512:c * 512 + 512], ps[6][:]), inc=("vec", 1))
                em.wait("vector", "vec", em.cnt["vec"])
                em.emit("vector", lambda e, c=c: e.tensor_add(
                    xnt[:, 0:512], xnt[:, 0:512], ps[7][:]), inc=("vec", 1))
                em.wait("vector", "vec", em.cnt["vec"])
                prev["xn_last"] = em.emit("vector", lambda e, c=c, g=g: e.tensor_scalar(
                    q1_t[:, c * 2048 + g * 512:c * 2048 + g * 512 + 512],
                    xnt[:, 0:512], sc_t[:, 0:1], None, ALU.mult), inc=("vec", 1))

            sqq = None
            for oc in range(12):
                pb = oc % 4
                ts2 = oc % 2
                em.wait("tensor", st_xbf, xbm)
                em.wait("tensor", "act", prev.get(f"qaev{pb}", 0))
                for c in range(16):
                    em.emit("tensor", lambda e, oc=oc, c=c: e.ldweights(
                        wqa_t[:, c * QA + oc * 128:c * QA + oc * 128 + 128]))
                    qm = em.emit("tensor", lambda e, oc=oc, c=c, pb=pb: e.matmul(
                        ps[pb][:], wqa_t[:, c * QA + oc * 128:c * QA + oc * 128 + 128],
                        xbf[:, c * 512:c * 512 + 512], start=(c == 0), stop=(c == 15)),
                        inc=(("ten", 1) if c == 15 else None))
                prev["qa_mm"] = qm
                em.wait("scalar", "ten", qm)
                em.wait("scalar", "ten", bm)
                em.wait("scalar", "vec", prev.get(f"qmul{ts2}", 0))
                cpm = em.emit("scalar", lambda e, pb=pb, ts2=ts2: e.copy(
                    tmq[:, ts2 * 512:ts2 * 512 + 512], ps[pb][:]), inc=("act", 1))
                prev[f"qaev{pb}"] = cpm
                em.wait("vector", "act", cpm)
                em.wait("vector", "ten", bm)
                em.wait("vector", "vec", em.cnt["vec"])
                prev[f"qmul{ts2}"] = em.emit("vector", lambda e, ts2=ts2: e.tensor_mul(
                    tmq[:, ts2 * 512:ts2 * 512 + 512], tmq[:, ts2 * 512:ts2 * 512 + 512], ps[6][:]), inc=("vec", 1))
                em.wait("vector", "vec", em.cnt["vec"])
                qafm = em.emit("vector", lambda e, oc=oc, ts2=ts2: e.scalar_tensor_tensor(
                    qa_f[:, oc * 512:oc * 512 + 512], ps[7][:], w1q[:, oc:oc + 1],
                    tmq[:, ts2 * 512:ts2 * 512 + 512], ALU.mult, ALU.add), inc=("vec", 1))
                em.wait("scalar", "vec", qafm)
                em.wait("scalar", "ten", prev.get("qsq_mm", 0))
                sqq = em.emit("scalar", lambda e, oc=oc: e.square(
                    qsq[:, oc * 512:oc * 512 + 512], qa_f[:, oc * 512:oc * 512 + 512]), inc=("act", 1))
                emit_xn_chunk(oc)
            for c in range(12, 16):
                emit_xn_chunk(c)
            em.wait("tensor", "act", sqq)
            for oc in range(12):
                prev["qsq_mm"] = em.emit("tensor", lambda e, oc=oc: e.matmul(
                    ps[4][0:1, :], ones_b[:], qsq[:, oc * 512:oc * 512 + 512],
                    start=(oc == 0), stop=(oc == 11)), inc=(("ten", 1) if oc == 11 else None))
            em.wait("vector", "ten", prev["qsq_mm"])
            em.emit("vector", lambda e: e.tensor_scalar(row(6), ps[4][0:1, :], 1.0 / QA, EPS_RMS, ALU.mult, ALU.add), inc=("vec", 1))
            em.wait("scalar", "vec", em.cnt["vec"])
            smx = em.emit("scalar", lambda e: e.sqrt(row(7), row(6)), inc=("act", 1))
            em.wait("vector", "act", smx)
            rsm = em.emit("vector", lambda e: e.reciprocal(row(6), row(7)), inc=("vec", 1))
            em.wait("tensor", "vec", rsm)
            bmq = em.emit("tensor", lambda e: e.matmul(ps[5][:], ones_r[:], row(6), start=True, stop=True), inc=("ten", 1))
            qan = None
            for oc in range(12):
                em.wait("vector", "ten", bmq)
                em.wait("vector", st_qasp, prev.get("qaspill", 0) if oc < 2 else 0)
                em.wait("vector", "vec", em.cnt["vec"])
                em.emit("vector", lambda e, oc=oc: e.tensor_mul(
                    tmq[:, 0:512], qa_f[:, oc * 512:oc * 512 + 512], ps[5][:]), inc=("vec", 1))
                em.wait("vector", "vec", em.cnt["vec"])
                qan = em.emit("vector", lambda e, oc=oc: e.tensor_scalar(
                    qa_n[:, oc * 512:oc * 512 + 512], tmq[:, 0:512], gqa_t[:, oc:oc + 1], None, ALU.mult), inc=("vec", 1))
            prev["qan"] = qan
            em.wait("gpsimd", "vec", qan)
            for pc in range(3):
                prev["qaspill"] = _dma(em, st_qasp,
                                       _r3(qacache[:, tg])[:, 4 * pc:4 * pc + 4, :],
                                       _s3(qa_n, 512)[:, 4 * pc:4 * pc + 4, :])
        em.barrier()

    with ExitStack() as a3:
        sba = lambda name, shape, dt: a3.enter_context(nc.sbuf_tensor(name, shape, dt))
        wst = sba("wst3", [128, 2 * 2048], F32)
        wqb_t = sba("wqb_t", [128, 12 * 3072], BF16)
        qa_ng = sba("qa_ng", [128, 12 * 512], BF16)
        qnstage = sba("qnstage", [128, 16 * 512], BF16)
        qr_f = sba("qr_f", [128, 2 * 512], F32)
        qr_t1 = sba("qr_t1", [128, 512], F32)
        qr_t2 = sba("qr_t2", [128, 512], F32)
        qrstage = sba("qrstage", [128, 8 * 512], BF16)
        cosr2 = sba("cosr2", [128, 512], F32)
        sinr2 = sba("sinr2", [128, 512], F32)

        for ic in range(12):
            load_weight(em, wst,
                        wqb_t[:, ic * 3072:ic * 3072 + 2048],
                        wqbn[ic * 128:(ic + 1) * 128, :], 1, 2048)
        for ic in range(12):
            load_weight(em, wst,
                        wqb_t[:, ic * 3072 + 2048:ic * 3072 + 3072],
                        wqbr[ic * 128:(ic + 1) * 128, :], 1, 1024)
        prev = {}
        for g in range(OG):
            tg = slice(TG * g, TG * (g + 1))
            em.wait("tensor", "vec", em.cnt["vec"])
            em.wait("tensor", "act", em.cnt["act"])
            if g > 0:
                em.wait("sync", "ten", prev["qb_last"])
                em.wait("sync", "vec", prev["qrf_last"])
            qam = None
            for pc in range(3):
                qam = _dma(em, st_x,
                           _s3(qa_ng, 512)[:, 4 * pc:4 * pc + 4, :],
                           _r3(qacache[:, tg])[:, 4 * pc:4 * pc + 4, :])
            _dma(em, st_cs, cosr2[0:ROPE, :], coso[:, tg])
            _dma(em, st_cs, cosr2[ROPE:128, :], coso[:, tg])
            _dma(em, st_cs, sinr2[0:ROPE, :], sino[:, tg])
            sm_ = _dma(em, st_cs, sinr2[ROPE:128, :], sino[:, tg])
            em.wait("tensor", st_x, qam)
            for oh in range(16):
                pb = oh % 2
                em.wait("tensor", "act", prev.get(f"qnev{pb}", 0))
                for ic in range(12):
                    em.emit("tensor", lambda e, oh=oh, ic=ic: e.ldweights(
                        wqb_t[:, ic * 3072 + oh * 128:ic * 3072 + oh * 128 + 128]))
                    qm = em.emit("tensor", lambda e, oh=oh, ic=ic, pb=pb: e.matmul(
                        ps[pb][:], wqb_t[:, ic * 3072 + oh * 128:ic * 3072 + oh * 128 + 128],
                        qa_ng[:, ic * 512:ic * 512 + 512], start=(ic == 0), stop=(ic == 11)),
                        inc=(("ten", 1) if ic == 11 else None))
                em.wait("scalar", "ten", qm)
                em.wait("scalar", st_qnsp, prev.get("qnspill", 0) if oh < 2 else 0)
                prev[f"qnev{pb}"] = em.emit("scalar", lambda e, oh=oh, pb=pb: e.copy(
                    qnstage[:, oh * 512:oh * 512 + 512], ps[pb][:]), inc=("act", 1))
            em.wait("gpsimd", "act", em.cnt["act"])
            for pc in range(4):
                prev["qnspill"] = _dma(em, st_qnsp,
                                       _r3(qncache[:, tg])[:, 4 * pc:4 * pc + 4, :],
                                       _s3(qnstage, 512)[:, 4 * pc:4 * pc + 4, :])
            em.wait("vector", st_cs, sm_)
            qrf_mark = None
            for oc in range(8):
                pb = 2 + (oc % 2)
                pr = 4 + (oc % 2)
                em.wait("tensor", "act", prev.get(f"qrev{pb}", 0))
                for ic in range(12):
                    em.emit("tensor", lambda e, oc=oc, ic=ic: e.ldweights(
                        wqb_t[:, ic * 3072 + 2048 + oc * 128:ic * 3072 + 2048 + oc * 128 + 128]))
                    qm = em.emit("tensor", lambda e, oc=oc, ic=ic, pb=pb: e.matmul(
                        ps[pb][:], wqb_t[:, ic * 3072 + 2048 + oc * 128:ic * 3072 + 2048 + oc * 128 + 128],
                        qa_ng[:, ic * 512:ic * 512 + 512], start=(ic == 0), stop=(ic == 11)),
                        inc=(("ten", 1) if ic == 11 else None))
                em.wait("scalar", "ten", qm)
                em.wait("scalar", "vec", prev.get(f"qrf{oc % 2}", 0))
                evm = em.emit("scalar", lambda e, oc=oc, pb=pb: e.copy(
                    qr_f[:, (oc % 2) * 512:(oc % 2) * 512 + 512], ps[pb][:]), inc=("act", 1))
                prev[f"qrev{pb}"] = evm
                em.wait("tensor", "act", evm)
                em.wait("tensor", "vec", prev.get(f"qrs{oc % 2}", 0))
                rm = em.emit("tensor", lambda e, oc=oc, pr=pr: e.matmul(
                    ps[pr][:], rotT[:], qr_f[:, (oc % 2) * 512:(oc % 2) * 512 + 512],
                    start=True, stop=True), inc=("ten", 1))
                em.wait("vector", "act", evm)
                em.wait("vector", "vec", em.cnt["vec"])
                em.emit("vector", lambda e, oc=oc: e.tensor_mul(
                    qr_t1[:], qr_f[:, (oc % 2) * 512:(oc % 2) * 512 + 512], cosr2[:]), inc=("vec", 1))
                em.wait("vector", "ten", rm)
                prev[f"qrs{oc % 2}"] = em.emit("vector", lambda e, oc=oc, pr=pr: e.tensor_mul(
                    qr_t2[:], ps[pr][:], sinr2[:]), inc=("vec", 1))
                em.wait("vector", "vec", em.cnt["vec"])
                em.wait("vector", st_qrsp, prev.get("qrspill", 0) if oc < 2 else 0)
                qrf_mark = em.emit("vector", lambda e, oc=oc: e.tensor_add(
                    qrstage[:, oc * 512:oc * 512 + 512], qr_t1[:], qr_t2[:]), inc=("vec", 1))
                prev[f"qrf{oc % 2}"] = qrf_mark
            prev["qb_last"] = em.cnt["ten"]
            prev["qrf_last"] = qrf_mark
            em.wait("gpsimd", "vec", em.cnt["vec"])
            for oc in range(8):
                _dma(em, st_qrsp,
                     qrcache[:, (2 * oc) * 2048 + TG * g:(2 * oc) * 2048 + TG * g + TG],
                     qrstage[0:ROPE, oc * 512:oc * 512 + 512])
                prev["qrspill"] = _dma(
                    em, st_qrsp,
                    qrcache[:, (2 * oc + 1) * 2048 + TG * g:(2 * oc + 1) * 2048 + TG * g + TG],
                    qrstage[ROPE:128, oc * 512:oc * 512 + 512])
        em.barrier()

    with ExitStack() as a4:
        sba = lambda name, shape, dt: a4.enter_context(nc.sbuf_tensor(name, shape, dt))
        kseg = [sba(f"kseg{i}", [128, 16 * 512], BF16) for i in range(2)]
        vseg = [sba(f"vseg{i}", [128, 4 * 2048], BF16) for i in range(2)]
        qn_g = sba("qn_g", [128, 16 * 512], BF16)
        qr_g = sba("qr_g", [128, 16 * 512], BF16)
        pt = sba("pt", [128, 8 * 512], BF16)
        ptot = sba("ptot", [128, 16 * 512], BF16)
        acc = sba("acc", [128, 16 * 512], BF16)
        m_t = sba("m_t", [128, 8 * 512], BF16)
        atn = sba("atn", [128, 2 * 512], BF16)

        mm_ = _dma(em, st_mask, m_t[:], masks_d[:])
        zm = em.emit("vector", lambda e: e.memset(qr_g[ROPE:128, :], 0.0), inc=("vec", 1))
        em.wait("tensor", "vec", zm)

        st_wpre = em.stream("stwpre", "gpsimd")
        wo_pre = [0]

        def prestage_wo(upto):
            if wo_pre[0] >= upto:
                return
            em.wait("gpsimd", "ten", em.cnt["ten"])
            for c in range(wo_pre[0], upto):
                _dma(em, st_wpre, wo_bf[c * 128:(c + 1) * 128, :],
                     wo[c * 128:(c + 1) * 128, :])
            wo_pre[0] = upto
        prev = {}
        seg_done = {}
        pt_done = {}
        ptot_done = {}
        ptot_h = {}
        bc_marks = {}
        q_marks = {}
        kv_marks = {}
        seg_slot = {}
        slot_free = {}
        scores_h_done = {}
        segslot = [0]
        for g in range(OG):
            tg = slice(TG * g, TG * (g + 1))
            nseg = 2 * (g + 1)
            nkb_g = 8 * (g + 1)
            if g >= 1:
                prestage_wo((6, 11, 16)[g - 1])
            if g > 0:
                em.wait("tensor", "vec", em.cnt["vec"])
                em.wait("tensor", "act", em.cnt["act"])

            def load_qh(gq, h):
                stq = em.stream(f"stqg{gq % 2}", "sync")
                if gq > 0:
                    em.wait("sync", "ten", scores_h_done.get((gq - 1, h), 0))
                m1 = _dma(em, stq, _s3(qn_g, 512)[:, h:h + 1, :],
                          _r3(qncache[:, TG * gq:TG * gq + TG])[:, h:h + 1, :])
                m1 = _dma(em, stq, qr_g[0:ROPE, h * 512:h * 512 + 512],
                          qrcache[:, h * 2048 + TG * gq:h * 2048 + TG * gq + TG])
                q_marks[gq] = (stq, m1)

            def load_seg(si, g=g):
                nseg_ = 2 * (g + 1)
                if si >= nseg_:
                    g2 = g + 1
                    si2 = si - nseg_
                    if g2 >= OG or si2 >= 2 * (g2 + 1) or si2 >= 2:
                        return
                    slot = segslot[0] % 2
                    segslot[0] += 1
                    seg_slot[(g2, si2)] = slot
                    em.wait("sync", "ten", slot_free.get(slot, 0))
                    for pc in range(4):
                        _dma(em, st_seg[slot],
                             _s3(kseg[slot], 512)[:, 4 * pc:4 * pc + 4, :],
                             _r3(kcache[:, 512 * si2:512 * si2 + 512])[:, 4 * pc:4 * pc + 4, :])
                    kv_marks[(g2, si2)] = _dma(em, st_seg[slot], _s3(vseg[slot], 2048),
                                               _r3(vcache[512 * si2:512 * si2 + 512, :]))
                    return
                if (g, si) in seg_slot:
                    return
                slot = segslot[0] % 2
                segslot[0] += 1
                seg_slot[(g, si)] = slot
                em.wait("sync", "ten", slot_free.get(slot, 0))
                for pc in range(4):
                    _dma(em, st_seg[slot],
                         _s3(kseg[slot], 512)[:, 4 * pc:4 * pc + 4, :],
                         _r3(kcache[:, 512 * si:512 * si + 512])[:, 4 * pc:4 * pc + 4, :])
                kv_marks[(g, si)] = _dma(em, st_seg[slot], _s3(vseg[slot], 2048),
                                         _r3(vcache[512 * si:512 * si + 512, :]))

            if g == 0:
                for h in range(H):
                    load_qh(0, h)
            load_seg(0)
            load_seg(1)
            tiles = [(seg, h, kb) for seg in range(nseg) for h in range(H)
                     for kb in range(NKB_SEG)]
            BANKS = (2, 3, 5)
            BSZ = 2

            def scores_batch(batch):
                for t, ti in batch:
                    seg, h, kb = t
                    pscore = BANKS[ti % 3]
                    if kb == 0 and h == 0:
                        em.wait("tensor", st_seg[seg_slot[(g, seg)]], kv_marks[(g, seg)])
                    if kb == 0 and h == 0 and seg == 0:
                        em.wait("tensor", q_marks[g][0], q_marks[g][1])
                    em.wait("tensor", "act", prev.get(f"exp{pscore}", 0))
                scm = {}
                for t, ti in batch:
                    seg, h, kb = t
                    pscore = BANKS[ti % 3]
                    sl = seg_slot[(g, seg)]
                    em.emit("tensor", lambda e, sl=sl, h=h, kb=kb: e.ldweights(
                        kseg[sl][:, h * 512 + kb * 128:h * 512 + kb * 128 + 128]))
                    em.emit("tensor", lambda e, sl=sl, h=h, kb=kb, pscore=pscore: e.matmul(
                        ps[pscore][:], kseg[sl][:, h * 512 + kb * 128:h * 512 + kb * 128 + 128],
                        qn_g[:, h * 512:h * 512 + 512], start=True, stop=False))
                    em.emit("tensor", lambda e, seg=seg, kb=kb: e.ldweights(
                        kpe_t[:, seg * 512 + kb * 128:seg * 512 + kb * 128 + 128]))
                    scm[ti] = em.emit("tensor", lambda e, seg=seg, h=h, kb=kb, pscore=pscore: e.matmul(
                        ps[pscore][:], kpe_t[:, seg * 512 + kb * 128:seg * 512 + kb * 128 + 128],
                        qr_g[:, h * 512:h * 512 + 512],
                        start=False, stop=True), inc=("ten", 1))
                    if seg == nseg - 1 and kb == NKB_SEG - 1:
                        scores_h_done[(g, h)] = scm[ti]
                        if g + 1 < OG:
                            load_qh(g + 1, h)
                out = []
                for t, ti in batch:
                    seg, h, kb = t
                    jglob = seg * NKB_SEG + kb
                    r = ti % 8
                    pscore = BANKS[ti % 3]
                    em.wait("scalar", "ten", scm[ti])
                    em.wait("scalar", "ten", pt_done.get(r, 0))
                    em.wait("scalar", "vec", ptot_done.get(r, 0))
                    ex_m = em.emit("scalar", lambda e, r=r, pscore=pscore: e.activation(
                        pt[:, r * 512:r * 512 + 512], ps[pscore][:], AF.Exp, scale=SCALE),
                        inc=("act", 1))
                    prev[f"exp{pscore}"] = ex_m
                    band = jglob >= nkb_g - 8
                    mk_m = None
                    if band:
                        bp = jglob - (nkb_g - 8)
                        em.wait("vector", "act", ex_m)
                        em.wait("vector", st_mask, mm_)
                        mk_m = em.emit("vector", lambda e, r=r, bp=bp: e.tensor_mul(
                            pt[:, r * 512:r * 512 + 512], pt[:, r * 512:r * 512 + 512],
                            m_t[:, bp * 512:bp * 512 + 512]), inc=("vec", 1))
                    out.append((t, ti, ex_m, mk_m))
                return out

            def av_batch(batch):
                for t, ti, ex_m, mk_m in batch:
                    seg, h, kb = t
                    if kb == 0:
                        em.wait("tensor", "vec", prev.get(f"accv{h % 2}", 0))
                    if mk_m is not None:
                        em.wait("tensor", "vec", mk_m)
                    else:
                        em.wait("tensor", "act", ex_m)
                for t, ti, ex_m, mk_m in batch:
                    seg, h, kb = t
                    r = ti % 8
                    pa = 0 if h % 2 == 0 else 4
                    sl = seg_slot[(g, seg)]
                    em.emit("tensor", lambda e, sl=sl, h=h, kb=kb: e.ldweights(
                        vseg[sl][:, kb * 2048 + h * 128:kb * 2048 + h * 128 + 128]))
                    av_m = em.emit("tensor", lambda e, sl=sl, h=h, kb=kb, r=r, pa=pa: e.matmul(
                        ps[pa][:], vseg[sl][:, kb * 2048 + h * 128:kb * 2048 + h * 128 + 128],
                        pt[:, r * 512:r * 512 + 512], start=(kb == 0), stop=(kb == 3)),
                        inc=("ten", 1))
                    pt_done[r] = av_m
                    if mk_m is None:
                        em.wait("vector", "act", ex_m)
                    if seg == 0 and kb == 0:
                        pm = em.emit("vector", lambda e, h=h, r=r: e.tensor_copy(
                            ptot[:, h * 512:h * 512 + 512], pt[:, r * 512:r * 512 + 512]), inc=("vec", 1))
                    else:
                        em.wait("vector", "vec", ptot_h.get(h, 0))
                        pm = em.emit("vector", lambda e, h=h, r=r: e.tensor_add(
                            ptot[:, h * 512:h * 512 + 512], ptot[:, h * 512:h * 512 + 512],
                            pt[:, r * 512:r * 512 + 512]), inc=("vec", 1))
                    ptot_done[r] = pm
                    ptot_h[h] = pm
                    if kb == 3:
                        em.wait("vector", "ten", av_m)
                        if seg == 0:
                            prev[f"accv{h % 2}"] = em.emit("vector", lambda e, h=h, pa=pa: e.tensor_copy(
                                acc[:, h * 512:h * 512 + 512], ps[pa][:]), inc=("vec", 1))
                        else:
                            prev[f"accv{h % 2}"] = em.emit("vector", lambda e, h=h, pa=pa: e.tensor_add(
                                acc[:, h * 512:h * 512 + 512], acc[:, h * 512:h * 512 + 512], ps[pa][:]), inc=("vec", 1))
                        if seg == nseg - 1:
                            emit_norm1(h)
                            if len(norm_pend) > 2:
                                emit_norm2()
                    if kb == 3 and h == H - 1:
                        seg_done[(g, seg)] = em.cnt["ten"]
                        slot_free[seg_slot[(g, seg)]] = em.cnt["ten"]
                        load_seg(seg + 2)

            norm_pend = []

            def emit_norm1(h):
                pu = 1 if h % 2 == 0 else 7
                em.wait("tensor", "vec", ptot_h[h])
                em.wait("tensor", "vec", prev.get(f"rcp{h % 2}", 0))
                sm_m = em.emit("tensor", lambda e, h=h, pu=pu: e.matmul(
                    ps[pu][0:1, :], ones_b[:], ptot[:, h * 512:h * 512 + 512],
                    start=True, stop=True), inc=("ten", 1))
                em.wait("vector", "ten", sm_m)
                em.wait("vector", "ten", bc_marks.get(h % 4, 0))
                def _recip(e, h=h, pu=pu):
                    with nc.allow_low_precision(reason="softmax denominator bcast in bf16"):
                        return e.reciprocal(rowb2(h % 4), ps[pu][0:1, :])
                rc_m = em.emit("vector", _recip, inc=("vec", 1))
                prev[f"rcp{h % 2}"] = rc_m
                norm_pend.append((h, rc_m))

            def emit_norm2(g=g, tg=tg):
                if not norm_pend:
                    return
                h, rc_m = norm_pend.pop(0)
                em.wait("tensor", "vec", rc_m)
                em.wait("tensor", "vec", prev.get("nrm", 0))
                bc_m = em.emit("tensor", lambda e, h=h: e.matmul(
                    ps[6][:], ones_rb[:], rowb2(h % 4), start=True, stop=True), inc=("ten", 1))
                bc_marks[h % 4] = bc_m
                em.wait("vector", "ten", bc_m)
                em.wait("vector", st_at[h % 2], prev.get(f"atsp{h % 2}", 0))
                nm = em.emit("vector", lambda e, h=h: e.tensor_mul(
                    atn[:, (h % 2) * 512:(h % 2) * 512 + 512], acc[:, h * 512:h * 512 + 512], ps[6][:]),
                    inc=("vec", 1))
                prev["nrm"] = nm
                em.wait("gpsimd", "vec", nm)
                prev[f"atsp{h % 2}"] = _dma(
                    em, st_at[h % 2], atcache[h * 128:h * 128 + 128, tg],
                    atn[:, (h % 2) * 512:(h % 2) * 512 + 512])

            batches = [[(t, ti) for t, ti in
                        zip(tiles[i:i + BSZ], range(i, i + BSZ))]
                       for i in range(0, len(tiles), BSZ)]
            pending = None
            for bt in batches:
                done = scores_batch(bt)
                if pending is not None:
                    av_batch(pending)
                pending = done
            av_batch(pending)
            while norm_pend:
                emit_norm2()
            seg_done[(g, "last")] = em.cnt["ten"]
            prev["glast"] = em.cnt["ten"]

        em.barrier()

    with ExitStack() as a5:
        sba = lambda name, shape, dt: a5.enter_context(nc.sbuf_tensor(name, shape, dt))
        wst5 = sba("wst5", [128, 2 * 1024], F32)
        wst2b = sba("wst5b", [128, 2 * 1024], F32)
        wo_t = sba("wo_t", [128, 16 * 2048], BF16)
        w2q = sba("w2q", [128, 16 * 2048], FP8)
        at_g = sba("at_g", [128, 16 * 512], BF16)
        xrot = sba("xrot", [128, 2 * 512], F32)
        frot = sba("frot", [128, 2 * 512], F32)

        for qc in range(4):
            em.emit("vector", lambda e, qc=qc: e.tensor_copy(
                q1_t[:, qc * 8192:(qc + 1) * 8192].bitcast(U32),
                q1_t[:, qc * 8192:(qc + 1) * 8192].bitcast(U32)), inc=("vec", 1))
        st_wo = em.stream("stwo", "sync")
        wom = 0
        for c in range(16):
            wom = _dma(em, st_wo, wo_t[:, c * 2048:(c + 1) * 2048],
                       wo_bf[c * 128:(c + 1) * 128, :])
        em.wait("tensor", st_wo, wom)
        am0 = None
        for pc in range(4):
            am0 = _dma(em, st_x, _s3(at_g, 512)[:, 4 * pc:4 * pc + 4, :],
                       _r3(atcache[:, 0:TG])[:, 4 * pc:4 * pc + 4, :])
        load_weight(em, wst2b, w2q, w2, 13, 2048, fp8_scale=sc_t[:, 1:2],
                    queue="gpsimd", bitcast_each=True)
        wcast5 = load_weight(em, wst5, w2q[:, 13 * 2048:], w2[13 * 128:, :], 3, 2048,
                             fp8_scale=sc_t[:, 1:2], queue="sync", bitcast_each=True)
        em.wait("tensor", "vec", wcast5)
        prev = {}
        for tgi in range(4):
            tg = slice(TG * tgi, TG * (tgi + 1))
            if tgi == 0:
                am = am0
            else:
                em.wait("sync", "ten", prev["olast"])
                am = None
                for pc in range(4):
                    am = _dma(em, st_x, _s3(at_g, 512)[:, 4 * pc:4 * pc + 4, :],
                              _r3(atcache[:, tg])[:, 4 * pc:4 * pc + 4, :])
            for m in range(16):
                pb = m % 2
                em.wait("sync", "vec", prev.get(f"res{pb}", 0))
                xm = _dma(em, st_xr[pb], xrot[:, pb * 512:pb * 512 + 512],
                          xT_own[m * 128:m * 128 + 128, tg])
                em.wait("tensor", st_x, am)
                em.wait("tensor", "vec", prev.get(f"res{pb}", 0))
                em.wait("tensor", "act", prev.get(f"fsc{pb}", 0))
                for h in range(16):
                    em.emit("tensor", lambda e, h=h, m=m: e.ldweights(
                        wo_t[:, h * 2048 + m * 128:h * 2048 + m * 128 + 128]))
                    em.emit("tensor", lambda e, h=h, m=m, pb=pb: e.matmul(
                        ps[pb][:], wo_t[:, h * 2048 + m * 128:h * 2048 + m * 128 + 128],
                        at_g[:, h * 512:h * 512 + 512], start=(h == 0), stop=(h == 15)))
                for c2 in range(8):
                    wap = w2q[:, 2 * c2 * 2048:(2 * c2 + 2) * 2048].rearrange(
                        "p (a q) -> p a q", q=2048)[:, :, m * 128:m * 128 + 128]
                    xap = q1_t[:, 2 * c2 * 2048:(2 * c2 + 2) * 2048].rearrange(
                        "p (a q) -> p a q", q=2048)[:, :, tgi * 512:tgi * 512 + 512]
                    em.emit("tensor", lambda e, wap=wap: e.ldweights(wap, perf_mode=DR))
                    mm2 = em.emit("tensor", lambda e, wap=wap, xap=xap, pb=pb, c2=c2: e.matmul(
                        ps[2 + pb][:], wap, xap, start=(c2 == 0), stop=(c2 == 7), perf_mode=DR),
                        inc=(("ten", 1) if c2 == 7 else None))
                prev["olast"] = mm2
                em.wait("scalar", "ten", mm2)
                em.wait("scalar", "vec", prev.get(f"res{pb}", 0))
                em.wait("scalar", st_out[pb], prev.get(f"ost{pb}", 0))
                fm = em.emit("scalar", lambda e, pb=pb: e.mul(
                    frot[:, pb * 512:pb * 512 + 512], ps[2 + pb][:], rcol[:]), inc=("act", 1))
                prev[f"fsc{pb}"] = fm
                em.wait("vector", "act", fm)
                em.wait("vector", "ten", mm2)
                em.wait("vector", st_xr[pb], xm)
                em.wait("vector", st_out[pb], prev.get(f"ost{pb}", 0))
                em.emit("vector", lambda e, pb=pb: e.tensor_add(
                    frot[:, pb * 512:pb * 512 + 512], frot[:, pb * 512:pb * 512 + 512], ps[pb][:]),
                    inc=("vec", 1))
                em.wait("vector", "vec", em.cnt["vec"])
                rm = em.emit("vector", lambda e, pb=pb: e.tensor_add(
                    frot[:, pb * 512:pb * 512 + 512], frot[:, pb * 512:pb * 512 + 512],
                    xrot[:, pb * 512:pb * 512 + 512]), inc=("vec", 1))
                prev[f"res{pb}"] = rm
                em.wait("gpsimd", "vec", rm)
                prev[f"ost{pb}"] = _dma(em, st_out[pb], outT[m * 128:m * 128 + 128, tg],
                                        frot[:, pb * 512:pb * 512 + 512])
        for name, queue in em.dstreams.items():
            eng = "sync" if queue == "sync" else "gpsimd"
            em.wait(eng, name, em.cnt[name])
        em.flush()

    q1_stack.close()
    top.close()
    return nc



def _prep_core_inputs(c, x, shared):
    b, h = c // 2, c % 2
    xT = np.ascontiguousarray(x[b].T)
    own = np.concatenate([np.arange(1024 * g + 512 * h, 1024 * g + 512 * h + 512)
                          for g in range(4)])
    xT_own = np.ascontiguousarray(xT[:, own])
    coso_h = np.ascontiguousarray(shared["cosg"][:, own])
    sino_h = np.ascontiguousarray(shared["sing"][:, own])
    kj = np.arange(128)[:, None]
    qi = np.arange(512)[None, :]
    masks = np.concatenate(
        [(128 * bp + kj <= 512 * h + qi).astype(BF) for bp in range(8)], axis=1)
    m = dict(shared)
    m.update({"xT_all": xT, "xT_own": xT_own, "coso": coso_h, "sino": sino_h,
              "masks": masks})
    return m


def prep_all_inputs(L0_attn_hidden_states, L0_attn_q_a_w, L0_attn_q_a_ln_w,
                    L0_attn_q_b_w, L0_attn_kv_a_w, L0_attn_kv_a_ln_w,
                    L0_attn_kv_b_w, L0_attn_o_w, L0_ffn_input2, L0_ffn_scale,
                    L0_ffn_scale1):
    x = np.asarray(L0_attn_hidden_states, np.float32)
    qaw = np.asarray(L0_attn_q_a_w, np.float32)
    qbw = np.asarray(L0_attn_q_b_w, np.float32)
    kvaw = np.asarray(L0_attn_kv_a_w, np.float32)
    kvbw = np.asarray(L0_attn_kv_b_w, np.float32)
    ow = np.asarray(L0_attn_o_w, np.float32)

    qb3 = qbw.reshape(H, QHD, QA)
    kb3 = kvbw.reshape(H, NOPE + VD, CKV)

    inv = THETA ** (-np.arange(0, ROPE, 2, dtype=np.float64) / ROPE)
    t = np.arange(S, dtype=np.float64)
    ang = np.concatenate([np.outer(inv, t), np.outer(inv, t)], 0)

    p64 = np.zeros((ROPE, ROPE), np.float32)
    for i in range(32):
        p64[i, 32 + i] = -1.0
        p64[32 + i, i] = 1.0
    rotT_h = np.zeros((128, 128), np.float32)
    rotT_h[:ROPE, :ROPE] = p64.T
    rotT_h[ROPE:, ROPE:] = p64.T

    s0 = np.float32(np.asarray(L0_ffn_scale))
    s1 = np.float32(np.asarray(L0_ffn_scale1))

    shared = {
        "wqa": np.ascontiguousarray(qaw.T),
        "wqbn": np.ascontiguousarray(qb3[:, :NOPE, :].reshape(H * NOPE, QA).T),
        "wqbr": np.ascontiguousarray(qb3[:, NOPE:, :].reshape(H * ROPE, QA).T),
        "wkvc": np.ascontiguousarray(kvaw[:CKV, :].T),
        "wkvr": np.ascontiguousarray(kvaw[CKV:, :].T),
        "wkbk": np.ascontiguousarray(kb3[:, :NOPE, :].reshape(H * NOPE, CKV).T),
        "wkbv": np.ascontiguousarray(kb3[:, NOPE:, :].reshape(H * VD, CKV).T),
        "wo": np.ascontiguousarray(ow.T),
        "w2": np.ascontiguousarray(np.asarray(L0_ffn_input2, np.float32)),
        "gqa": np.ascontiguousarray(
            np.asarray(L0_attn_q_a_ln_w, np.float32).reshape(QA // 128, 128).T),
        "gkv": np.ascontiguousarray(
            np.asarray(L0_attn_kv_a_ln_w, np.float32).reshape(CKV // 128, 128).T),
        "cosg": np.cos(ang).astype(np.float32),
        "sing": np.sin(ang).astype(np.float32),
        "ones_c": np.ones((128, 1), np.float32),
        "ones_b": np.ones((128, 1), BF),
        "ones_r": np.ones((1, 128), np.float32),
        "ones_rb": np.ones((1, 128), BF),
        "rotT": rotT_h,
        "scales": np.concatenate([np.full((128, 1), s0, np.float32),
                                  np.full((128, 1), s1, np.float32)], 1),
    }
    return [_prep_core_inputs(c, x, shared) for c in range(8)]


def assemble_output(results):
    out = np.empty((B, S, D), np.float32)
    for c in range(8):
        b, h = c // 2, c % 2
        own = np.concatenate([np.arange(1024 * g + 512 * h, 1024 * g + 512 * h + 512)
                              for g in range(4)])
        out[b, own, :] = results[c]["outT"].T
    return out


def kernel(**inputs):
    in_maps = prep_all_inputs(**inputs)
    nc = build_program()
    res = run_bass_kernel_spmd(nc, in_maps, core_ids=list(range(8)))
    return assemble_output(res.results)
